# version 18
# baseline (speedup 1.0000x reference)
from contextlib import ExitStack
import numpy as np
import concourse.bass as bass
import concourse.mybir as mybir
from concourse.bass_utils import run_bass_kernel_spmd

F32 = mybir.dt.float32
I32 = mybir.dt.int32
AF = mybir.ActivationFunctionType
ALU = mybir.AluOpType
AX = mybir.AxisListType

ENGINES = ("tensor", "vector", "scalar", "gpsimd", "sync")



def ap_box(ap):
    t = ap.tensor
    esz = mybir.dt.size(ap.dtype)
    pairs = [list(pp) for pp in ap.ap]
    tname = type(t).__name__
    is_dram = "DRam" in tname
    off = int(ap.offset)
    if is_dram:
        plo, phi = 0, 1
        lo = hi = off * esz
        free = pairs
    else:
        pstep, pcount = pairs[0]
        base_part = getattr(t, "base_partition", 0) or 0
        pbase = off // pstep if pstep > 0 else 0
        foff = off - pbase * pstep if pstep > 0 else off
        plo = base_part + pbase
        phi = plo + pcount
        lo = hi = foff * esz
        free = pairs[1:]
    for step, count in free:
        dd = step * (count - 1) * esz
        if dd >= 0:
            hi += dd
        else:
            lo += dd
    hi += esz
    if "PSum" in tname:
        plo, phi = 0, 128
        lo = (lo // 2048) * 2048
        hi = ((hi + 2047) // 2048) * 2048
    return (t.name, plo, phi, lo, hi)


class _Item:
    __slots__ = ("fns", "waits", "inc", "sem")

    def __init__(self, fns, waits, inc, sem):
        self.fns = fns
        self.waits = waits
        self.inc = inc
        self.sem = sem


class _NameTrack:
    __slots__ = ("wring", "rring", "wfloor", "rfloor")
    RING = 192

    def __init__(self):
        self.wring = []
        self.rring = []
        self.wfloor = {}
        self.rfloor = {}

    def _push(self, ring, floor, entry):
        ring.append(entry)
        if len(ring) > self.RING:
            _, s, c = ring.pop(0)
            if c > floor.get(s, 0):
                floor[s] = c

    def add_w(self, box, sem, cnt):
        self._push(self.wring, self.wfloor, (box, sem, cnt))

    def add_r(self, box, sem, cnt):
        self._push(self.rring, self.rfloor, (box, sem, cnt))


def _olap(a, b):
    return (a[1] < b[2] and b[1] < a[2] and a[3] < b[4] and b[3] < a[4])


class Prog:
    def __init__(self, nc):
        self.nc = nc
        self.cnt = {e: 0 for e in ENGINES}
        self.is_dma_sem = {}
        self.guar = {e: {} for e in ENGINES}
        self.items = {e: [] for e in ENGINES}
        self.track = {}
        self.n_waits = 0
        self.n_overwaits = 0
        self.n_inst = 0

    def _tr(self, name):
        t = self.track.get(name)
        if t is None:
            t = self.track[name] = _NameTrack()
        return t

    def _deps(self, boxes_r, boxes_w):
        deps = {}

        def add(s, c):
            if c > deps.get(s, 0):
                deps[s] = c

        for box in boxes_r:
            t = self.track.get(box[0])
            if t is None:
                continue
            for s, c in t.wfloor.items():
                add(s, c)
            for wb, ws, wc in t.wring:
                if _olap(box, wb):
                    add(ws, wc)
        for box in boxes_w:
            t = self.track.get(box[0])
            if t is None:
                continue
            for s, c in t.wfloor.items():
                add(s, c)
            for s, c in t.rfloor.items():
                add(s, c)
            for wb, ws, wc in t.wring:
                if _olap(box, wb):
                    add(ws, wc)
            for rb, rs, rc in t.rring:
                if _olap(box, rb):
                    add(rs, rc)
        return deps

    def _plan(self, engine, fns, boxes_r, boxes_w, inc, sem):
        deps = self._deps(boxes_r, boxes_w)
        waits = []
        g = self.guar[engine]
        for s, c in sorted(deps.items()):
            if self.is_dma_sem.get(s) and c < self.cnt[s]:
                c = self.cnt[s]
                self.n_overwaits += 1
            if c > g.get(s, 0):
                waits.append((s, c))
                g[s] = c
        self.n_waits += len(waits)
        self.n_inst += len(fns)
        self.cnt[sem] += inc
        c = self.cnt[sem]
        self.items[engine].append(_Item(list(fns), waits, inc, sem))
        for b in boxes_w:
            self._tr(b[0]).add_w(b, sem, c)
        for b in boxes_r:
            self._tr(b[0]).add_r(b, sem, c)
        return c

    def emit(self, engine, fn, reads=(), writes=(), inc=1):
        return self.emit_group(engine, [fn], reads, writes, inc)

    def emit_group(self, engine, fns, reads=(), writes=(), inc=1):
        boxes_r = [b if isinstance(b, tuple) else ap_box(b) for b in reads]
        boxes_w = [b if isinstance(b, tuple) else ap_box(b) for b in writes]
        return self._plan(engine, fns, boxes_r, boxes_w, inc, engine)

    def dma(self, engine, out, in_, stream="d0", **kw):
        if stream not in self.cnt:
            self.cnt[stream] = 0
            self.is_dma_sem[stream] = True
        self.n_inst += 1
        return self._plan(engine,
                          [lambda E: E.dma_start(out=out, in_=in_, **kw)],
                          [ap_box(in_)], [ap_box(out)], 16, stream)

    def idma(self, engine, fn, reads=(), writes=(), stream="id0"):
        if stream not in self.cnt:
            self.cnt[stream] = 0
            self.is_dma_sem[stream] = True
        self.n_inst += 1
        boxes_r = [b if isinstance(b, tuple) else ap_box(b) for b in reads]
        boxes_w = [b if isinstance(b, tuple) else ap_box(b) for b in writes]
        return self._plan(engine, [fn], boxes_r, boxes_w, 16, stream)

    def wait_all(self, engine):
        waits = []
        g = self.guar[engine]
        for s, c in self.cnt.items():
            if c > g.get(s, 0):
                waits.append((s, c))
                g[s] = c
        self.items[engine].append(_Item([], waits, 0, engine))

    def finalize(self, stack):
        nc = self.nc
        self.sem = {s: stack.enter_context(nc.semaphore(f"S_{s}"))
                    for s in self.cnt}
        block = stack.enter_context(nc.Block())
        prog = self

        def section(eng_name):
            def body(E):
                sem = prog.sem
                for it in prog.items[eng_name]:
                    for (ds, dv) in it.waits:
                        E.wait_ge(sem[ds], dv)
                    ins = None
                    for fn in it.fns:
                        ins = fn(E)
                    if ins is not None and it.inc:
                        ins.then_inc(sem[it.sem], it.inc)
            return body

        for e in ENGINES:
            if self.items[e]:
                getattr(block, e)(section(e))
        return block



SEQ, DM, NH, DH = 2048, 1024, 16, 64
CTX, LAT = 1536, 512
NT = SEQ // 128
NCH = DM // 128
SCALE = 0.125


def _ref_trig():
    import jax
    import jax.numpy as jnp
    with jax.default_device(jax.devices("cpu")[0]):
        inv_freq = 1.0 / 10000.0 ** (jnp.arange(0, DH, 2,
                                                dtype=jnp.float32) / DH)
        freqs = jnp.arange(SEQ, dtype=jnp.float32)[:, None] * inv_freq[None, :]
        emb = jnp.concatenate([freqs, freqs], axis=-1)
        cos = np.asarray(jnp.cos(emb)).astype(np.float64)
        sin = np.asarray(jnp.sin(emb)).astype(np.float64)
    return cos, sin


def host_tables():
    cos64, sin64 = _ref_trig()
    sin64s = np.concatenate([-sin64[:, :32], sin64[:, 32:]], axis=1)
    tabs = {}
    j = np.arange(128) % 64
    tabs["cfm_q"] = np.ascontiguousarray(SCALE * cos64[:, j].T).astype(np.float32)
    tabs["sfm_q"] = np.ascontiguousarray(SCALE * sin64s[:, j].T).astype(np.float32)
    tabs["ctm"] = np.ascontiguousarray(
        cos64.reshape(NT, 128, DH).transpose(1, 0, 2)).astype(np.float32)
    tabs["stm"] = np.ascontiguousarray(
        sin64s.reshape(NT, 128, DH).transpose(1, 0, 2)).astype(np.float32)
    psw = np.zeros((128, 128), dtype=np.float32)
    m = np.arange(128)
    src = (m // 64) * 64 + ((m % 64) + 32) % 64
    psw[src, m] = 1.0
    tabs["psw"] = psw
    tabs["ident"] = np.eye(128, dtype=np.float32)
    pidx = np.arange(128)[:, None]
    kc = np.arange(512)[None, :]
    cm = np.zeros((128, 4, 512), dtype=np.float32)
    for qc in range(4):
        cm[:, qc, :] = np.where(kc <= 128 * qc + pidx, 0.0, -1e30)
    tabs["cmask"] = cm
    oh8 = np.zeros((8, 8, 128), dtype=np.float32)
    for e in range(8):
        oh8[e, e, :] = 1.0
    tabs["oh8"] = np.ascontiguousarray(oh8.transpose(1, 0, 2)).reshape(8, 1024)
    tabs["tokid"] = np.ascontiguousarray(
        (np.arange(16)[None, :] * 128 + np.arange(128)[:, None])
        .astype(np.int32))
    tk = (np.arange(16)[None, :] * 128 + np.arange(128)[:, None]).astype(np.float64)
    tabs["tokone"] = np.ascontiguousarray(
        np.stack([tk, np.ones_like(tk)], axis=2)).astype(np.float32)
    tabs["iota_r"] = np.ascontiguousarray(np.broadcast_to(
        np.arange(1024.0)[None, :], (128, 1024))).astype(np.float32)
    tabs["sent8"] = np.full((8, 1024), 2048, dtype=np.int32)
    tabs["eoff"] = np.ascontiguousarray(np.broadcast_to(
        (np.arange(8) * 1024.0)[None, :], (128, 8))).astype(np.float32)
    tabs["eclamp"] = np.ascontiguousarray(np.broadcast_to(
        (8192.0 + np.arange(8))[None, :], (128, 8))).astype(np.float32)
    tabs["eoff2"] = np.ascontiguousarray(np.broadcast_to(
        (np.arange(8) * 1025.0)[None, :], (128, 8))).astype(np.float32)
    return tabs


def host_layouts(ins):
    L = {}

    def wslab(W):
        return np.ascontiguousarray(
            np.asarray(W).reshape(8, 128, 8, 128).transpose(0, 3, 2, 1)
        ).reshape(8, 128, 1024)

    L["wq_r"] = wslab(ins["L0_attn_Wq"])
    L["wkv_r"] = wslab(ins["L0_attn_Wkv"])
    L["wo_r"] = wslab(ins["L0_attn_Wo"])
    gup = np.asarray(ins["L0_ffn_gate_up_proj"])
    L["gup_r"] = np.ascontiguousarray(
        gup.reshape(8, 8, 128, 32, 128).transpose(0, 3, 2, 1, 4)
    ).reshape(8, 32, 128, 1024)
    dwn = np.asarray(ins["L0_ffn_down_proj"])
    L["dwn_r"] = np.ascontiguousarray(
        dwn.reshape(8, 16, 128, 8, 128).transpose(0, 3, 2, 1, 4)
    ).reshape(8, 8, 128, 2048)
    rw = np.asarray(ins["L0_ffn_router_weight"])
    L["rt_r"] = np.ascontiguousarray(
        rw.T.reshape(8, 128, 8).transpose(1, 0, 2))
    L["rb_rep"] = np.ascontiguousarray(np.broadcast_to(
        np.asarray(ins["L0_ffn_router_bias"])[None, :], (128, 8)))
    L["bo_cm"] = np.ascontiguousarray(
        np.asarray(ins["L0_attn_bo"]).reshape(8, 128).T)
    L["gub_cm"] = np.ascontiguousarray(
        np.asarray(ins["L0_ffn_gate_up_bias"]).reshape(8, 32, 128)
        .transpose(2, 0, 1)).reshape(128, 256)
    L["dwb"] = np.ascontiguousarray(np.asarray(ins["L0_ffn_down_bias"]))
    L["g_rep"] = np.ascontiguousarray(np.broadcast_to(
        np.tile(np.asarray(ins["L0_attn_ln_g"]), NH)[None, :], (128, DM)))
    L["b_rep"] = np.ascontiguousarray(np.broadcast_to(
        np.tile(np.asarray(ins["L0_attn_ln_b"]), NH)[None, :], (128, DM)))
    return L



def build_program(taps=(), phases=99):
    nc = bass.Bass("TRN2", target_bir_lowering=False)
    stack = ExitStack()
    d = {}
    d["x"] = nc.dram_tensor("x", [SEQ, DM], F32, kind="ExternalInput").ap()
    for nm, shp in [("wq_r", [8, 128, 1024]), ("wkv_r", [8, 128, 1024]),
                    ("wo_r", [8, 128, 1024]), ("gup_r", [8, 32, 128, 1024]),
                    ("dwn_r", [8, 8, 128, 2048]), ("rt_r", [128, 8, 8]),
                    ("rb_rep", [128, 8]), ("bo_cm", [128, 8]),
                    ("gub_cm", [128, 256]), ("dwb", [8, 1024]),
                    ("g_rep", [128, DM]), ("b_rep", [128, DM]),
                    ("cfm_q", [128, SEQ]), ("sfm_q", [128, SEQ]),
                    ("ctm", [128, NT, DH]), ("stm", [128, NT, DH]),
                    ("psw", [128, 128]), ("ident", [128, 128]),
                    ("cmask", [128, 4, 512]), ("oh8", [8, 1024])]:
        d[nm] = nc.dram_tensor(nm, shp, F32, kind="ExternalInput").ap()
    d["y"] = nc.dram_tensor("y", [SEQ, DM], F32, kind="ExternalOutput").ap()
    d["tokid"] = nc.dram_tensor("tokid", [128, 16], I32,
                                kind="ExternalInput").ap()
    d["eoff"] = nc.dram_tensor("eoff", [128, 8], F32,
                               kind="ExternalInput").ap()
    d["eclamp"] = nc.dram_tensor("eclamp", [128, 8], F32,
                                 kind="ExternalInput").ap()
    d["eoff2"] = nc.dram_tensor("eoff2", [128, 8], F32,
                                kind="ExternalInput").ap()
    d["tokone"] = nc.dram_tensor("tokone", [128, 16, 2], F32,
                                 kind="ExternalInput").ap()
    d["iota_r"] = nc.dram_tensor("iota_r", [128, 1024], F32,
                                 kind="ExternalInput").ap()
    d["sent8"] = nc.dram_tensor("sent8", [8, 1024], I32,
                                kind="ExternalInput").ap()
    d["xfm_sp"] = nc.dram_tensor("xfm_sp", [NCH, 128, SEQ], F32).ap()
    d["x2tm_sp"] = nc.dram_tensor("x2tm_sp", [2049, 1032], F32).ap()
    d["idxtab"] = nc.dram_tensor("idxtab", [8200], I32).ap()
    d["ytmp"] = nc.dram_tensor("ytmp", [2049, 1024], F32).ap()
    d["ytmpb"] = nc.dram_tensor("ytmpb", [2049, 1024], F32).ap()
    d["qrot_sp"] = nc.dram_tensor("qrot_sp", [NCH, 128, SEQ], F32).ap()
    d["opair_sp"] = nc.dram_tensor("opair_sp", [NCH, 128, SEQ], F32).ap()
    tap_t = {}
    if "ddump" in taps:
        tap_t["ddump"] = nc.dram_tensor("tap_ddump", [8, 1025, 1024], F32,
                                        kind="ExternalOutput").ap()
    if "slotl" in taps:
        tap_t["slotl"] = nc.dram_tensor("tap_slotl", [128, 128], I32,
                                        kind="ExternalOutput").ap()
    for tname in taps:
        if tname in ("ddump", "slotl"):
            continue
        shp = {"xn": [128, NCH * SEQ], "lkv": [128, NT * DM],
               "qrot": [NCH, 128, SEQ], "opair": [NCH, 128, SEQ],
               "xfm": [NCH, 128, SEQ],
               "xn2": [128, 4096], "w8": [8, SEQ], "act": [128, 8192],
               "y1": [128, NCH * SEQ], "y2": [128, NCH * SEQ]}[tname]
        tap_t[tname] = nc.dram_tensor("tap_" + tname, shp, F32,
                                      kind="ExternalOutput").ap()

    sb = {}

    def S(name, shape):
        sb[name] = stack.enter_context(
            nc.sbuf_tensor("s_" + name, shape, F32)).ap()
        return sb[name]

    for nm, shp in [("rb_rep", [128, 8]), ("bo_cm", [128, 8]),
                    ("gub_cm", [128, 256]), ("dwb", [8, 1024]),
                    ("ident", [128, 128]), ("rt_sb", [128, 8, 8]),
                    ("eoff", [128, 8]), ("eclamp", [128, 8]),
                    ("eoff2", [128, 8]),
                    ("ones_col", [128, 1]),
                    ("ones_row", [1, 128]),
                    ("eps", [128, 1]), ("gub1", [128, 256]),
                    ("sm_a", [128, 16]), ("sm_b", [128, 16]),
                    ("sm_c", [128, 16]), ("sm_d", [128, 16]),
                    ("w8", [8, SEQ]), ("oh8", [8, 1024])]:
        S(nm, shp)
    sb["ixall"] = stack.enter_context(
        nc.sbuf_tensor("s_ixall", [128, 64], I32)).ap()
    BIGA = S("BIGA", [128, NCH * SEQ])
    BIGB = S("BIGB", [128, NT * DM])
    BIGC = S("BIGC", [128, 12288])
    PS = stack.enter_context(nc.psum_tensor("ps", [128, 8, 512], F32)).ap()

    p = Prog(nc)

    def biga(c, lo=0, hi=SEQ):
        return BIGA[:, c * SEQ + lo: c * SEQ + hi]

    def bigb_t(t, lo=0, hi=DM):
        return BIGB[:, t * DM + lo: t * DM + hi]

    def bank(b, lo=0, hi=512):
        return PS[:, b, lo:hi]

    def bigc(off, n):
        return BIGC[:, off:off + n]

    for nm in ["ident", "rb_rep", "bo_cm", "gub_cm", "dwb", "oh8", "eoff", "eclamp", "eoff2"]:
        p.dma("sync", sb[nm], d[nm], stream="tab")
    p.dma("sync", sb["rt_sb"], d["rt_r"], stream="tab")
    p.emit("vector", lambda E: E.memset(sb["ones_col"], 1.0),
           writes=[sb["ones_col"]])
    p.emit("vector", lambda E: E.memset(sb["ones_row"], 1.0),
           writes=[sb["ones_row"]])
    p.emit("vector", lambda E: E.memset(sb["eps"], 1e-5), writes=[sb["eps"]])
    p.emit("vector", lambda E: E.tensor_copy(
        sb["gub1"].rearrange("p (e m) -> p e m", e=8)[:, :, 0:16],
        sb["gub_cm"].rearrange("p (e m) -> p e m", e=8)[:, :, 0:16]),
           reads=[sb["gub_cm"]], writes=[sb["gub1"]])
    p.emit("vector", lambda E: E.tensor_scalar_add(
        sb["gub1"].rearrange("p (e m) -> p e m", e=8)[:, :, 16:32],
        sb["gub_cm"].rearrange("p (e m) -> p e m", e=8)[:, :, 16:32], 1.0),
           reads=[sb["gub_cm"]], writes=[sb["gub1"]])

    if phases >= 1:
        xt = [bigc(0, 1024), bigc(1024, 1024)]
        xn1 = bigc(2048, 1024)
        tfm = [bigc(3072, 1024), bigc(4096, 1024)]
        sqt = bigc(5120, 1024)
        for t in range(NT):
            xb = xt[t % 2]
            tf = tfm[t % 2]
            p.dma("sync", xb, d["x"][t * 128:(t + 1) * 128, :],
                  stream=f"ldx{t % 4}")
            s_ = sb["sm_a"][:, t:t + 1]
            ss_ = sb["sm_b"][:, t:t + 1]
            m_ = sb["sm_c"][:, t:t + 1]
            r_ = sb["sm_d"][:, t:t + 1]
            p.emit("vector", lambda E, xb=xb, s_=s_: E.reduce_sum(
                s_, xb, axis=AX.X), reads=[xb], writes=[s_])
            p.emit("scalar", lambda E, xb=xb: E.activation(
                out=sqt, in_=xb, func=AF.Square), reads=[xb], writes=[sqt])
            p.emit("vector", lambda E, ss_=ss_: E.reduce_sum(
                ss_, sqt, axis=AX.X), reads=[sqt], writes=[ss_])
            p.emit("vector", lambda E, s_=s_, m_=m_: E.tensor_scalar_mul(
                m_, s_, 1.0 / DM), reads=[s_], writes=[m_])
            p.emit("vector", lambda E, m_=m_, r_=r_: E.tensor_tensor(
                r_, m_, m_, ALU.mult), reads=[m_], writes=[r_])
            p.emit("vector", lambda E, ss_=ss_, r_=r_, s_=s_:
                   E.scalar_tensor_tensor(s_, ss_, 1.0 / DM, r_,
                                          ALU.mult, ALU.subtract),
                   reads=[ss_, r_], writes=[s_])
            p.emit("scalar", lambda E, s_=s_, r_=r_: E.activation(
                out=r_, in_=s_, func=AF.Sqrt, bias=sb["eps"], scale=1.0),
                   reads=[s_, sb["eps"]], writes=[r_])
            p.emit("vector", lambda E, r_=r_: E.reciprocal(r_, r_),
                   reads=[r_], writes=[r_])
            p.emit("vector", lambda E, xb=xb, m_=m_, r_=r_: E.tensor_scalar(
                xn1, xb, m_, r_, ALU.subtract, ALU.mult),
                   reads=[xb, m_, r_], writes=[xn1])
            for c in range(NCH):
                bb = c % 2
                p.emit("tensor", lambda E, xb=xb, c=c, bb=bb: E.transpose(
                    bank(bb, 0, 128), xb[:, c * 128:(c + 1) * 128],
                    sb["ident"]),
                       reads=[xb, sb["ident"]], writes=[bank(bb, 0, 128)])
                p.emit("scalar", lambda E, c=c, bb=bb, tf=tf: E.copy(
                    tf[:, c * 128:(c + 1) * 128], bank(bb, 0, 128)),
                       reads=[bank(bb, 0, 128)],
                       writes=[tf[:, c * 128:(c + 1) * 128]])
                p.emit("tensor", lambda E, c=c, bb=bb: E.transpose(
                    bank(bb, 128, 256), xn1[:, c * 128:(c + 1) * 128],
                    sb["ident"]),
                       reads=[xn1, sb["ident"]], writes=[bank(bb, 128, 256)])
                p.emit("scalar", lambda E, c=c, bb=bb, t=t: E.copy(
                    biga(c, t * 128, t * 128 + 128), bank(bb, 128, 256)),
                       reads=[bank(bb, 128, 256)],
                       writes=[biga(c, t * 128, t * 128 + 128)])
            xfm_dst = bass.AP(d["xfm_sp"].tensor, t * 128,
                              [[SEQ, 128], [128 * SEQ, NCH], [1, 128]])
            tfm_src = bass.AP(BIGC.tensor, int(tf.offset),
                              [[12288, 128], [128, NCH], [1, 128]])
            p.dma("gpsimd", xfm_dst, tfm_src, stream=f"xfo{t % 2}")
        if "xn" in tap_t:
            p.dma("gpsimd", tap_t["xn"], BIGA, stream="tap")
        if "xfm" in tap_t:
            for c in range(NCH):
                p.dma("gpsimd", tap_t["xfm"][c], d["xfm_sp"][c], stream="tap")

    if phases >= 2:
        wsl = [bigc(0, 1024), bigc(1024, 1024)]
        ktmp = bigc(2048, 512)
        t1 = bigc(2560, 1024)
        t2 = bigc(3584, 1024)
        ctm = bigc(4608, 1024)
        stm = bigc(5632, 1024)
        grep = bigc(6656, 1024)
        brep = bigc(7680, 1024)
        p.dma("sync", ctm, d["ctm"].rearrange("p t j -> p (t j)"), stream="tab2")
        p.dma("sync", stm, d["stm"].rearrange("p t j -> p (t j)"), stream="tab2")
        p.dma("sync", grep, d["g_rep"], stream="tab2")
        p.dma("sync", brep, d["b_rep"], stream="tab2")
        for mch in range(NCH):
            wb = wsl[mch % 2]
            p.dma("sync", wb, d["wkv_r"][mch], stream=f"wkv{mch % 2}")
            for g in range(4):
                fns = []
                rds = [wb]
                for c in range(NCH):
                    fns.append(lambda E, wb=wb, c=c, g=g: E.matmul(
                        bank(g % 2), wb[:, c * 128:(c + 1) * 128],
                        biga(c, g * 512, (g + 1) * 512),
                        start=(c == 0), stop=(c == NCH - 1)))
                    rds.append(biga(c, g * 512, (g + 1) * 512))
                p.emit_group("tensor", fns, reads=rds, writes=[bank(g % 2)])
                p.emit("scalar", lambda E, g=g: E.copy(ktmp, bank(g % 2)),
                       reads=[bank(g % 2)], writes=[ktmp])
                for qq in range(4):
                    t = g * 4 + qq
                    p.emit("tensor", lambda E, qq=qq: E.transpose(
                        bank(2 + (qq % 2), 0, 128),
                        ktmp[:, qq * 128:(qq + 1) * 128], sb["ident"]),
                           reads=[ktmp, sb["ident"]],
                           writes=[bank(2 + (qq % 2), 0, 128)])
                    p.emit("scalar", lambda E, t=t, mch=mch, qq=qq: E.copy(
                        bigb_t(t, mch * 128, (mch + 1) * 128),
                        bank(2 + (qq % 2), 0, 128)),
                           reads=[bank(2 + (qq % 2), 0, 128)],
                           writes=[bigb_t(t, mch * 128, (mch + 1) * 128)])
        for t in range(NT):
            kv = bigb_t(t)
            kv3 = kv.rearrange("p (h j) -> p h j", h=NH)
            ksw = bass.AP(BIGB.tensor, t * DM + 32,
                          [[NT * DM, 128], [DH, NH], [-32, 2], [1, 32]])
            ctm_t = ctm[:, t * DH:(t + 1) * DH]
            ctm_b = ctm_t.rearrange("p (o j) -> p o j", o=1)
            stm_t = stm[:, t * DH:(t + 1) * DH]
            stm_b = stm_t.rearrange("p (o s j) -> p o s j", o=1, s=2)
            t1v3 = t1.rearrange("p (h j) -> p h j", h=NH)
            t1v4 = t1.rearrange("p (h s j) -> p h s j", h=NH, s=2)
            t2v3 = t2.rearrange("p (h j) -> p h j", h=NH)
            p.emit("vector", lambda E, ksw=ksw, t1v4=t1v4, stm_b=stm_b:
                   E.tensor_tensor(t1v4, ksw,
                                   stm_b.to_broadcast((128, NH, 2, 32)),
                                   ALU.mult),
                   reads=[kv, stm], writes=[t1])
            p.emit("vector", lambda E, kv3=kv3, t2v3=t2v3, ctm_b=ctm_b:
                   E.tensor_tensor(t2v3, kv3,
                                   ctm_b.to_broadcast((128, NH, DH)), ALU.mult),
                   reads=[kv, ctm], writes=[t2])
            p.emit("vector", lambda E: E.tensor_tensor(t1, t1, t2, ALU.add),
                   reads=[t1, t2], writes=[t1])
            s16, ss16, m16, r16 = (sb["sm_a"], sb["sm_b"], sb["sm_c"],
                                   sb["sm_d"])
            p.emit("vector", lambda E, t1v3=t1v3: E.reduce_sum(
                s16, t1v3, axis=AX.X), reads=[t1], writes=[s16])
            p.emit("scalar", lambda E: E.activation(
                out=t2, in_=t1, func=AF.Square), reads=[t1], writes=[t2])
            p.emit("vector", lambda E, t2v3=t2v3: E.reduce_sum(
                ss16, t2v3, axis=AX.X), reads=[t2], writes=[ss16])
            p.emit("vector", lambda E: E.tensor_scalar_mul(m16, s16, 1.0 / DH),
                   reads=[s16], writes=[m16])
            p.emit("vector", lambda E: E.tensor_tensor(r16, m16, m16, ALU.mult),
                   reads=[m16], writes=[r16])
            p.emit("vector", lambda E: E.scalar_tensor_tensor(
                s16, ss16, 1.0 / DH, r16, ALU.mult, ALU.subtract),
                   reads=[ss16, r16], writes=[s16])
            p.emit("scalar", lambda E: E.activation(
                out=r16, in_=s16, func=AF.Sqrt, bias=sb["eps"], scale=1.0),
                   reads=[s16, sb["eps"]], writes=[r16])
            p.emit("vector", lambda E: E.reciprocal(r16, r16),
                   reads=[r16], writes=[r16])
            m16b = m16.rearrange("p (h j) -> p h j", j=1)
            r16b = r16.rearrange("p (h j) -> p h j", j=1)
            p.emit("vector", lambda E, t1v3=t1v3, m16b=m16b: E.tensor_tensor(
                t1v3, t1v3, m16b.to_broadcast((128, NH, DH)), ALU.subtract),
                   reads=[t1, m16], writes=[t1])
            p.emit("vector", lambda E, t1v3=t1v3, r16b=r16b: E.tensor_tensor(
                t1v3, t1v3, r16b.to_broadcast((128, NH, DH)), ALU.mult),
                   reads=[t1, r16], writes=[t1])
            p.emit("vector", lambda E: E.tensor_tensor(
                t2, t1, grep, ALU.mult), reads=[t1, grep], writes=[t2])
            p.emit("vector", lambda E, kv=kv: E.tensor_tensor(
                kv, t2, brep, ALU.add), reads=[t2, brep], writes=[kv])
        if "lkv" in tap_t:
            p.dma("gpsimd", tap_t["lkv"], BIGB, stream="tap")

    if phases >= 3:
        wsl = [bigc(0, 1024), bigc(1024, 1024)]
        qtmp = bigc(2048, 2048)
        qsw = bigc(4096, 512)
        qrot = bigc(4608, 2048)
        cfm = bigc(6656, 2048)
        sfm = bigc(8704, 2048)
        pswb = bigc(10752, 128)
        p.dma("sync", cfm, d["cfm_q"], stream="tab3")
        p.dma("sync", sfm, d["sfm_q"], stream="tab3")
        p.dma("sync", pswb, d["psw"], stream="tab3")
        for hp in range(NCH):
            wb = wsl[hp % 2]
            p.dma("sync", wb, d["wq_r"][hp], stream=f"wq{hp % 2}")
            for g in range(4):
                fns = []
                rds = [wb]
                for c in range(NCH):
                    fns.append(lambda E, wb=wb, c=c, g=g: E.matmul(
                        bank(g % 2), wb[:, c * 128:(c + 1) * 128],
                        biga(c, g * 512, (g + 1) * 512),
                        start=(c == 0), stop=(c == NCH - 1)))
                    rds.append(biga(c, g * 512, (g + 1) * 512))
                p.emit_group("tensor", fns, reads=rds, writes=[bank(g % 2)])
                p.emit("scalar", lambda E, g=g: E.copy(
                    qtmp[:, g * 512:(g + 1) * 512], bank(g % 2)),
                       reads=[bank(g % 2)],
                       writes=[qtmp[:, g * 512:(g + 1) * 512]])
            for g in range(4):
                p.emit("tensor", lambda E, g=g: E.matmul(
                    bank(2 + g % 2), pswb, qtmp[:, g * 512:(g + 1) * 512],
                    start=True, stop=True),
                       reads=[pswb, qtmp], writes=[bank(2 + g % 2)])
                p.emit("vector", lambda E, g=g: E.tensor_tensor(
                    qsw, bank(2 + g % 2), sfm[:, g * 512:(g + 1) * 512],
                    ALU.mult),
                       reads=[bank(2 + g % 2), sfm], writes=[qsw])
                p.emit("vector", lambda E, g=g: E.tensor_tensor(
                    qrot[:, g * 512:(g + 1) * 512],
                    qtmp[:, g * 512:(g + 1) * 512],
                    cfm[:, g * 512:(g + 1) * 512], ALU.mult),
                       reads=[qtmp, cfm],
                       writes=[qrot[:, g * 512:(g + 1) * 512]])
                p.emit("vector", lambda E, g=g: E.tensor_tensor(
                    qrot[:, g * 512:(g + 1) * 512],
                    qrot[:, g * 512:(g + 1) * 512], qsw, ALU.add),
                       reads=[qrot, qsw],
                       writes=[qrot[:, g * 512:(g + 1) * 512]])
            p.dma("gpsimd", d["qrot_sp"][hp], qrot, stream=f"qo{hp % 2}")
        if "qrot" in tap_t:
            for hp in range(NCH):
                p.dma("gpsimd", tap_t["qrot"][hp], d["qrot_sp"][hp],
                      stream="tap")

    if phases >= 4:
        QR = BIGA[:, 0:2048]
        LT = BIGA[:, 2048:4096]
        PH = [BIGA[:, 4096:6144], BIGA[:, 6144:8192]]
        PT = BIGA[:, 8192:10240]
        OP = BIGA[:, 10240:12288]
        CMK = BIGA[:, 12288:14336]
        p.dma("sync", CMK, d["cmask"].rearrange("p a b -> p (a b)"),
              stream="tab4")
        sacc = sb["sm_a"]
        nmx = sb["sm_b"]
        ssum = sb["sm_c"]
        rin = sb["sm_d"]

        def softmax_h(h01, nbank, nkey, qc, masked):
            b0 = h01 * nbank
            if masked:
                p.emit("vector", lambda E: E.tensor_tensor(
                    bank(b0 + 3), bank(b0 + 3),
                    CMK[:, qc * 512:(qc + 1) * 512], ALU.add),
                       reads=[bank(b0 + 3), CMK], writes=[bank(b0 + 3)])
            sview = PS[:, b0:b0 + nbank, :].rearrange("p a b -> p (a b)")
            nm = nmx[:, h01:h01 + 1]
            p.emit("vector", lambda E, sview=sview, nm=nm: E.reduce_max(
                nm, sview, axis=AX.X, negate=True),
                   reads=[PS[:, b0:b0 + nbank, :]], writes=[nm])
            for kg in range(nbank):
                p.emit("scalar", lambda E, kg=kg, nm=nm, h01=h01, b0=b0:
                       E.activation(
                           out=PH[h01][:, kg * 512:(kg + 1) * 512],
                           in_=bank(b0 + kg), func=AF.Exp, bias=nm, scale=1.0,
                           accum_out=sacc[:, h01 * 8 + kg:h01 * 8 + kg + 1]),
                       reads=[bank(b0 + kg), nm],
                       writes=[PH[h01][:, kg * 512:(kg + 1) * 512],
                               sacc[:, h01 * 8 + kg:h01 * 8 + kg + 1]])
            ss = ssum[:, h01:h01 + 1]
            rv = rin[:, h01:h01 + 1]
            p.emit("vector", lambda E, ss=ss, h01=h01, nbank=nbank:
                   E.reduce_sum(ss, sacc[:, h01 * 8:h01 * 8 + nbank],
                                axis=AX.X),
                   reads=[sacc], writes=[ss])
            p.emit("vector", lambda E, ss=ss, rv=rv: E.reciprocal(rv, ss),
                   reads=[ss], writes=[rv])
            p.emit("vector", lambda E, h01=h01, nkey=nkey, rv=rv:
                   E.tensor_scalar_mul(PH[h01][:, 0:nkey],
                                       PH[h01][:, 0:nkey], rv),
                   reads=[PH[h01], rv], writes=[PH[h01][:, 0:nkey]])

        def pt_and_v(hp, h01, nkc, opcol):
            h = 2 * hp + h01
            for kb in range((nkc + 3) // 4):
                kk = list(range(4 * kb, min(4 * kb + 4, nkc)))
                fns = []
                for ii, kchunk in enumerate(kk):
                    fns.append(lambda E, kchunk=kchunk, ii=ii, kb=kb:
                               E.transpose(
                                   PS[:, 6 + kb % 2, ii * 128:(ii + 1) * 128],
                                   PH[h01][:, kchunk * 128:(kchunk + 1) * 128],
                                   sb["ident"]))
                p.emit_group("tensor", fns, reads=[PH[h01], sb["ident"]],
                             writes=[PS[:, 6 + kb % 2, 0:128 * len(kk)]])
                if kb % 2 == 0:
                    p.emit("scalar", lambda E, kb=kb, n=len(kk): E.copy(
                        PT[:, kb * 512:kb * 512 + n * 128],
                        PS[:, 6 + kb % 2, 0:n * 128]),
                           reads=[PS[:, 6 + kb % 2, 0:128 * len(kk)]],
                           writes=[PT[:, kb * 512:kb * 512 + 128 * len(kk)]])
                else:
                    p.emit("vector", lambda E, kb=kb, n=len(kk): E.tensor_copy(
                        PT[:, kb * 512:kb * 512 + n * 128],
                        PS[:, 6 + kb % 2, 0:n * 128]),
                           reads=[PS[:, 6 + kb % 2, 0:128 * len(kk)]],
                           writes=[PT[:, kb * 512:kb * 512 + 128 * len(kk)]])
            vb = 4 + h01
            vb_ap = PS[64 * h01:64 * h01 + 64, vb, 0:128]
            fns = []
            for kchunk in range(nkc):
                fns.append(lambda E, kchunk=kchunk: E.matmul(
                    vb_ap, bigb_t(kchunk, h * 64, h * 64 + 64),
                    PT[:, kchunk * 128:(kchunk + 1) * 128],
                    start=(kchunk == 0), stop=(kchunk == nkc - 1),
                    tile_position=(0, 64 * h01)))
            p.emit_group("tensor", fns, reads=[PT, BIGB], writes=[vb_ap])
            p.emit("scalar", lambda E: E.copy(
                OP[64 * h01:64 * h01 + 64, opcol:opcol + 128], vb_ap),
                   reads=[vb_ap],
                   writes=[OP[64 * h01:64 * h01 + 64, opcol:opcol + 128]])

        for hp in range(NCH):
            p.dma("sync", QR, d["qrot_sp"][hp], stream=f"qi{hp % 3}")
            for t in range(NT):
                p.emit("tensor", lambda E, t=t, hp=hp: E.transpose(
                    bank(6 + t % 2, 0, 128),
                    bigb_t(t, hp * 128, (hp + 1) * 128), sb["ident"]),
                       reads=[BIGB, sb["ident"]],
                       writes=[bank(6 + t % 2, 0, 128)])
                eng = "scalar" if t % 2 == 0 else "vector"
                if t % 2 == 0:
                    p.emit("scalar", lambda E, t=t: E.copy(
                        LT[:, t * 128:(t + 1) * 128], bank(6 + t % 2, 0, 128)),
                           reads=[bank(6 + t % 2, 0, 128)],
                           writes=[LT[:, t * 128:(t + 1) * 128]])
                else:
                    p.emit("vector", lambda E, t=t: E.tensor_copy(
                        LT[:, t * 128:(t + 1) * 128], bank(6 + t % 2, 0, 128)),
                           reads=[bank(6 + t % 2, 0, 128)],
                           writes=[LT[:, t * 128:(t + 1) * 128]])
            for qc in range(4):
                qlo = 1536 + qc * 128
                for h01 in range(2):
                    for kg in range(4):
                        p.emit("tensor", lambda E, h01=h01, kg=kg, qlo=qlo:
                               E.matmul(bank(h01 * 4 + kg),
                                        QR[64 * h01:64 * h01 + 64,
                                           qlo:qlo + 128],
                                        LT[64 * h01:64 * h01 + 64,
                                           kg * 512:(kg + 1) * 512],
                                        start=True, stop=True,
                                        tile_position=(64 * h01, 0)),
                               reads=[QR, LT], writes=[bank(h01 * 4 + kg)])
                for h01 in range(2):
                    softmax_h(h01, 4, 2048, qc, True)
                for h01 in range(2):
                    pt_and_v(hp, h01, 16, qc * 128)
            for qc in range(12):
                qlo = qc * 128
                for h01 in range(2):
                    for kg in range(3):
                        p.emit("tensor", lambda E, h01=h01, kg=kg, qlo=qlo:
                               E.matmul(bank(h01 * 3 + kg),
                                        QR[64 * h01:64 * h01 + 64,
                                           qlo:qlo + 128],
                                        LT[64 * h01:64 * h01 + 64,
                                           kg * 512:(kg + 1) * 512],
                                        start=True, stop=True,
                                        tile_position=(64 * h01, 0)),
                               reads=[QR, LT], writes=[bank(h01 * 3 + kg)])
                for h01 in range(2):
                    softmax_h(h01, 3, 1536, qc, False)
                for h01 in range(2):
                    pt_and_v(hp, h01, 12, 512 + qc * 128)
            p.dma("gpsimd", d["opair_sp"][hp], OP, stream=f"oo{hp % 2}")
        if "opair" in tap_t:
            for hp in range(NCH):
                p.dma("gpsimd", tap_t["opair"][hp], d["opair_sp"][hp],
                      stream="tap")

    if phases >= 5:
        WOS = BIGB[:, 0:8192]
        XFI = BIGB[:, 8192:12288]
        OPI = BIGB[:, 12288:16384]
        for dm in range(NCH):
            p.dma("sync", WOS[:, dm * 1024:(dm + 1) * 1024], d["wo_r"][dm],
                  stream="wo")
        for g in range(4):
            for c in range(NCH):
                p.dma("sync", XFI[:, c * 512:(c + 1) * 512],
                      d["xfm_sp"][c][:, g * 512:(g + 1) * 512],
                      stream=f"xfi{g % 2}")
            for hp in range(NCH):
                p.dma("sync", OPI[:, hp * 512:(hp + 1) * 512],
                      d["opair_sp"][hp][:, g * 512:(g + 1) * 512],
                      stream=f"oi{g % 2}")
            for dm in range(NCH):
                fns = []
                for hp in range(NCH):
                    fns.append(lambda E, dm=dm, hp=hp: E.matmul(
                        bank(dm % 4),
                        WOS[:, dm * 1024 + hp * 128:dm * 1024 + hp * 128 + 128],
                        OPI[:, hp * 512:(hp + 1) * 512],
                        start=(hp == 0), stop=(hp == NCH - 1)))
                p.emit_group("tensor", fns, reads=[WOS, OPI],
                             writes=[bank(dm % 4)])
                p.emit("vector", lambda E, dm=dm, g=g: E.scalar_tensor_tensor(
                    biga(dm, g * 512, (g + 1) * 512), bank(dm % 4),
                    sb["bo_cm"][:, dm:dm + 1],
                    XFI[:, dm * 512:(dm + 1) * 512], ALU.add, ALU.add),
                       reads=[bank(dm % 4), sb["bo_cm"], XFI],
                       writes=[biga(dm, g * 512, (g + 1) * 512)])
        if "y1" in tap_t:
            p.dma("gpsimd", tap_t["y1"], BIGA, stream="tap")

    if phases >= 6:
        MB2 = BIGB[:, 0:2048]
        RB2 = BIGB[:, 2048:4096]
        SQT = BIGB[:, 4096:6144]
        st_s = bigc(0, 2048)[0:1, :]
        st_ss = bigc(2048, 2048)[0:1, :]
        st_m = bigc(4096, 2048)[0:1, :]
        st_r = bigc(6144, 2048)[0:1, :]
        for c in range(NCH):
            p.emit("scalar", lambda E, c=c: E.activation(
                out=SQT, in_=biga(c), func=AF.Square),
                   reads=[biga(c)], writes=[SQT])
            for g in range(4):
                p.emit("tensor", lambda E, c=c, g=g: E.matmul(
                    PS[0:1, g, :], sb["ones_col"],
                    biga(c, g * 512, (g + 1) * 512),
                    start=(c == 0), stop=(c == NCH - 1)),
                       reads=[sb["ones_col"], biga(c)],
                       writes=[PS[0:1, g, :]])
                p.emit("tensor", lambda E, c=c, g=g: E.matmul(
                    PS[0:1, 4 + g, :], sb["ones_col"],
                    SQT[:, g * 512:(g + 1) * 512],
                    start=(c == 0), stop=(c == NCH - 1)),
                       reads=[sb["ones_col"], SQT],
                       writes=[PS[0:1, 4 + g, :]])
        for g in range(4):
            p.emit("scalar", lambda E, g=g: E.copy(
                st_s[:, g * 512:(g + 1) * 512], PS[0:1, g, :]),
                   reads=[PS[0:1, g, :]],
                   writes=[st_s[:, g * 512:(g + 1) * 512]])
            p.emit("scalar", lambda E, g=g: E.copy(
                st_ss[:, g * 512:(g + 1) * 512], PS[0:1, 4 + g, :]),
                   reads=[PS[0:1, 4 + g, :]],
                   writes=[st_ss[:, g * 512:(g + 1) * 512]])
        p.emit("vector", lambda E: E.tensor_scalar_mul(st_m, st_s, 1.0 / DM),
               reads=[st_s], writes=[st_m])
        p.emit("vector", lambda E: E.tensor_tensor(st_r, st_m, st_m, ALU.mult),
               reads=[st_m], writes=[st_r])
        p.emit("vector", lambda E: E.scalar_tensor_tensor(
            st_s, st_ss, 1.0 / DM, st_r, ALU.mult, ALU.subtract),
               reads=[st_ss, st_r], writes=[st_s])
        p.emit("scalar", lambda E: E.activation(
            out=st_r, in_=st_s, func=AF.Sqrt, bias=sb["eps"][0:1, :],
            scale=1.0),
               reads=[st_s, sb["eps"]], writes=[st_r])
        p.emit("vector", lambda E: E.reciprocal(st_r, st_r),
               reads=[st_r], writes=[st_r])
        for g in range(4):
            p.emit("tensor", lambda E, g=g: E.matmul(
                PS[:, g % 2, :], sb["ones_row"],
                st_m[:, g * 512:(g + 1) * 512], start=True, stop=True),
                   reads=[sb["ones_row"], st_m], writes=[PS[:, g % 2, :]])
            p.emit("scalar", lambda E, g=g: E.copy(
                MB2[:, g * 512:(g + 1) * 512], PS[:, g % 2, :]),
                   reads=[PS[:, g % 2, :]],
                   writes=[MB2[:, g * 512:(g + 1) * 512]])
            p.emit("tensor", lambda E, g=g: E.matmul(
                PS[:, 2 + g % 2, :], sb["ones_row"],
                st_r[:, g * 512:(g + 1) * 512], start=True, stop=True),
                   reads=[sb["ones_row"], st_r], writes=[PS[:, 2 + g % 2, :]])
            p.emit("scalar", lambda E, g=g: E.copy(
                RB2[:, g * 512:(g + 1) * 512], PS[:, 2 + g % 2, :]),
                   reads=[PS[:, 2 + g % 2, :]],
                   writes=[RB2[:, g * 512:(g + 1) * 512]])

    if phases >= 7:
        MB2 = BIGB[:, 0:2048]
        RB2 = BIGB[:, 2048:4096]
        XN2 = BIGB[:, 4096:8192]
        ACTB = BIGB[:, 8192:16384]
        GS = [bigc(0, 1024), bigc(1024, 1024), bigc(2048, 1024),
              bigc(3072, 1024)]
        DSL = [bigc(4096, 2048), bigc(6144, 2048)]
        WBE = bigc(8192, 512)
        SIG = bigc(8704, 512)
        for g in range(4):
            glo = g * 512
            for c in range(NCH):
                p.emit("vector", lambda E, c=c, glo=glo: E.tensor_tensor(
                    XN2[:, c * 512:(c + 1) * 512], biga(c, glo, glo + 512),
                    MB2[:, glo:glo + 512], ALU.subtract),
                       reads=[biga(c, glo, glo + 512), MB2],
                       writes=[XN2[:, c * 512:(c + 1) * 512]])
                p.emit("vector", lambda E, c=c, glo=glo: E.tensor_tensor(
                    XN2[:, c * 512:(c + 1) * 512],
                    XN2[:, c * 512:(c + 1) * 512],
                    RB2[:, glo:glo + 512], ALU.mult),
                       reads=[XN2, RB2], writes=[XN2[:, c * 512:(c + 1) * 512]])
            if g == 0 and "xn2" in tap_t:
                p.dma("gpsimd", tap_t["xn2"], XN2, stream="tap")
            LGA = bigc(5376, 32)
            MXA = bigc(5408, 16)
            RVA = bigc(5424, 16)
            TMA = bigc(5440, 32)
            lg3 = LGA.rearrange("p (t e) -> p t e", e=8)
            tm3 = TMA.rearrange("p (t e) -> p t e", e=8)
            mx3 = MXA[:, 0:4].rearrange("p (t o) -> p t o", o=1)
            rv3 = RVA[:, 0:4].rearrange("p (t o) -> p t o", o=1)
            rb3 = sb["rb_rep"].rearrange("p (o e) -> p o e", o=1)
            for tch in range(4):
                fns = []
                for c in range(NCH):
                    fns.append(lambda E, c=c, tch=tch: E.matmul(
                        PS[:, 4 + tch, 0:8],
                        XN2[:, c * 512 + tch * 128:c * 512 + tch * 128 + 128],
                        sb["rt_sb"][:, c, :],
                        start=(c == 0), stop=(c == NCH - 1)))
                p.emit_group("tensor", fns, reads=[XN2, sb["rt_sb"]],
                             writes=[PS[:, 4 + tch, 0:8]])
            for tch in range(4):
                p.emit("scalar", lambda E, tch=tch: E.copy(
                    LGA[:, tch * 8:(tch + 1) * 8], PS[:, 4 + tch, 0:8]),
                       reads=[PS[:, 4 + tch, 0:8]],
                       writes=[LGA[:, tch * 8:(tch + 1) * 8]])
            p.emit("vector", lambda E: E.tensor_tensor(
                lg3, lg3, rb3.to_broadcast((128, 4, 8)), ALU.add),
                   reads=[LGA, sb["rb_rep"]], writes=[LGA])
            p.emit("vector", lambda E: E.reduce_max(
                MXA[:, 0:4], lg3, axis=AX.X, negate=True),
                   reads=[LGA], writes=[MXA])
            p.emit("vector", lambda E: E.tensor_tensor(
                lg3, lg3, mx3.to_broadcast((128, 4, 8)), ALU.add),
                   reads=[LGA, MXA], writes=[LGA])
            p.emit("scalar", lambda E: E.activation(
                out=LGA, in_=LGA, func=AF.Exp),
                   reads=[LGA], writes=[LGA])
            p.emit("vector", lambda E: E.reduce_sum(
                RVA[:, 0:4], lg3, axis=AX.X),
                   reads=[LGA], writes=[RVA])
            p.emit("vector", lambda E: E.reciprocal(
                RVA[:, 0:4], RVA[:, 0:4]), reads=[RVA], writes=[RVA])
            p.emit("vector", lambda E: E.tensor_tensor(
                lg3, lg3, rv3.to_broadcast((128, 4, 8)), ALU.mult),
                   reads=[LGA, RVA], writes=[LGA])
            p.emit("vector", lambda E: E.reduce_max(
                MXA[:, 0:4], lg3, axis=AX.X), reads=[LGA], writes=[MXA])
            p.emit("vector", lambda E: E.tensor_tensor(
                tm3, lg3, mx3.to_broadcast((128, 4, 8)), ALU.is_ge),
                   reads=[LGA, MXA], writes=[TMA])
            p.emit("vector", lambda E: E.scalar_tensor_tensor(
                TMA, TMA, -1e30, LGA, ALU.mult, ALU.add),
                   reads=[TMA, LGA], writes=[TMA])
            p.emit("vector", lambda E: E.reduce_max(
                MXA[:, 0:4], tm3, axis=AX.X), reads=[TMA], writes=[MXA])
            p.emit("vector", lambda E: E.tensor_tensor(
                tm3, lg3, mx3.to_broadcast((128, 4, 8)), ALU.is_ge),
                   reads=[LGA, MXA], writes=[TMA])
            p.emit("vector", lambda E: E.tensor_tensor(
                TMA, TMA, LGA, ALU.mult), reads=[TMA, LGA], writes=[TMA])
            for tch in range(4):
                bb = 2 + tch % 2
                p.emit("tensor", lambda E, tch=tch, bb=bb: E.transpose(
                    PS[0:8, bb, 0:128], TMA[:, tch * 8:(tch + 1) * 8],
                    sb["ident"]),
                       reads=[TMA, sb["ident"]], writes=[PS[0:8, bb, 0:128]])
                p.emit("scalar", lambda E, tch=tch, glo=glo, bb=bb: E.copy(
                    sb["w8"][:, glo + tch * 128:glo + tch * 128 + 128],
                    PS[0:8, bb, 0:128]),
                       reads=[PS[0:8, bb, 0:128]],
                       writes=[sb["w8"][:, glo + tch * 128:glo + tch * 128 + 128]])
            for dm in range(NCH):
                p.emit("tensor", lambda E, dm=dm, glo=glo: E.matmul(
                    PS[:, 4 + dm % 2, :], sb["dwb"][:, dm * 128:(dm + 1) * 128],
                    sb["w8"][:, glo:glo + 512], start=True, stop=True),
                       reads=[sb["dwb"], sb["w8"]],
                       writes=[PS[:, 4 + dm % 2, :]])
                p.emit("vector", lambda E, dm=dm, glo=glo: E.tensor_tensor(
                    biga(dm, glo, glo + 512), biga(dm, glo, glo + 512),
                    PS[:, 4 + dm % 2, :], ALU.add),
                       reads=[biga(dm, glo, glo + 512), PS[:, 4 + dm % 2, :]],
                       writes=[biga(dm, glo, glo + 512)])
            x2t = [bigc(0, 1024), bigc(1024, 1024)]
            for tch in range(4):
                xt2 = x2t[tch % 2]
                for c in range(NCH):
                    bb = c % 2
                    p.emit("tensor", lambda E, c=c, tch=tch, bb=bb: E.transpose(
                        bank(bb, 0, 128),
                        XN2[:, c * 512 + tch * 128:c * 512 + tch * 128 + 128],
                        sb["ident"]),
                           reads=[XN2, sb["ident"]], writes=[bank(bb, 0, 128)])
                    p.emit("scalar", lambda E, c=c, bb=bb, xt2=xt2: E.copy(
                        xt2[:, c * 128:(c + 1) * 128], bank(bb, 0, 128)),
                           reads=[bank(bb, 0, 128)],
                           writes=[xt2[:, c * 128:(c + 1) * 128]])
                row0 = glo + tch * 128
                p.dma("gpsimd", d["x2tm_sp"][row0:row0 + 128, 0:1024], xt2,
                      stream=f"x2o{tch % 2}")
            wstg = bigc(2048, 128)
            for toff in range(4):
                t = 4 * g + toff
                p.emit("tensor", lambda E, t=t, toff=toff: E.transpose(
                    bank(2 + toff % 2, 0, 8), sb["w8"][:, t * 128:(t + 1) * 128],
                    sb["ident"][0:8, 0:8]),
                       reads=[sb["w8"], sb["ident"]],
                       writes=[bank(2 + toff % 2, 0, 8)])
                p.emit("scalar", lambda E, t=t, toff=toff: E.copy(
                    wstg[:, toff * 8:toff * 8 + 8], bank(2 + toff % 2, 0, 8)),
                       reads=[bank(2 + toff % 2, 0, 8)],
                       writes=[wstg[:, toff * 8:toff * 8 + 8]])
                p.dma("gpsimd", d["x2tm_sp"][t * 128:(t + 1) * 128, 1024:1032],
                      wstg[:, toff * 8:toff * 8 + 8], stream=f"wto{t % 2}")
            yst = [bigc(3328, 1024), bigc(4352, 1024)]
            for toff in range(4):
                t = 4 * g + toff
                ys = yst[t % 2]
                for c in range(NCH):
                    bb = 4 + c % 2
                    p.emit("tensor", lambda E, t=t, c=c, bb=bb: E.transpose(
                        bank(bb, 0, 128), biga(c, t * 128, t * 128 + 128),
                        sb["ident"]),
                           reads=[biga(c, t * 128, t * 128 + 128),
                                  sb["ident"]],
                           writes=[bank(bb, 0, 128)])
                    if c % 2 == 0:
                        p.emit("scalar", lambda E, c=c, bb=bb, ys=ys: E.copy(
                            ys[:, c * 128:(c + 1) * 128], bank(bb, 0, 128)),
                               reads=[bank(bb, 0, 128)],
                               writes=[ys[:, c * 128:(c + 1) * 128]])
                    else:
                        p.emit("vector", lambda E, c=c, bb=bb, ys=ys:
                               E.tensor_copy(
                                   ys[:, c * 128:(c + 1) * 128],
                                   bank(bb, 0, 128)),
                               reads=[bank(bb, 0, 128)],
                               writes=[ys[:, c * 128:(c + 1) * 128]])
                p.dma("gpsimd", d["ytmp"][t * 128:(t + 1) * 128, :], ys,
                      stream=f"ylo{t % 2}")
        if "w8" in tap_t:
            p.dma("gpsimd", tap_t["w8"], sb["w8"], stream="tap")
        zr = bigc(2176, 1032)
        p.emit("vector", lambda E: E.memset(zr[0:1, :], 0.0),
               writes=[zr[0:1, :]])
        p.dma("gpsimd", d["x2tm_sp"][2048:2049, :], zr[0:1, :], stream="zr")
        p.dma("gpsimd", d["ytmp"][2048:2049, :], zr[0:1, 0:1024], stream="zr")


        FL8 = bigc(0, 2048)[0:8, :]
        CU8 = bigc(2048, 2048)[0:8, :]
        ZER = bigc(4096, 2048)[0:8, :]
        CUT = bigc(6144, 128)
        FLT = bigc(6272, 128)
        SLT = bigc(6400, 128)
        PEN = bigc(6528, 128)
        IXF = bigc(10240, 16)
        IXG = bigc(10256, 8)
        SELB = bigc(6656, 1024)
        IOTA = bigc(7680, 1024)
        TKO = bigc(8704, 32)
        p.dma("sync", IOTA, d["iota_r"], stream="tki")
        p.dma("sync", TKO, d["tokone"].rearrange("p a b -> p (a b)"),
              stream="tki")
        p.emit("vector", lambda E: E.memset(ZER, 0.0), writes=[ZER])
        p.emit("vector", lambda E: E.tensor_single_scalar(
            FL8, sb["w8"], 0.0, ALU.is_gt),
               reads=[sb["w8"]], writes=[FL8])
        p.emit("vector", lambda E: E.tensor_tensor_scan(
            CU8, FL8, ZER, 0.0, ALU.add, ALU.bypass),
               reads=[FL8, ZER], writes=[CU8])
        for t in range(NT):
            bb = 6 + t % 2
            p.emit("tensor", lambda E, t=t, bb=bb: E.transpose(
                bank(bb, 0, 8), CU8[:, t * 128:(t + 1) * 128],
                sb["ident"][0:8, 0:8]),
                   reads=[CU8, sb["ident"]], writes=[bank(bb, 0, 8)])
            p.emit("scalar", lambda E, t=t, bb=bb: E.copy(
                CUT[:, t * 8:(t + 1) * 8], bank(bb, 0, 8)),
                   reads=[bank(bb, 0, 8)], writes=[CUT[:, t * 8:(t + 1) * 8]])
            p.emit("tensor", lambda E, t=t, bb=bb: E.transpose(
                bank(bb, 8, 16), FL8[:, t * 128:(t + 1) * 128],
                sb["ident"][0:8, 0:8]),
                   reads=[FL8, sb["ident"]], writes=[bank(bb, 8, 16)])
            p.emit("scalar", lambda E, t=t, bb=bb: E.copy(
                FLT[:, t * 8:(t + 1) * 8], bank(bb, 8, 16)),
                   reads=[bank(bb, 8, 16)], writes=[FLT[:, t * 8:(t + 1) * 8]])
        p.emit("vector", lambda E: E.tensor_scalar_add(SLT, CUT, -1.0),
               reads=[CUT], writes=[SLT])
        p.emit("vector", lambda E: E.tensor_tensor(SLT, SLT, FLT, ALU.mult),
               reads=[SLT, FLT], writes=[SLT])
        p.emit("vector", lambda E: E.tensor_scalar(
            PEN, FLT, -100000.0, 100000.0, ALU.mult, ALU.add),
               reads=[FLT], writes=[PEN])
        p.emit("vector", lambda E: E.tensor_tensor(SLT, SLT, PEN, ALU.add),
               reads=[SLT, PEN], writes=[SLT])
        for e in range(8):
            for t in range(NT):
                p.emit("vector", lambda E, e=e, t=t: E.tensor_scalar(
                    SELB, IOTA, SLT[:, t * 8 + e:t * 8 + e + 1], None,
                    ALU.is_equal),
                       reads=[IOTA, SLT], writes=[SELB])
                fns = []
                for sn in range(8):
                    fns.append(lambda E, sn=sn, t=t: E.matmul(
                        PS[:, sn, 0:2], SELB[:, sn * 128:(sn + 1) * 128],
                        TKO[:, t * 2:t * 2 + 2],
                        start=(t == 0), stop=(t == NT - 1)))
                p.emit_group("tensor", fns, reads=[SELB, TKO],
                             writes=[PS[:, sn_, 0:2] for sn_ in range(8)])
            for sn in range(8):
                p.emit("scalar", lambda E, sn=sn: E.copy(
                    IXF[:, sn * 2:sn * 2 + 2], PS[:, sn, 0:2]),
                       reads=[PS[:, sn, 0:2]],
                       writes=[IXF[:, sn * 2:sn * 2 + 2]])
            idxv = bass.AP(BIGC.tensor, int(IXF.offset), [[12288, 128], [2, 8]])
            occv = bass.AP(BIGC.tensor, int(IXF.offset) + 1,
                           [[12288, 128], [2, 8]])
            p.emit("vector", lambda E: E.scalar_tensor_tensor(
                IXG, occv, -2048.0, idxv, ALU.mult, ALU.add),
                   reads=[IXF], writes=[IXG])
            p.emit("vector", lambda E: E.tensor_scalar_add(IXG, IXG, 2048.0),
                   reads=[IXG], writes=[IXG])
            p.emit("vector", lambda E, e=e: E.tensor_copy(
                sb["ixall"][:, e * 8:(e + 1) * 8], IXG),
                   reads=[IXG], writes=[sb["ixall"]])
        p.dma("gpsimd", d["ytmpb"][2048:2049, :], zr[0:1, 0:1024], stream="zr")
        zt16 = bigc(8736, 1024)
        p.emit("vector", lambda E: E.memset(zt16, 0.0), writes=[zt16])
        for t in range(NT):
            p.dma("gpsimd", d["ytmpb"][t * 128:(t + 1) * 128, :], zt16,
                  stream=f"zb{t % 2}")

        XG = BIGB[:, 0:4096]
        ACTB = BIGB[:, 4096:12288]
        DTL = BIGB[:, 12288:16384]
        GS = [bigc(0, 1024), bigc(1024, 1024), bigc(2048, 1024),
              bigc(3072, 1024)]
        DSL = [bigc(4096, 2048), bigc(6144, 2048)]
        GTI = [bigc(8192, 1032), bigc(9224, 1032)]
        WROW = bigc(10256, 512)
        WBE = bigc(10768, 512)
        SIG = bigc(11280, 512)
        for e in range(8):
            ixb = sb["ixall"][:, e * 8:(e + 1) * 8]
            for T in range(2):
                for s2 in range(4):
                    si = T * 4 + s2
                    gt_ = GTI[s2 % 2]
                    ix1 = ixb[:, si:si + 1]
                    p.idma("gpsimd", lambda E, gt_=gt_, ix1=ix1:
                           E.indirect_dma_start(
                               gt_, None, d["x2tm_sp"],
                               bass.IndirectOffsetOnAxis(ap=ix1, axis=0)),
                           reads=[("x2tm_sp", 0, 1, 0, 1 << 40), ix1],
                           writes=[gt_], stream=f"gth{s2 % 2}")
                    for cb in range(2):
                        bb = 6 + cb
                        fns = []
                        for cc in range(4):
                            c = cb * 4 + cc
                            fns.append(lambda E, c=c, cc=cc, gt_=gt_, bb=bb:
                                       E.transpose(
                                           bank(bb, cc * 128, (cc + 1) * 128),
                                           gt_[:, c * 128:(c + 1) * 128],
                                           sb["ident"]))
                        p.emit_group("tensor", fns, reads=[gt_, sb["ident"]],
                                     writes=[bank(bb)])
                        xg3 = XG.rearrange("p (c s) -> p c s", c=8)
                        xgd = xg3[:, cb * 4:(cb + 1) * 4,
                                  s2 * 128:(s2 + 1) * 128]
                        psv = PS[:, bb, :].rearrange("p (a b) -> p a b", a=4)
                        if cb == 0:
                            p.emit("scalar", lambda E, xgd=xgd, psv=psv:
                                   E.copy(xgd, psv),
                                   reads=[bank(bb)], writes=[XG])
                        else:
                            p.emit("vector", lambda E, xgd=xgd, psv=psv:
                                   E.tensor_copy(xgd, psv),
                                   reads=[bank(bb)], writes=[XG])
                    p.emit("tensor", lambda E, gt_=gt_, e=e: E.transpose(
                        PS[0:1, 5, 0:128], gt_[:, 1024 + e:1025 + e],
                        sb["ident"]),
                           reads=[gt_, sb["ident"]], writes=[PS[0:1, 5, 0:128]])
                    p.emit("scalar", lambda E, s2=s2: E.copy(
                        WROW[0:1, s2 * 128:(s2 + 1) * 128], PS[0:1, 5, 0:128]),
                           reads=[PS[0:1, 5, 0:128]],
                           writes=[WROW[0:1, s2 * 128:(s2 + 1) * 128]])
                p.emit("tensor", lambda E: E.matmul(
                    PS[:, 5, :], sb["ones_row"], WROW[0:1, :],
                    start=True, stop=True),
                       reads=[sb["ones_row"], WROW], writes=[PS[:, 5, :]])
                p.emit("scalar", lambda E: E.copy(WBE, PS[:, 5, :]),
                       reads=[PS[:, 5, :]], writes=[WBE])
                for mp in range(16):
                    gsl = GS[(2 * mp) % 4]
                    usl = GS[(2 * mp + 1) % 4]
                    p.dma("sync", gsl, d["gup_r"][e, mp], stream=f"gu{mp % 2}")
                    p.dma("sync", usl, d["gup_r"][e, 16 + mp],
                          stream=f"gv{mp % 2}")
                    bg = bank(mp % 2)
                    bu = bank(2 + mp % 2)
                    for wsl_, bk in ((gsl, bg), (usl, bu)):
                        fns = []
                        for c in range(NCH):
                            fns.append(lambda E, wsl_=wsl_, c=c, bk=bk:
                                       E.matmul(
                                           bk, wsl_[:, c * 128:(c + 1) * 128],
                                           XG[:, c * 512:(c + 1) * 512],
                                           start=(c == 0),
                                           stop=(c == NCH - 1)))
                        p.emit_group("tensor", fns, reads=[wsl_, XG],
                                     writes=[bk])
                    gcs = ACTB[:, mp * 512:(mp + 1) * 512]
                    p.emit("vector", lambda E, bg=bg, gcs=gcs, e=e, mp=mp:
                           E.tensor_scalar(
                               gcs, bg,
                               sb["gub1"][:, e * 32 + mp:e * 32 + mp + 1],
                               7.0, ALU.add, ALU.min),
                           reads=[bg, sb["gub1"]], writes=[gcs])
                    p.emit("scalar", lambda E, gcs=gcs: E.activation(
                        out=SIG, in_=gcs, func=AF.Sigmoid, scale=1.702),
                           reads=[gcs], writes=[SIG])
                    p.emit("vector", lambda E, gcs=gcs: E.tensor_tensor(
                        gcs, gcs, SIG, ALU.mult),
                           reads=[gcs, SIG], writes=[gcs])
                    p.emit("vector", lambda E, bu=bu, e=e, mp=mp:
                           E.tensor_scalar(
                               SIG, bu,
                               sb["gub1"][:, e * 32 + 16 + mp:e * 32 + 17 + mp],
                               8.0, ALU.add, ALU.min),
                           reads=[bu, sb["gub1"]], writes=[SIG])
                    p.emit("vector", lambda E: E.scalar_tensor_tensor(
                        SIG, SIG, -6.0, WBE, ALU.max, ALU.mult),
                           reads=[SIG, WBE], writes=[SIG])
                    p.emit("vector", lambda E, gcs=gcs: E.tensor_tensor(
                        gcs, gcs, SIG, ALU.mult),
                           reads=[gcs, SIG], writes=[gcs])
                if e == 0 and T == 0 and "act" in tap_t:
                    p.dma("gpsimd", tap_t["act"], ACTB, stream="tap")
                for dm in range(NCH):
                    dsl = DSL[dm % 2]
                    p.dma("sync", dsl, d["dwn_r"][e, dm], stream=f"dw{dm % 2}")
                    bk = bank(4 + dm % 2)
                    fns = []
                    for i in range(16):
                        fns.append(lambda E, i=i, bk=bk, dsl=dsl: E.matmul(
                            bk, dsl[:, i * 128:(i + 1) * 128],
                            ACTB[:, i * 512:(i + 1) * 512],
                            start=(i == 0), stop=(i == 15)))
                    p.emit_group("tensor", fns, reads=[dsl, ACTB], writes=[bk])
                    if dm % 2 == 0:
                        p.emit("scalar", lambda E, dm=dm, bk=bk: E.copy(
                            DTL[:, dm * 512:(dm + 1) * 512], bk),
                               reads=[bk],
                               writes=[DTL[:, dm * 512:(dm + 1) * 512]])
                    else:
                        p.emit("vector", lambda E, dm=dm, bk=bk: E.tensor_copy(
                            DTL[:, dm * 512:(dm + 1) * 512], bk),
                               reads=[bk],
                               writes=[DTL[:, dm * 512:(dm + 1) * 512]])
                dt3 = DTL.rearrange("p (c s) -> p c s", c=8)
                dtm = [bigc(8192, 1024), bigc(9216, 1024)]
                for s2 in range(4):
                    dstg = dtm[s2 % 2]
                    for cb in range(2):
                        bb = 6 + cb
                        fns = []
                        for cc in range(4):
                            dmc = cb * 4 + cc
                            fns.append(lambda E, dmc=dmc, cc=cc, s2=s2, bb=bb:
                                       E.transpose(
                                           bank(bb, cc * 128, (cc + 1) * 128),
                                           dt3[:, dmc,
                                              s2 * 128:(s2 + 1) * 128],
                                           sb["ident"]))
                        p.emit_group("tensor", fns, reads=[DTL, sb["ident"]],
                                     writes=[bank(bb)])
                        p.emit("scalar", lambda E, cb=cb, bb=bb, dstg=dstg:
                               E.copy(dstg[:, cb * 512:(cb + 1) * 512],
                                      bank(bb)),
                               reads=[bank(bb)],
                               writes=[dstg[:, cb * 512:(cb + 1) * 512]])
                    ix1 = ixb[:, T * 4 + s2:T * 4 + s2 + 1]
                    ytgt = d["ytmp"] if e % 2 == 0 else d["ytmpb"]
                    ynm = "ytmp" if e % 2 == 0 else "ytmpb"
                    p.idma("gpsimd", lambda E, dstg=dstg, ix1=ix1, ytgt=ytgt:
                           E.indirect_dma_start(
                               ytgt,
                               bass.IndirectOffsetOnAxis(ap=ix1, axis=0),
                               dstg, None, compute_op=ALU.add),
                           reads=[dstg, ix1, (ynm, 0, 1, 0, 1 << 40)],
                           writes=[(ynm, 0, 1, 0, 1 << 40)],
                           stream=f"sca{e % 2}")

        ACC = [BIGB[:, 0:1024], BIGB[:, 1024:2048]]
        BCC = [BIGB[:, 2048:3072], BIGB[:, 3072:4096]]
        for t in range(NT):
            acc = ACC[t % 2]
            bcc = BCC[t % 2]
            p.dma("sync", acc, d["ytmp"][t * 128:(t + 1) * 128, :],
                  stream=f"fin{t % 2}")
            p.dma("sync", bcc, d["ytmpb"][t * 128:(t + 1) * 128, :],
                  stream=f"fib{t % 2}")
            p.emit("vector", lambda E, acc=acc, bcc=bcc: E.tensor_tensor(
                acc, acc, bcc, ALU.add), reads=[acc, bcc], writes=[acc])
            p.dma("gpsimd", d["y"][t * 128:(t + 1) * 128, :], acc,
                  stream=f"fo{t % 2}")

    p.wait_all("sync")
    p.wait_all("gpsimd")
    p.finalize(stack)
    return nc, stack, p


def kernel(**inputs):
    tabs = host_tables()
    lay = host_layouts(inputs)
    x = np.ascontiguousarray(np.asarray(inputs["L0_attn_x"], dtype=np.float32))
    nc, stack, p = build_program()
    base = {}
    base.update(tabs)
    base.update(lay)
    in_maps = []
    for i in range(8):
        m = dict(base)
        m["x"] = np.ascontiguousarray(x[i])
        in_maps.append(m)
    res = run_bass_kernel_spmd(nc, in_maps, core_ids=list(range(8)))
    stack.close()
    out = np.stack([r["y"] for r in res.results], axis=0)
    return out


# version 19
# speedup vs baseline: 1.0097x; 1.0097x over previous
from contextlib import ExitStack
import numpy as np
import concourse.bass as bass
import concourse.mybir as mybir
from concourse.bass_utils import run_bass_kernel_spmd

F32 = mybir.dt.float32
I32 = mybir.dt.int32
AF = mybir.ActivationFunctionType
ALU = mybir.AluOpType
AX = mybir.AxisListType

ENGINES = ("tensor", "vector", "scalar", "gpsimd", "sync")



def ap_box(ap):
    t = ap.tensor
    esz = mybir.dt.size(ap.dtype)
    pairs = [list(pp) for pp in ap.ap]
    tname = type(t).__name__
    is_dram = "DRam" in tname
    off = int(ap.offset)
    if is_dram:
        plo, phi = 0, 1
        lo = hi = off * esz
        free = pairs
    else:
        pstep, pcount = pairs[0]
        base_part = getattr(t, "base_partition", 0) or 0
        pbase = off // pstep if pstep > 0 else 0
        foff = off - pbase * pstep if pstep > 0 else off
        plo = base_part + pbase
        phi = plo + pcount
        lo = hi = foff * esz
        free = pairs[1:]
    for step, count in free:
        dd = step * (count - 1) * esz
        if dd >= 0:
            hi += dd
        else:
            lo += dd
    hi += esz
    if "PSum" in tname:
        plo, phi = 0, 128
        lo = (lo // 2048) * 2048
        hi = ((hi + 2047) // 2048) * 2048
    return (t.name, plo, phi, lo, hi)


class _Item:
    __slots__ = ("fns", "waits", "inc", "sem")

    def __init__(self, fns, waits, inc, sem):
        self.fns = fns
        self.waits = waits
        self.inc = inc
        self.sem = sem


class _NameTrack:
    __slots__ = ("wring", "rring", "wfloor", "rfloor")
    RING = 192

    def __init__(self):
        self.wring = []
        self.rring = []
        self.wfloor = {}
        self.rfloor = {}

    def _push(self, ring, floor, entry):
        ring.append(entry)
        if len(ring) > self.RING:
            _, s, c = ring.pop(0)
            if c > floor.get(s, 0):
                floor[s] = c

    def add_w(self, box, sem, cnt):
        self._push(self.wring, self.wfloor, (box, sem, cnt))

    def add_r(self, box, sem, cnt):
        self._push(self.rring, self.rfloor, (box, sem, cnt))


def _olap(a, b):
    return (a[1] < b[2] and b[1] < a[2] and a[3] < b[4] and b[3] < a[4])


class Prog:
    def __init__(self, nc):
        self.nc = nc
        self.cnt = {e: 0 for e in ENGINES}
        self.is_dma_sem = {}
        self.guar = {e: {} for e in ENGINES}
        self.items = {e: [] for e in ENGINES}
        self.track = {}
        self.n_waits = 0
        self.n_overwaits = 0
        self.n_inst = 0

    def _tr(self, name):
        t = self.track.get(name)
        if t is None:
            t = self.track[name] = _NameTrack()
        return t

    def _deps(self, boxes_r, boxes_w):
        deps = {}

        def add(s, c):
            if c > deps.get(s, 0):
                deps[s] = c

        for box in boxes_r:
            t = self.track.get(box[0])
            if t is None:
                continue
            for s, c in t.wfloor.items():
                add(s, c)
            for wb, ws, wc in t.wring:
                if _olap(box, wb):
                    add(ws, wc)
        for box in boxes_w:
            t = self.track.get(box[0])
            if t is None:
                continue
            for s, c in t.wfloor.items():
                add(s, c)
            for s, c in t.rfloor.items():
                add(s, c)
            for wb, ws, wc in t.wring:
                if _olap(box, wb):
                    add(ws, wc)
            for rb, rs, rc in t.rring:
                if _olap(box, rb):
                    add(rs, rc)
        return deps

    def _plan(self, engine, fns, boxes_r, boxes_w, inc, sem):
        deps = self._deps(boxes_r, boxes_w)
        waits = []
        g = self.guar[engine]
        for s, c in sorted(deps.items()):
            if self.is_dma_sem.get(s) and c < self.cnt[s]:
                c = self.cnt[s]
                self.n_overwaits += 1
            if c > g.get(s, 0):
                waits.append((s, c))
                g[s] = c
        self.n_waits += len(waits)
        self.n_inst += len(fns)
        self.cnt[sem] += inc
        c = self.cnt[sem]
        self.items[engine].append(_Item(list(fns), waits, inc, sem))
        for b in boxes_w:
            self._tr(b[0]).add_w(b, sem, c)
        for b in boxes_r:
            self._tr(b[0]).add_r(b, sem, c)
        return c

    def emit(self, engine, fn, reads=(), writes=(), inc=1):
        return self.emit_group(engine, [fn], reads, writes, inc)

    def emit_group(self, engine, fns, reads=(), writes=(), inc=1):
        boxes_r = [b if isinstance(b, tuple) else ap_box(b) for b in reads]
        boxes_w = [b if isinstance(b, tuple) else ap_box(b) for b in writes]
        return self._plan(engine, fns, boxes_r, boxes_w, inc, engine)

    def dma(self, engine, out, in_, stream="d0", **kw):
        if stream not in self.cnt:
            self.cnt[stream] = 0
            self.is_dma_sem[stream] = True
        self.n_inst += 1
        return self._plan(engine,
                          [lambda E: E.dma_start(out=out, in_=in_, **kw)],
                          [ap_box(in_)], [ap_box(out)], 16, stream)

    def idma(self, engine, fn, reads=(), writes=(), stream="id0"):
        if stream not in self.cnt:
            self.cnt[stream] = 0
            self.is_dma_sem[stream] = True
        self.n_inst += 1
        boxes_r = [b if isinstance(b, tuple) else ap_box(b) for b in reads]
        boxes_w = [b if isinstance(b, tuple) else ap_box(b) for b in writes]
        return self._plan(engine, [fn], boxes_r, boxes_w, 16, stream)

    def wait_all(self, engine):
        waits = []
        g = self.guar[engine]
        for s, c in self.cnt.items():
            if c > g.get(s, 0):
                waits.append((s, c))
                g[s] = c
        self.items[engine].append(_Item([], waits, 0, engine))

    def finalize(self, stack):
        nc = self.nc
        self.sem = {s: stack.enter_context(nc.semaphore(f"S_{s}"))
                    for s in self.cnt}
        block = stack.enter_context(nc.Block())
        prog = self

        def section(eng_name):
            def body(E):
                sem = prog.sem
                for it in prog.items[eng_name]:
                    for (ds, dv) in it.waits:
                        E.wait_ge(sem[ds], dv)
                    ins = None
                    for fn in it.fns:
                        ins = fn(E)
                    if ins is not None and it.inc:
                        ins.then_inc(sem[it.sem], it.inc)
            return body

        for e in ENGINES:
            if self.items[e]:
                getattr(block, e)(section(e))
        return block



SEQ, DM, NH, DH = 2048, 1024, 16, 64
CTX, LAT = 1536, 512
NT = SEQ // 128
NCH = DM // 128
SCALE = 0.125


def _ref_trig():
    import jax
    import jax.numpy as jnp
    with jax.default_device(jax.devices("cpu")[0]):
        inv_freq = 1.0 / 10000.0 ** (jnp.arange(0, DH, 2,
                                                dtype=jnp.float32) / DH)
        freqs = jnp.arange(SEQ, dtype=jnp.float32)[:, None] * inv_freq[None, :]
        emb = jnp.concatenate([freqs, freqs], axis=-1)
        cos = np.asarray(jnp.cos(emb)).astype(np.float64)
        sin = np.asarray(jnp.sin(emb)).astype(np.float64)
    return cos, sin


def host_tables():
    cos64, sin64 = _ref_trig()
    sin64s = np.concatenate([-sin64[:, :32], sin64[:, 32:]], axis=1)
    tabs = {}
    j = np.arange(128) % 64
    tabs["cfm_q"] = np.ascontiguousarray(SCALE * cos64[:, j].T).astype(np.float32)
    tabs["sfm_q"] = np.ascontiguousarray(SCALE * sin64s[:, j].T).astype(np.float32)
    tabs["ctm"] = np.ascontiguousarray(
        cos64.reshape(NT, 128, DH).transpose(1, 0, 2)).astype(np.float32)
    tabs["stm"] = np.ascontiguousarray(
        sin64s.reshape(NT, 128, DH).transpose(1, 0, 2)).astype(np.float32)
    psw = np.zeros((128, 128), dtype=np.float32)
    m = np.arange(128)
    src = (m // 64) * 64 + ((m % 64) + 32) % 64
    psw[src, m] = 1.0
    tabs["psw"] = psw
    tabs["ident"] = np.eye(128, dtype=np.float32)
    pidx = np.arange(128)[:, None]
    kc = np.arange(512)[None, :]
    cm = np.zeros((128, 4, 512), dtype=np.float32)
    for qc in range(4):
        cm[:, qc, :] = np.where(kc <= 128 * qc + pidx, 0.0, -1e30)
    tabs["cmask"] = cm
    oh8 = np.zeros((8, 8, 128), dtype=np.float32)
    for e in range(8):
        oh8[e, e, :] = 1.0
    tabs["oh8"] = np.ascontiguousarray(oh8.transpose(1, 0, 2)).reshape(8, 1024)
    tabs["tokid"] = np.ascontiguousarray(
        (np.arange(16)[None, :] * 128 + np.arange(128)[:, None])
        .astype(np.int32))
    tk = (np.arange(16)[None, :] * 128 + np.arange(128)[:, None]).astype(np.float64)
    tabs["tokone"] = np.ascontiguousarray(
        np.stack([tk, np.ones_like(tk)], axis=2)).astype(np.float32)
    tabs["iota_r"] = np.ascontiguousarray(np.broadcast_to(
        np.arange(1024.0)[None, :], (128, 1024))).astype(np.float32)
    tabs["sent8"] = np.full((8, 1024), 2048, dtype=np.int32)
    tabs["eoff"] = np.ascontiguousarray(np.broadcast_to(
        (np.arange(8) * 1024.0)[None, :], (128, 8))).astype(np.float32)
    tabs["eclamp"] = np.ascontiguousarray(np.broadcast_to(
        (8192.0 + np.arange(8))[None, :], (128, 8))).astype(np.float32)
    tabs["eoff2"] = np.ascontiguousarray(np.broadcast_to(
        (np.arange(8) * 1025.0)[None, :], (128, 8))).astype(np.float32)
    return tabs


def host_layouts(ins):
    L = {}

    def wslab(W):
        return np.ascontiguousarray(
            np.asarray(W).reshape(8, 128, 8, 128).transpose(0, 3, 2, 1)
        ).reshape(8, 128, 1024)

    L["wq_r"] = wslab(ins["L0_attn_Wq"])
    L["wkv_r"] = wslab(ins["L0_attn_Wkv"])
    L["wo_r"] = wslab(ins["L0_attn_Wo"])
    gup = np.asarray(ins["L0_ffn_gate_up_proj"])
    L["gup_r"] = np.ascontiguousarray(
        gup.reshape(8, 8, 128, 32, 128).transpose(0, 3, 2, 1, 4)
    ).reshape(8, 32, 128, 1024)
    dwn = np.asarray(ins["L0_ffn_down_proj"])
    L["dwn_r"] = np.ascontiguousarray(
        dwn.reshape(8, 16, 128, 8, 128).transpose(0, 3, 2, 1, 4)
    ).reshape(8, 8, 128, 2048)
    rw = np.asarray(ins["L0_ffn_router_weight"])
    L["rt_r"] = np.ascontiguousarray(
        rw.T.reshape(8, 128, 8).transpose(1, 0, 2))
    L["rb_rep"] = np.ascontiguousarray(np.broadcast_to(
        np.asarray(ins["L0_ffn_router_bias"])[None, :], (128, 8)))
    L["bo_cm"] = np.ascontiguousarray(
        np.asarray(ins["L0_attn_bo"]).reshape(8, 128).T)
    L["gub_cm"] = np.ascontiguousarray(
        np.asarray(ins["L0_ffn_gate_up_bias"]).reshape(8, 32, 128)
        .transpose(2, 0, 1)).reshape(128, 256)
    L["dwb"] = np.ascontiguousarray(np.asarray(ins["L0_ffn_down_bias"]))
    L["g_rep"] = np.ascontiguousarray(np.broadcast_to(
        np.tile(np.asarray(ins["L0_attn_ln_g"]), NH)[None, :], (128, DM)))
    L["b_rep"] = np.ascontiguousarray(np.broadcast_to(
        np.tile(np.asarray(ins["L0_attn_ln_b"]), NH)[None, :], (128, DM)))
    return L



def build_program(taps=(), phases=99):
    nc = bass.Bass("TRN2", target_bir_lowering=False)
    stack = ExitStack()
    d = {}
    d["x"] = nc.dram_tensor("x", [SEQ, DM], F32, kind="ExternalInput").ap()
    for nm, shp in [("wq_r", [8, 128, 1024]), ("wkv_r", [8, 128, 1024]),
                    ("wo_r", [8, 128, 1024]), ("gup_r", [8, 32, 128, 1024]),
                    ("dwn_r", [8, 8, 128, 2048]), ("rt_r", [128, 8, 8]),
                    ("rb_rep", [128, 8]), ("bo_cm", [128, 8]),
                    ("gub_cm", [128, 256]), ("dwb", [8, 1024]),
                    ("g_rep", [128, DM]), ("b_rep", [128, DM]),
                    ("cfm_q", [128, SEQ]), ("sfm_q", [128, SEQ]),
                    ("ctm", [128, NT, DH]), ("stm", [128, NT, DH]),
                    ("psw", [128, 128]), ("ident", [128, 128]),
                    ("cmask", [128, 4, 512]), ("oh8", [8, 1024])]:
        d[nm] = nc.dram_tensor(nm, shp, F32, kind="ExternalInput").ap()
    d["y"] = nc.dram_tensor("y", [SEQ, DM], F32, kind="ExternalOutput").ap()
    d["tokid"] = nc.dram_tensor("tokid", [128, 16], I32,
                                kind="ExternalInput").ap()
    d["eoff"] = nc.dram_tensor("eoff", [128, 8], F32,
                               kind="ExternalInput").ap()
    d["eclamp"] = nc.dram_tensor("eclamp", [128, 8], F32,
                                 kind="ExternalInput").ap()
    d["eoff2"] = nc.dram_tensor("eoff2", [128, 8], F32,
                                kind="ExternalInput").ap()
    d["tokone"] = nc.dram_tensor("tokone", [128, 16, 2], F32,
                                 kind="ExternalInput").ap()
    d["iota_r"] = nc.dram_tensor("iota_r", [128, 1024], F32,
                                 kind="ExternalInput").ap()
    d["sent8"] = nc.dram_tensor("sent8", [8, 1024], I32,
                                kind="ExternalInput").ap()
    d["xfm_sp"] = nc.dram_tensor("xfm_sp", [NCH, 128, SEQ], F32).ap()
    d["x2tm_sp"] = nc.dram_tensor("x2tm_sp", [2049, 1032], F32).ap()
    d["idxtab"] = nc.dram_tensor("idxtab", [8200], I32).ap()
    d["ytmp"] = nc.dram_tensor("ytmp", [2049, 1024], F32).ap()
    d["ytmpb"] = nc.dram_tensor("ytmpb", [2049, 1024], F32).ap()
    d["qrot_sp"] = nc.dram_tensor("qrot_sp", [NCH, 128, SEQ], F32).ap()
    d["opair_sp"] = nc.dram_tensor("opair_sp", [NCH, 128, SEQ], F32).ap()
    tap_t = {}
    if "ddump" in taps:
        tap_t["ddump"] = nc.dram_tensor("tap_ddump", [8, 1025, 1024], F32,
                                        kind="ExternalOutput").ap()
    if "slotl" in taps:
        tap_t["slotl"] = nc.dram_tensor("tap_slotl", [128, 128], I32,
                                        kind="ExternalOutput").ap()
    for tname in taps:
        if tname in ("ddump", "slotl"):
            continue
        shp = {"xn": [128, NCH * SEQ], "lkv": [128, NT * DM],
               "qrot": [NCH, 128, SEQ], "opair": [NCH, 128, SEQ],
               "xfm": [NCH, 128, SEQ],
               "xn2": [128, 4096], "w8": [8, SEQ], "act": [128, 8192],
               "y1": [128, NCH * SEQ], "y2": [128, NCH * SEQ]}[tname]
        tap_t[tname] = nc.dram_tensor("tap_" + tname, shp, F32,
                                      kind="ExternalOutput").ap()

    sb = {}

    def S(name, shape):
        sb[name] = stack.enter_context(
            nc.sbuf_tensor("s_" + name, shape, F32)).ap()
        return sb[name]

    for nm, shp in [("rb_rep", [128, 8]), ("bo_cm", [128, 8]),
                    ("gub_cm", [128, 256]), ("dwb", [8, 1024]),
                    ("ident", [128, 128]), ("rt_sb", [128, 8, 8]),
                    ("eoff", [128, 8]), ("eclamp", [128, 8]),
                    ("eoff2", [128, 8]),
                    ("ones_col", [128, 1]),
                    ("ones_row", [1, 128]),
                    ("eps", [128, 1]), ("gub1", [128, 256]),
                    ("sm_a", [128, 16]), ("sm_b", [128, 16]),
                    ("sm_c", [128, 16]), ("sm_d", [128, 16]),
                    ("w8", [8, SEQ]), ("oh8", [8, 1024])]:
        S(nm, shp)
    sb["ixall"] = stack.enter_context(
        nc.sbuf_tensor("s_ixall", [128, 64], I32)).ap()
    BIGA = S("BIGA", [128, NCH * SEQ])
    BIGB = S("BIGB", [128, NT * DM])
    BIGC = S("BIGC", [128, 12288])
    PS = stack.enter_context(nc.psum_tensor("ps", [128, 8, 512], F32)).ap()

    p = Prog(nc)

    def biga(c, lo=0, hi=SEQ):
        return BIGA[:, c * SEQ + lo: c * SEQ + hi]

    def bigb_t(t, lo=0, hi=DM):
        return BIGB[:, t * DM + lo: t * DM + hi]

    def bank(b, lo=0, hi=512):
        return PS[:, b, lo:hi]

    def bigc(off, n):
        return BIGC[:, off:off + n]

    for nm in ["ident", "rb_rep", "bo_cm", "gub_cm", "dwb", "oh8", "eoff", "eclamp", "eoff2"]:
        p.dma("sync", sb[nm], d[nm], stream="tab")
    p.dma("sync", sb["rt_sb"], d["rt_r"], stream="tab")
    p.emit("vector", lambda E: E.memset(sb["ones_col"], 1.0),
           writes=[sb["ones_col"]])
    p.emit("vector", lambda E: E.memset(sb["ones_row"], 1.0),
           writes=[sb["ones_row"]])
    p.emit("vector", lambda E: E.memset(sb["eps"], 1e-5), writes=[sb["eps"]])
    p.emit("vector", lambda E: E.tensor_copy(
        sb["gub1"].rearrange("p (e m) -> p e m", e=8)[:, :, 0:16],
        sb["gub_cm"].rearrange("p (e m) -> p e m", e=8)[:, :, 0:16]),
           reads=[sb["gub_cm"]], writes=[sb["gub1"]])
    p.emit("vector", lambda E: E.tensor_scalar_add(
        sb["gub1"].rearrange("p (e m) -> p e m", e=8)[:, :, 16:32],
        sb["gub_cm"].rearrange("p (e m) -> p e m", e=8)[:, :, 16:32], 1.0),
           reads=[sb["gub_cm"]], writes=[sb["gub1"]])

    if phases >= 1:
        xt = [bigc(0, 1024), bigc(1024, 1024)]
        xn1 = bigc(2048, 1024)
        tfm = [bigc(3072, 1024), bigc(4096, 1024)]
        sqt = bigc(5120, 1024)
        for t in range(NT):
            xb = xt[t % 2]
            tf = tfm[t % 2]
            p.dma("sync", xb, d["x"][t * 128:(t + 1) * 128, :],
                  stream=f"ldx{t % 4}")
            s_ = sb["sm_a"][:, t:t + 1]
            ss_ = sb["sm_b"][:, t:t + 1]
            m_ = sb["sm_c"][:, t:t + 1]
            r_ = sb["sm_d"][:, t:t + 1]
            p.emit("vector", lambda E, xb=xb, s_=s_: E.reduce_sum(
                s_, xb, axis=AX.X), reads=[xb], writes=[s_])
            p.emit("scalar", lambda E, xb=xb: E.activation(
                out=sqt, in_=xb, func=AF.Square), reads=[xb], writes=[sqt])
            p.emit("vector", lambda E, ss_=ss_: E.reduce_sum(
                ss_, sqt, axis=AX.X), reads=[sqt], writes=[ss_])
            p.emit("vector", lambda E, s_=s_, m_=m_: E.tensor_scalar_mul(
                m_, s_, 1.0 / DM), reads=[s_], writes=[m_])
            p.emit("vector", lambda E, m_=m_, r_=r_: E.tensor_tensor(
                r_, m_, m_, ALU.mult), reads=[m_], writes=[r_])
            p.emit("vector", lambda E, ss_=ss_, r_=r_, s_=s_:
                   E.scalar_tensor_tensor(s_, ss_, 1.0 / DM, r_,
                                          ALU.mult, ALU.subtract),
                   reads=[ss_, r_], writes=[s_])
            p.emit("scalar", lambda E, s_=s_, r_=r_: E.activation(
                out=r_, in_=s_, func=AF.Sqrt, bias=sb["eps"], scale=1.0),
                   reads=[s_, sb["eps"]], writes=[r_])
            p.emit("vector", lambda E, r_=r_: E.reciprocal(r_, r_),
                   reads=[r_], writes=[r_])
            p.emit("vector", lambda E, xb=xb, m_=m_, r_=r_: E.tensor_scalar(
                xn1, xb, m_, r_, ALU.subtract, ALU.mult),
                   reads=[xb, m_, r_], writes=[xn1])
            for c in range(NCH):
                bb = c % 2
                p.emit("tensor", lambda E, xb=xb, c=c, bb=bb: E.transpose(
                    bank(bb, 0, 128), xb[:, c * 128:(c + 1) * 128],
                    sb["ident"]),
                       reads=[xb, sb["ident"]], writes=[bank(bb, 0, 128)])
                p.emit("scalar", lambda E, c=c, bb=bb, tf=tf: E.copy(
                    tf[:, c * 128:(c + 1) * 128], bank(bb, 0, 128)),
                       reads=[bank(bb, 0, 128)],
                       writes=[tf[:, c * 128:(c + 1) * 128]])
                p.emit("tensor", lambda E, c=c, bb=bb: E.transpose(
                    bank(bb, 128, 256), xn1[:, c * 128:(c + 1) * 128],
                    sb["ident"]),
                       reads=[xn1, sb["ident"]], writes=[bank(bb, 128, 256)])
                p.emit("scalar", lambda E, c=c, bb=bb, t=t: E.copy(
                    biga(c, t * 128, t * 128 + 128), bank(bb, 128, 256)),
                       reads=[bank(bb, 128, 256)],
                       writes=[biga(c, t * 128, t * 128 + 128)])
            xfm_dst = bass.AP(d["xfm_sp"].tensor, t * 128,
                              [[SEQ, 128], [128 * SEQ, NCH], [1, 128]])
            tfm_src = bass.AP(BIGC.tensor, int(tf.offset),
                              [[12288, 128], [128, NCH], [1, 128]])
            p.dma("gpsimd", xfm_dst, tfm_src, stream=f"xfo{t % 2}")
        if "xn" in tap_t:
            p.dma("gpsimd", tap_t["xn"], BIGA, stream="tap")
        if "xfm" in tap_t:
            for c in range(NCH):
                p.dma("gpsimd", tap_t["xfm"][c], d["xfm_sp"][c], stream="tap")

    if phases >= 2:
        wsl = [bigc(0, 1024), bigc(1024, 1024)]
        ktmp = bigc(2048, 512)
        t1 = bigc(2560, 1024)
        t2 = bigc(3584, 1024)
        ctm = bigc(4608, 1024)
        stm = bigc(5632, 1024)
        grep = bigc(6656, 1024)
        brep = bigc(7680, 1024)
        p.dma("sync", ctm, d["ctm"].rearrange("p t j -> p (t j)"), stream="tab2")
        p.dma("sync", stm, d["stm"].rearrange("p t j -> p (t j)"), stream="tab2")
        p.dma("sync", grep, d["g_rep"], stream="tab2")
        p.dma("sync", brep, d["b_rep"], stream="tab2")
        for mch in range(NCH):
            wb = wsl[mch % 2]
            p.dma("sync", wb, d["wkv_r"][mch], stream=f"wkv{mch % 2}")
            for g in range(4):
                fns = []
                rds = [wb]
                for c in range(NCH):
                    fns.append(lambda E, wb=wb, c=c, g=g: E.matmul(
                        bank(g % 2), wb[:, c * 128:(c + 1) * 128],
                        biga(c, g * 512, (g + 1) * 512),
                        start=(c == 0), stop=(c == NCH - 1)))
                    rds.append(biga(c, g * 512, (g + 1) * 512))
                p.emit_group("tensor", fns, reads=rds, writes=[bank(g % 2)])
                p.emit("scalar", lambda E, g=g: E.copy(ktmp, bank(g % 2)),
                       reads=[bank(g % 2)], writes=[ktmp])
                for qq in range(4):
                    t = g * 4 + qq
                    p.emit("tensor", lambda E, qq=qq: E.transpose(
                        bank(2 + (qq % 2), 0, 128),
                        ktmp[:, qq * 128:(qq + 1) * 128], sb["ident"]),
                           reads=[ktmp, sb["ident"]],
                           writes=[bank(2 + (qq % 2), 0, 128)])
                    p.emit("scalar", lambda E, t=t, mch=mch, qq=qq: E.copy(
                        bigb_t(t, mch * 128, (mch + 1) * 128),
                        bank(2 + (qq % 2), 0, 128)),
                           reads=[bank(2 + (qq % 2), 0, 128)],
                           writes=[bigb_t(t, mch * 128, (mch + 1) * 128)])
        for t in range(NT):
            kv = bigb_t(t)
            kv3 = kv.rearrange("p (h j) -> p h j", h=NH)
            ksw = bass.AP(BIGB.tensor, t * DM + 32,
                          [[NT * DM, 128], [DH, NH], [-32, 2], [1, 32]])
            ctm_t = ctm[:, t * DH:(t + 1) * DH]
            ctm_b = ctm_t.rearrange("p (o j) -> p o j", o=1)
            stm_t = stm[:, t * DH:(t + 1) * DH]
            stm_b = stm_t.rearrange("p (o s j) -> p o s j", o=1, s=2)
            t1v3 = t1.rearrange("p (h j) -> p h j", h=NH)
            t1v4 = t1.rearrange("p (h s j) -> p h s j", h=NH, s=2)
            t2v3 = t2.rearrange("p (h j) -> p h j", h=NH)
            p.emit("vector", lambda E, ksw=ksw, t1v4=t1v4, stm_b=stm_b:
                   E.tensor_tensor(t1v4, ksw,
                                   stm_b.to_broadcast((128, NH, 2, 32)),
                                   ALU.mult),
                   reads=[kv, stm], writes=[t1])
            p.emit("vector", lambda E, kv3=kv3, t2v3=t2v3, ctm_b=ctm_b:
                   E.tensor_tensor(t2v3, kv3,
                                   ctm_b.to_broadcast((128, NH, DH)), ALU.mult),
                   reads=[kv, ctm], writes=[t2])
            p.emit("vector", lambda E: E.tensor_tensor(t1, t1, t2, ALU.add),
                   reads=[t1, t2], writes=[t1])
            s16, ss16, m16, r16 = (sb["sm_a"], sb["sm_b"], sb["sm_c"],
                                   sb["sm_d"])
            p.emit("vector", lambda E, t1v3=t1v3: E.reduce_sum(
                s16, t1v3, axis=AX.X), reads=[t1], writes=[s16])
            p.emit("scalar", lambda E: E.activation(
                out=t2, in_=t1, func=AF.Square), reads=[t1], writes=[t2])
            p.emit("vector", lambda E, t2v3=t2v3: E.reduce_sum(
                ss16, t2v3, axis=AX.X), reads=[t2], writes=[ss16])
            p.emit("vector", lambda E: E.tensor_scalar_mul(m16, s16, 1.0 / DH),
                   reads=[s16], writes=[m16])
            p.emit("vector", lambda E: E.tensor_tensor(r16, m16, m16, ALU.mult),
                   reads=[m16], writes=[r16])
            p.emit("vector", lambda E: E.scalar_tensor_tensor(
                s16, ss16, 1.0 / DH, r16, ALU.mult, ALU.subtract),
                   reads=[ss16, r16], writes=[s16])
            p.emit("scalar", lambda E: E.activation(
                out=r16, in_=s16, func=AF.Sqrt, bias=sb["eps"], scale=1.0),
                   reads=[s16, sb["eps"]], writes=[r16])
            p.emit("vector", lambda E: E.reciprocal(r16, r16),
                   reads=[r16], writes=[r16])
            m16b = m16.rearrange("p (h j) -> p h j", j=1)
            r16b = r16.rearrange("p (h j) -> p h j", j=1)
            p.emit("vector", lambda E, t1v3=t1v3, m16b=m16b: E.tensor_tensor(
                t1v3, t1v3, m16b.to_broadcast((128, NH, DH)), ALU.subtract),
                   reads=[t1, m16], writes=[t1])
            p.emit("vector", lambda E, t1v3=t1v3, r16b=r16b: E.tensor_tensor(
                t1v3, t1v3, r16b.to_broadcast((128, NH, DH)), ALU.mult),
                   reads=[t1, r16], writes=[t1])
            p.emit("vector", lambda E: E.tensor_tensor(
                t2, t1, grep, ALU.mult), reads=[t1, grep], writes=[t2])
            p.emit("vector", lambda E, kv=kv: E.tensor_tensor(
                kv, t2, brep, ALU.add), reads=[t2, brep], writes=[kv])
        if "lkv" in tap_t:
            p.dma("gpsimd", tap_t["lkv"], BIGB, stream="tap")

    if phases >= 3:
        wsl = [bigc(0, 1024), bigc(1024, 1024)]
        qtmp = bigc(2048, 2048)
        qsw = bigc(4096, 512)
        qrot = bigc(4608, 2048)
        cfm = bigc(6656, 2048)
        sfm = bigc(8704, 2048)
        pswb = bigc(10752, 128)
        p.dma("sync", cfm, d["cfm_q"], stream="tab3")
        p.dma("sync", sfm, d["sfm_q"], stream="tab3")
        p.dma("sync", pswb, d["psw"], stream="tab3")
        for hp in range(NCH):
            wb = wsl[hp % 2]
            p.dma("sync", wb, d["wq_r"][hp], stream=f"wq{hp % 2}")
            for g in range(4):
                fns = []
                rds = [wb]
                for c in range(NCH):
                    fns.append(lambda E, wb=wb, c=c, g=g: E.matmul(
                        bank(g % 2), wb[:, c * 128:(c + 1) * 128],
                        biga(c, g * 512, (g + 1) * 512),
                        start=(c == 0), stop=(c == NCH - 1)))
                    rds.append(biga(c, g * 512, (g + 1) * 512))
                p.emit_group("tensor", fns, reads=rds, writes=[bank(g % 2)])
                p.emit("scalar", lambda E, g=g: E.copy(
                    qtmp[:, g * 512:(g + 1) * 512], bank(g % 2)),
                       reads=[bank(g % 2)],
                       writes=[qtmp[:, g * 512:(g + 1) * 512]])
            for g in range(4):
                p.emit("tensor", lambda E, g=g: E.matmul(
                    bank(2 + g % 2), pswb, qtmp[:, g * 512:(g + 1) * 512],
                    start=True, stop=True),
                       reads=[pswb, qtmp], writes=[bank(2 + g % 2)])
                p.emit("vector", lambda E, g=g: E.tensor_tensor(
                    qsw, bank(2 + g % 2), sfm[:, g * 512:(g + 1) * 512],
                    ALU.mult),
                       reads=[bank(2 + g % 2), sfm], writes=[qsw])
                p.emit("vector", lambda E, g=g: E.tensor_tensor(
                    qrot[:, g * 512:(g + 1) * 512],
                    qtmp[:, g * 512:(g + 1) * 512],
                    cfm[:, g * 512:(g + 1) * 512], ALU.mult),
                       reads=[qtmp, cfm],
                       writes=[qrot[:, g * 512:(g + 1) * 512]])
                p.emit("vector", lambda E, g=g: E.tensor_tensor(
                    qrot[:, g * 512:(g + 1) * 512],
                    qrot[:, g * 512:(g + 1) * 512], qsw, ALU.add),
                       reads=[qrot, qsw],
                       writes=[qrot[:, g * 512:(g + 1) * 512]])
            p.dma("gpsimd", d["qrot_sp"][hp], qrot, stream=f"qo{hp % 2}")
        if "qrot" in tap_t:
            for hp in range(NCH):
                p.dma("gpsimd", tap_t["qrot"][hp], d["qrot_sp"][hp],
                      stream="tap")

    if phases >= 4:
        QR = BIGA[:, 0:2048]
        LT = BIGA[:, 2048:4096]
        PH = [BIGA[:, 4096:6144], BIGA[:, 6144:8192]]
        PTX = [BIGA[:, 8192:10240], BIGA[:, 14336:16384]]
        OP = BIGA[:, 10240:12288]
        CMK = BIGA[:, 12288:14336]
        p.dma("sync", CMK, d["cmask"].rearrange("p a b -> p (a b)"),
              stream="tab4")
        sacc = sb["sm_a"]
        nmx = sb["sm_b"]
        ssum = sb["sm_c"]
        rin = sb["sm_d"]

        def softmax_h(h01, nbank, nkey, qc, masked):
            b0 = h01 * nbank
            if masked:
                p.emit("vector", lambda E: E.tensor_tensor(
                    bank(b0 + 3), bank(b0 + 3),
                    CMK[:, qc * 512:(qc + 1) * 512], ALU.add),
                       reads=[bank(b0 + 3), CMK], writes=[bank(b0 + 3)])
            sview = PS[:, b0:b0 + nbank, :].rearrange("p a b -> p (a b)")
            nm = nmx[:, h01:h01 + 1]
            p.emit("vector", lambda E, sview=sview, nm=nm: E.reduce_max(
                nm, sview, axis=AX.X, negate=True),
                   reads=[PS[:, b0:b0 + nbank, :]], writes=[nm])
            for kg in range(nbank):
                p.emit("scalar", lambda E, kg=kg, nm=nm, h01=h01, b0=b0:
                       E.activation(
                           out=PH[h01][:, kg * 512:(kg + 1) * 512],
                           in_=bank(b0 + kg), func=AF.Exp, bias=nm, scale=1.0,
                           accum_out=sacc[:, h01 * 8 + kg:h01 * 8 + kg + 1]),
                       reads=[bank(b0 + kg), nm],
                       writes=[PH[h01][:, kg * 512:(kg + 1) * 512],
                               sacc[:, h01 * 8 + kg:h01 * 8 + kg + 1]])
            ss = ssum[:, h01:h01 + 1]
            rv = rin[:, h01:h01 + 1]
            p.emit("vector", lambda E, ss=ss, h01=h01, nbank=nbank:
                   E.reduce_sum(ss, sacc[:, h01 * 8:h01 * 8 + nbank],
                                axis=AX.X),
                   reads=[sacc], writes=[ss])
            p.emit("vector", lambda E, ss=ss, rv=rv: E.reciprocal(rv, ss),
                   reads=[ss], writes=[rv])
            p.emit("vector", lambda E, h01=h01, nkey=nkey, rv=rv:
                   E.tensor_scalar_mul(PH[h01][:, 0:nkey],
                                       PH[h01][:, 0:nkey], rv),
                   reads=[PH[h01], rv], writes=[PH[h01][:, 0:nkey]])

        def pt_part(h01, nkc):
            PT = PTX[h01]
            for kb in range((nkc + 3) // 4):
                kk = list(range(4 * kb, min(4 * kb + 4, nkc)))
                fns = []
                for ii, kchunk in enumerate(kk):
                    fns.append(lambda E, kchunk=kchunk, ii=ii, kb=kb:
                               E.transpose(
                                   PS[:, 6 + kb % 2, ii * 128:(ii + 1) * 128],
                                   PH[h01][:, kchunk * 128:(kchunk + 1) * 128],
                                   sb["ident"]))
                p.emit_group("tensor", fns, reads=[PH[h01], sb["ident"]],
                             writes=[PS[:, 6 + kb % 2, 0:128 * len(kk)]])
                if kb % 2 == 0:
                    p.emit("scalar", lambda E, kb=kb, n=len(kk), PT=PT: E.copy(
                        PT[:, kb * 512:kb * 512 + n * 128],
                        PS[:, 6 + kb % 2, 0:n * 128]),
                           reads=[PS[:, 6 + kb % 2, 0:128 * len(kk)]],
                           writes=[PT[:, kb * 512:kb * 512 + 128 * len(kk)]])
                else:
                    p.emit("vector", lambda E, kb=kb, n=len(kk), PT=PT:
                           E.tensor_copy(
                               PT[:, kb * 512:kb * 512 + n * 128],
                               PS[:, 6 + kb % 2, 0:n * 128]),
                           reads=[PS[:, 6 + kb % 2, 0:128 * len(kk)]],
                           writes=[PT[:, kb * 512:kb * 512 + 128 * len(kk)]])

        def v_part(hp, h01, nkc, opcol):
            PT = PTX[h01]
            h = 2 * hp + h01
            vb = 4 + h01
            vb_ap = PS[64 * h01:64 * h01 + 64, vb, 0:128]
            fns = []
            for kchunk in range(nkc):
                fns.append(lambda E, kchunk=kchunk, PT=PT, vb_ap=vb_ap,
                           h=h, h01=h01, nkc=nkc: E.matmul(
                    vb_ap, bigb_t(kchunk, h * 64, h * 64 + 64),
                    PT[:, kchunk * 128:(kchunk + 1) * 128],
                    start=(kchunk == 0), stop=(kchunk == nkc - 1),
                    tile_position=(0, 64 * h01)))
            p.emit_group("tensor", fns, reads=[PT, BIGB], writes=[vb_ap])
            p.emit("scalar", lambda E, h01=h01, opcol=opcol, vb_ap=vb_ap:
                   E.copy(OP[64 * h01:64 * h01 + 64, opcol:opcol + 128],
                          vb_ap),
                   reads=[vb_ap],
                   writes=[OP[64 * h01:64 * h01 + 64, opcol:opcol + 128]])

        for hp in range(NCH):
            p.dma("sync", QR, d["qrot_sp"][hp], stream=f"qi{hp % 3}")
            for t in range(NT):
                p.emit("tensor", lambda E, t=t, hp=hp: E.transpose(
                    bank(6 + t % 2, 0, 128),
                    bigb_t(t, hp * 128, (hp + 1) * 128), sb["ident"]),
                       reads=[BIGB, sb["ident"]],
                       writes=[bank(6 + t % 2, 0, 128)])
                eng = "scalar" if t % 2 == 0 else "vector"
                if t % 2 == 0:
                    p.emit("scalar", lambda E, t=t: E.copy(
                        LT[:, t * 128:(t + 1) * 128], bank(6 + t % 2, 0, 128)),
                           reads=[bank(6 + t % 2, 0, 128)],
                           writes=[LT[:, t * 128:(t + 1) * 128]])
                else:
                    p.emit("vector", lambda E, t=t: E.tensor_copy(
                        LT[:, t * 128:(t + 1) * 128], bank(6 + t % 2, 0, 128)),
                           reads=[bank(6 + t % 2, 0, 128)],
                           writes=[LT[:, t * 128:(t + 1) * 128]])
            for qc in range(4):
                qlo = 1536 + qc * 128
                for h01 in range(2):
                    for kg in range(4):
                        p.emit("tensor", lambda E, h01=h01, kg=kg, qlo=qlo:
                               E.matmul(bank(h01 * 4 + kg),
                                        QR[64 * h01:64 * h01 + 64,
                                           qlo:qlo + 128],
                                        LT[64 * h01:64 * h01 + 64,
                                           kg * 512:(kg + 1) * 512],
                                        start=True, stop=True,
                                        tile_position=(64 * h01, 0)),
                               reads=[QR, LT], writes=[bank(h01 * 4 + kg)])
                for h01 in range(2):
                    softmax_h(h01, 4, 2048, qc, True)
                for h01 in range(2):
                    pt_part(h01, 16)
                for h01 in range(2):
                    v_part(hp, h01, 16, qc * 128)
            for qc in range(12):
                qlo = qc * 128
                for h01 in range(2):
                    for kg in range(3):
                        p.emit("tensor", lambda E, h01=h01, kg=kg, qlo=qlo:
                               E.matmul(bank(h01 * 3 + kg),
                                        QR[64 * h01:64 * h01 + 64,
                                           qlo:qlo + 128],
                                        LT[64 * h01:64 * h01 + 64,
                                           kg * 512:(kg + 1) * 512],
                                        start=True, stop=True,
                                        tile_position=(64 * h01, 0)),
                               reads=[QR, LT], writes=[bank(h01 * 3 + kg)])
                for h01 in range(2):
                    softmax_h(h01, 3, 1536, qc, False)
                for h01 in range(2):
                    pt_part(h01, 12)
                for h01 in range(2):
                    v_part(hp, h01, 12, 512 + qc * 128)
            p.dma("gpsimd", d["opair_sp"][hp], OP, stream=f"oo{hp % 2}")
        if "opair" in tap_t:
            for hp in range(NCH):
                p.dma("gpsimd", tap_t["opair"][hp], d["opair_sp"][hp],
                      stream="tap")

    if phases >= 5:
        WOS = BIGB[:, 0:8192]
        XFI = BIGB[:, 8192:12288]
        OPI = BIGB[:, 12288:16384]
        for dm in range(NCH):
            p.dma("sync", WOS[:, dm * 1024:(dm + 1) * 1024], d["wo_r"][dm],
                  stream="wo")
        for g in range(4):
            for c in range(NCH):
                p.dma("sync", XFI[:, c * 512:(c + 1) * 512],
                      d["xfm_sp"][c][:, g * 512:(g + 1) * 512],
                      stream=f"xfi{g % 2}")
            for hp in range(NCH):
                p.dma("sync", OPI[:, hp * 512:(hp + 1) * 512],
                      d["opair_sp"][hp][:, g * 512:(g + 1) * 512],
                      stream=f"oi{g % 2}")
            for dm in range(NCH):
                fns = []
                for hp in range(NCH):
                    fns.append(lambda E, dm=dm, hp=hp: E.matmul(
                        bank(dm % 4),
                        WOS[:, dm * 1024 + hp * 128:dm * 1024 + hp * 128 + 128],
                        OPI[:, hp * 512:(hp + 1) * 512],
                        start=(hp == 0), stop=(hp == NCH - 1)))
                p.emit_group("tensor", fns, reads=[WOS, OPI],
                             writes=[bank(dm % 4)])
                p.emit("vector", lambda E, dm=dm, g=g: E.scalar_tensor_tensor(
                    biga(dm, g * 512, (g + 1) * 512), bank(dm % 4),
                    sb["bo_cm"][:, dm:dm + 1],
                    XFI[:, dm * 512:(dm + 1) * 512], ALU.add, ALU.add),
                       reads=[bank(dm % 4), sb["bo_cm"], XFI],
                       writes=[biga(dm, g * 512, (g + 1) * 512)])
        if "y1" in tap_t:
            p.dma("gpsimd", tap_t["y1"], BIGA, stream="tap")

    if phases >= 6:
        MB2 = BIGB[:, 0:2048]
        RB2 = BIGB[:, 2048:4096]
        SQT = BIGB[:, 4096:6144]
        st_s = bigc(0, 2048)[0:1, :]
        st_ss = bigc(2048, 2048)[0:1, :]
        st_m = bigc(4096, 2048)[0:1, :]
        st_r = bigc(6144, 2048)[0:1, :]
        for c in range(NCH):
            p.emit("scalar", lambda E, c=c: E.activation(
                out=SQT, in_=biga(c), func=AF.Square),
                   reads=[biga(c)], writes=[SQT])
            for g in range(4):
                p.emit("tensor", lambda E, c=c, g=g: E.matmul(
                    PS[0:1, g, :], sb["ones_col"],
                    biga(c, g * 512, (g + 1) * 512),
                    start=(c == 0), stop=(c == NCH - 1)),
                       reads=[sb["ones_col"], biga(c)],
                       writes=[PS[0:1, g, :]])
                p.emit("tensor", lambda E, c=c, g=g: E.matmul(
                    PS[0:1, 4 + g, :], sb["ones_col"],
                    SQT[:, g * 512:(g + 1) * 512],
                    start=(c == 0), stop=(c == NCH - 1)),
                       reads=[sb["ones_col"], SQT],
                       writes=[PS[0:1, 4 + g, :]])
        for g in range(4):
            p.emit("scalar", lambda E, g=g: E.copy(
                st_s[:, g * 512:(g + 1) * 512], PS[0:1, g, :]),
                   reads=[PS[0:1, g, :]],
                   writes=[st_s[:, g * 512:(g + 1) * 512]])
            p.emit("scalar", lambda E, g=g: E.copy(
                st_ss[:, g * 512:(g + 1) * 512], PS[0:1, 4 + g, :]),
                   reads=[PS[0:1, 4 + g, :]],
                   writes=[st_ss[:, g * 512:(g + 1) * 512]])
        p.emit("vector", lambda E: E.tensor_scalar_mul(st_m, st_s, 1.0 / DM),
               reads=[st_s], writes=[st_m])
        p.emit("vector", lambda E: E.tensor_tensor(st_r, st_m, st_m, ALU.mult),
               reads=[st_m], writes=[st_r])
        p.emit("vector", lambda E: E.scalar_tensor_tensor(
            st_s, st_ss, 1.0 / DM, st_r, ALU.mult, ALU.subtract),
               reads=[st_ss, st_r], writes=[st_s])
        p.emit("scalar", lambda E: E.activation(
            out=st_r, in_=st_s, func=AF.Sqrt, bias=sb["eps"][0:1, :],
            scale=1.0),
               reads=[st_s, sb["eps"]], writes=[st_r])
        p.emit("vector", lambda E: E.reciprocal(st_r, st_r),
               reads=[st_r], writes=[st_r])
        for g in range(4):
            p.emit("tensor", lambda E, g=g: E.matmul(
                PS[:, g % 2, :], sb["ones_row"],
                st_m[:, g * 512:(g + 1) * 512], start=True, stop=True),
                   reads=[sb["ones_row"], st_m], writes=[PS[:, g % 2, :]])
            p.emit("scalar", lambda E, g=g: E.copy(
                MB2[:, g * 512:(g + 1) * 512], PS[:, g % 2, :]),
                   reads=[PS[:, g % 2, :]],
                   writes=[MB2[:, g * 512:(g + 1) * 512]])
            p.emit("tensor", lambda E, g=g: E.matmul(
                PS[:, 2 + g % 2, :], sb["ones_row"],
                st_r[:, g * 512:(g + 1) * 512], start=True, stop=True),
                   reads=[sb["ones_row"], st_r], writes=[PS[:, 2 + g % 2, :]])
            p.emit("scalar", lambda E, g=g: E.copy(
                RB2[:, g * 512:(g + 1) * 512], PS[:, 2 + g % 2, :]),
                   reads=[PS[:, 2 + g % 2, :]],
                   writes=[RB2[:, g * 512:(g + 1) * 512]])

    if phases >= 7:
        MB2 = BIGB[:, 0:2048]
        RB2 = BIGB[:, 2048:4096]
        XN2 = BIGB[:, 4096:8192]
        ACTB = BIGB[:, 8192:16384]
        GS = [bigc(0, 1024), bigc(1024, 1024), bigc(2048, 1024),
              bigc(3072, 1024)]
        DSL = [bigc(4096, 2048), bigc(6144, 2048)]
        WBE = bigc(8192, 512)
        SIG = bigc(8704, 512)
        for g in range(4):
            glo = g * 512
            for c in range(NCH):
                p.emit("vector", lambda E, c=c, glo=glo: E.tensor_tensor(
                    XN2[:, c * 512:(c + 1) * 512], biga(c, glo, glo + 512),
                    MB2[:, glo:glo + 512], ALU.subtract),
                       reads=[biga(c, glo, glo + 512), MB2],
                       writes=[XN2[:, c * 512:(c + 1) * 512]])
                p.emit("vector", lambda E, c=c, glo=glo: E.tensor_tensor(
                    XN2[:, c * 512:(c + 1) * 512],
                    XN2[:, c * 512:(c + 1) * 512],
                    RB2[:, glo:glo + 512], ALU.mult),
                       reads=[XN2, RB2], writes=[XN2[:, c * 512:(c + 1) * 512]])
            if g == 0 and "xn2" in tap_t:
                p.dma("gpsimd", tap_t["xn2"], XN2, stream="tap")
            LGA = bigc(5376, 32)
            MXA = bigc(5408, 16)
            RVA = bigc(5424, 16)
            TMA = bigc(5440, 32)
            lg3 = LGA.rearrange("p (t e) -> p t e", e=8)
            tm3 = TMA.rearrange("p (t e) -> p t e", e=8)
            mx3 = MXA[:, 0:4].rearrange("p (t o) -> p t o", o=1)
            rv3 = RVA[:, 0:4].rearrange("p (t o) -> p t o", o=1)
            rb3 = sb["rb_rep"].rearrange("p (o e) -> p o e", o=1)
            for tch in range(4):
                fns = []
                for c in range(NCH):
                    fns.append(lambda E, c=c, tch=tch: E.matmul(
                        PS[:, 4 + tch, 0:8],
                        XN2[:, c * 512 + tch * 128:c * 512 + tch * 128 + 128],
                        sb["rt_sb"][:, c, :],
                        start=(c == 0), stop=(c == NCH - 1)))
                p.emit_group("tensor", fns, reads=[XN2, sb["rt_sb"]],
                             writes=[PS[:, 4 + tch, 0:8]])
            for tch in range(4):
                p.emit("scalar", lambda E, tch=tch: E.copy(
                    LGA[:, tch * 8:(tch + 1) * 8], PS[:, 4 + tch, 0:8]),
                       reads=[PS[:, 4 + tch, 0:8]],
                       writes=[LGA[:, tch * 8:(tch + 1) * 8]])
            p.emit("vector", lambda E: E.tensor_tensor(
                lg3, lg3, rb3.to_broadcast((128, 4, 8)), ALU.add),
                   reads=[LGA, sb["rb_rep"]], writes=[LGA])
            p.emit("vector", lambda E: E.reduce_max(
                MXA[:, 0:4], lg3, axis=AX.X, negate=True),
                   reads=[LGA], writes=[MXA])
            p.emit("vector", lambda E: E.tensor_tensor(
                lg3, lg3, mx3.to_broadcast((128, 4, 8)), ALU.add),
                   reads=[LGA, MXA], writes=[LGA])
            p.emit("scalar", lambda E: E.activation(
                out=LGA, in_=LGA, func=AF.Exp),
                   reads=[LGA], writes=[LGA])
            p.emit("vector", lambda E: E.reduce_sum(
                RVA[:, 0:4], lg3, axis=AX.X),
                   reads=[LGA], writes=[RVA])
            p.emit("vector", lambda E: E.reciprocal(
                RVA[:, 0:4], RVA[:, 0:4]), reads=[RVA], writes=[RVA])
            p.emit("vector", lambda E: E.tensor_tensor(
                lg3, lg3, rv3.to_broadcast((128, 4, 8)), ALU.mult),
                   reads=[LGA, RVA], writes=[LGA])
            p.emit("vector", lambda E: E.reduce_max(
                MXA[:, 0:4], lg3, axis=AX.X), reads=[LGA], writes=[MXA])
            p.emit("vector", lambda E: E.tensor_tensor(
                tm3, lg3, mx3.to_broadcast((128, 4, 8)), ALU.is_ge),
                   reads=[LGA, MXA], writes=[TMA])
            p.emit("vector", lambda E: E.scalar_tensor_tensor(
                TMA, TMA, -1e30, LGA, ALU.mult, ALU.add),
                   reads=[TMA, LGA], writes=[TMA])
            p.emit("vector", lambda E: E.reduce_max(
                MXA[:, 0:4], tm3, axis=AX.X), reads=[TMA], writes=[MXA])
            p.emit("vector", lambda E: E.tensor_tensor(
                tm3, lg3, mx3.to_broadcast((128, 4, 8)), ALU.is_ge),
                   reads=[LGA, MXA], writes=[TMA])
            p.emit("vector", lambda E: E.tensor_tensor(
                TMA, TMA, LGA, ALU.mult), reads=[TMA, LGA], writes=[TMA])
            for tch in range(4):
                bb = 2 + tch % 2
                p.emit("tensor", lambda E, tch=tch, bb=bb: E.transpose(
                    PS[0:8, bb, 0:128], TMA[:, tch * 8:(tch + 1) * 8],
                    sb["ident"]),
                       reads=[TMA, sb["ident"]], writes=[PS[0:8, bb, 0:128]])
                p.emit("scalar", lambda E, tch=tch, glo=glo, bb=bb: E.copy(
                    sb["w8"][:, glo + tch * 128:glo + tch * 128 + 128],
                    PS[0:8, bb, 0:128]),
                       reads=[PS[0:8, bb, 0:128]],
                       writes=[sb["w8"][:, glo + tch * 128:glo + tch * 128 + 128]])
            for dm in range(NCH):
                p.emit("tensor", lambda E, dm=dm, glo=glo: E.matmul(
                    PS[:, 4 + dm % 2, :], sb["dwb"][:, dm * 128:(dm + 1) * 128],
                    sb["w8"][:, glo:glo + 512], start=True, stop=True),
                       reads=[sb["dwb"], sb["w8"]],
                       writes=[PS[:, 4 + dm % 2, :]])
                p.emit("vector", lambda E, dm=dm, glo=glo: E.tensor_tensor(
                    biga(dm, glo, glo + 512), biga(dm, glo, glo + 512),
                    PS[:, 4 + dm % 2, :], ALU.add),
                       reads=[biga(dm, glo, glo + 512), PS[:, 4 + dm % 2, :]],
                       writes=[biga(dm, glo, glo + 512)])
            x2t = [bigc(0, 1024), bigc(1024, 1024)]
            for tch in range(4):
                xt2 = x2t[tch % 2]
                for c in range(NCH):
                    bb = c % 2
                    p.emit("tensor", lambda E, c=c, tch=tch, bb=bb: E.transpose(
                        bank(bb, 0, 128),
                        XN2[:, c * 512 + tch * 128:c * 512 + tch * 128 + 128],
                        sb["ident"]),
                           reads=[XN2, sb["ident"]], writes=[bank(bb, 0, 128)])
                    p.emit("scalar", lambda E, c=c, bb=bb, xt2=xt2: E.copy(
                        xt2[:, c * 128:(c + 1) * 128], bank(bb, 0, 128)),
                           reads=[bank(bb, 0, 128)],
                           writes=[xt2[:, c * 128:(c + 1) * 128]])
                row0 = glo + tch * 128
                p.dma("gpsimd", d["x2tm_sp"][row0:row0 + 128, 0:1024], xt2,
                      stream=f"x2o{tch % 2}")
            wstg = bigc(2048, 128)
            for toff in range(4):
                t = 4 * g + toff
                p.emit("tensor", lambda E, t=t, toff=toff: E.transpose(
                    bank(2 + toff % 2, 0, 8), sb["w8"][:, t * 128:(t + 1) * 128],
                    sb["ident"][0:8, 0:8]),
                       reads=[sb["w8"], sb["ident"]],
                       writes=[bank(2 + toff % 2, 0, 8)])
                p.emit("scalar", lambda E, t=t, toff=toff: E.copy(
                    wstg[:, toff * 8:toff * 8 + 8], bank(2 + toff % 2, 0, 8)),
                       reads=[bank(2 + toff % 2, 0, 8)],
                       writes=[wstg[:, toff * 8:toff * 8 + 8]])
                p.dma("gpsimd", d["x2tm_sp"][t * 128:(t + 1) * 128, 1024:1032],
                      wstg[:, toff * 8:toff * 8 + 8], stream=f"wto{t % 2}")
            yst = [bigc(3328, 1024), bigc(4352, 1024)]
            for toff in range(4):
                t = 4 * g + toff
                ys = yst[t % 2]
                for c in range(NCH):
                    bb = 4 + c % 2
                    p.emit("tensor", lambda E, t=t, c=c, bb=bb: E.transpose(
                        bank(bb, 0, 128), biga(c, t * 128, t * 128 + 128),
                        sb["ident"]),
                           reads=[biga(c, t * 128, t * 128 + 128),
                                  sb["ident"]],
                           writes=[bank(bb, 0, 128)])
                    p.emit("scalar", lambda E, c=c, bb=bb, ys=ys: E.copy(
                        ys[:, c * 128:(c + 1) * 128], bank(bb, 0, 128)),
                           reads=[bank(bb, 0, 128)],
                           writes=[ys[:, c * 128:(c + 1) * 128]])
                p.dma("gpsimd", d["ytmp"][t * 128:(t + 1) * 128, :], ys,
                      stream=f"ylo{t % 2}")
        if "w8" in tap_t:
            p.dma("gpsimd", tap_t["w8"], sb["w8"], stream="tap")
        zr = bigc(2176, 1032)
        p.emit("vector", lambda E: E.memset(zr[0:1, :], 0.0),
               writes=[zr[0:1, :]])
        p.dma("gpsimd", d["x2tm_sp"][2048:2049, :], zr[0:1, :], stream="zr")
        p.dma("gpsimd", d["ytmp"][2048:2049, :], zr[0:1, 0:1024], stream="zr")


        FL8 = bigc(0, 2048)[0:8, :]
        CU8 = bigc(2048, 2048)[0:8, :]
        ZER = bigc(4096, 2048)[0:8, :]
        CUT = bigc(6144, 128)
        FLT = bigc(6272, 128)
        SLT = bigc(6400, 128)
        PEN = bigc(6528, 128)
        IXF = bigc(10240, 16)
        IXG = bigc(10256, 8)
        SELB = bigc(6656, 1024)
        IOTA = bigc(7680, 1024)
        TKO = bigc(8704, 32)
        p.dma("sync", IOTA, d["iota_r"], stream="tki")
        p.dma("sync", TKO, d["tokone"].rearrange("p a b -> p (a b)"),
              stream="tki")
        p.emit("vector", lambda E: E.memset(ZER, 0.0), writes=[ZER])
        p.emit("vector", lambda E: E.tensor_single_scalar(
            FL8, sb["w8"], 0.0, ALU.is_gt),
               reads=[sb["w8"]], writes=[FL8])
        p.emit("vector", lambda E: E.tensor_tensor_scan(
            CU8, FL8, ZER, 0.0, ALU.add, ALU.bypass),
               reads=[FL8, ZER], writes=[CU8])
        for t in range(NT):
            bb = 6 + t % 2
            p.emit("tensor", lambda E, t=t, bb=bb: E.transpose(
                bank(bb, 0, 8), CU8[:, t * 128:(t + 1) * 128],
                sb["ident"][0:8, 0:8]),
                   reads=[CU8, sb["ident"]], writes=[bank(bb, 0, 8)])
            p.emit("scalar", lambda E, t=t, bb=bb: E.copy(
                CUT[:, t * 8:(t + 1) * 8], bank(bb, 0, 8)),
                   reads=[bank(bb, 0, 8)], writes=[CUT[:, t * 8:(t + 1) * 8]])
            p.emit("tensor", lambda E, t=t, bb=bb: E.transpose(
                bank(bb, 8, 16), FL8[:, t * 128:(t + 1) * 128],
                sb["ident"][0:8, 0:8]),
                   reads=[FL8, sb["ident"]], writes=[bank(bb, 8, 16)])
            p.emit("scalar", lambda E, t=t, bb=bb: E.copy(
                FLT[:, t * 8:(t + 1) * 8], bank(bb, 8, 16)),
                   reads=[bank(bb, 8, 16)], writes=[FLT[:, t * 8:(t + 1) * 8]])
        p.emit("vector", lambda E: E.tensor_scalar_add(SLT, CUT, -1.0),
               reads=[CUT], writes=[SLT])
        p.emit("vector", lambda E: E.tensor_tensor(SLT, SLT, FLT, ALU.mult),
               reads=[SLT, FLT], writes=[SLT])
        p.emit("vector", lambda E: E.tensor_scalar(
            PEN, FLT, -100000.0, 100000.0, ALU.mult, ALU.add),
               reads=[FLT], writes=[PEN])
        p.emit("vector", lambda E: E.tensor_tensor(SLT, SLT, PEN, ALU.add),
               reads=[SLT, PEN], writes=[SLT])
        for e in range(8):
            for t in range(NT):
                p.emit("vector", lambda E, e=e, t=t: E.tensor_scalar(
                    SELB, IOTA, SLT[:, t * 8 + e:t * 8 + e + 1], None,
                    ALU.is_equal),
                       reads=[IOTA, SLT], writes=[SELB])
                fns = []
                for sn in range(8):
                    fns.append(lambda E, sn=sn, t=t: E.matmul(
                        PS[:, sn, 0:2], SELB[:, sn * 128:(sn + 1) * 128],
                        TKO[:, t * 2:t * 2 + 2],
                        start=(t == 0), stop=(t == NT - 1)))
                p.emit_group("tensor", fns, reads=[SELB, TKO],
                             writes=[PS[:, sn_, 0:2] for sn_ in range(8)])
            for sn in range(8):
                p.emit("scalar", lambda E, sn=sn: E.copy(
                    IXF[:, sn * 2:sn * 2 + 2], PS[:, sn, 0:2]),
                       reads=[PS[:, sn, 0:2]],
                       writes=[IXF[:, sn * 2:sn * 2 + 2]])
            idxv = bass.AP(BIGC.tensor, int(IXF.offset), [[12288, 128], [2, 8]])
            occv = bass.AP(BIGC.tensor, int(IXF.offset) + 1,
                           [[12288, 128], [2, 8]])
            p.emit("vector", lambda E: E.scalar_tensor_tensor(
                IXG, occv, -2048.0, idxv, ALU.mult, ALU.add),
                   reads=[IXF], writes=[IXG])
            p.emit("vector", lambda E: E.tensor_scalar_add(IXG, IXG, 2048.0),
                   reads=[IXG], writes=[IXG])
            p.emit("vector", lambda E, e=e: E.tensor_copy(
                sb["ixall"][:, e * 8:(e + 1) * 8], IXG),
                   reads=[IXG], writes=[sb["ixall"]])
        p.dma("gpsimd", d["ytmpb"][2048:2049, :], zr[0:1, 0:1024], stream="zr")
        zt16 = bigc(8736, 1024)
        p.emit("vector", lambda E: E.memset(zt16, 0.0), writes=[zt16])
        for t in range(NT):
            p.dma("gpsimd", d["ytmpb"][t * 128:(t + 1) * 128, :], zt16,
                  stream=f"zb{t % 2}")

        XG = BIGB[:, 0:4096]
        ACTB = BIGB[:, 4096:12288]
        DTL = BIGB[:, 12288:16384]
        GS = [bigc(0, 1024), bigc(1024, 1024), bigc(2048, 1024),
              bigc(3072, 1024)]
        DSL = [bigc(4096, 2048), bigc(6144, 2048)]
        GTI = [bigc(8192, 1032), bigc(9224, 1032)]
        WROW = bigc(10256, 512)
        WBE = bigc(10768, 512)
        SIG = bigc(11280, 512)
        for e in range(8):
            ixb = sb["ixall"][:, e * 8:(e + 1) * 8]
            for T in range(2):
                for s2 in range(4):
                    si = T * 4 + s2
                    gt_ = GTI[s2 % 2]
                    ix1 = ixb[:, si:si + 1]
                    p.idma("gpsimd", lambda E, gt_=gt_, ix1=ix1:
                           E.indirect_dma_start(
                               gt_, None, d["x2tm_sp"],
                               bass.IndirectOffsetOnAxis(ap=ix1, axis=0)),
                           reads=[("x2tm_sp", 0, 1, 0, 1 << 40), ix1],
                           writes=[gt_], stream=f"gth{s2 % 2}")
                    for cb in range(2):
                        bb = 6 + cb
                        fns = []
                        for cc in range(4):
                            c = cb * 4 + cc
                            fns.append(lambda E, c=c, cc=cc, gt_=gt_, bb=bb:
                                       E.transpose(
                                           bank(bb, cc * 128, (cc + 1) * 128),
                                           gt_[:, c * 128:(c + 1) * 128],
                                           sb["ident"]))
                        p.emit_group("tensor", fns, reads=[gt_, sb["ident"]],
                                     writes=[bank(bb)])
                        xg3 = XG.rearrange("p (c s) -> p c s", c=8)
                        p.emit("scalar", lambda E, cb=cb, s2=s2, bb=bb, xg3=xg3:
                               E.copy(xg3[:, cb * 4:(cb + 1) * 4,
                                          s2 * 128:(s2 + 1) * 128],
                                      PS[:, bb, :].rearrange(
                                          "p (a b) -> p a b", a=4)),
                               reads=[bank(bb)], writes=[XG])
                    p.emit("tensor", lambda E, gt_=gt_, e=e: E.transpose(
                        PS[0:1, 5, 0:128], gt_[:, 1024 + e:1025 + e],
                        sb["ident"]),
                           reads=[gt_, sb["ident"]], writes=[PS[0:1, 5, 0:128]])
                    p.emit("scalar", lambda E, s2=s2: E.copy(
                        WROW[0:1, s2 * 128:(s2 + 1) * 128], PS[0:1, 5, 0:128]),
                           reads=[PS[0:1, 5, 0:128]],
                           writes=[WROW[0:1, s2 * 128:(s2 + 1) * 128]])
                p.emit("tensor", lambda E: E.matmul(
                    PS[:, 5, :], sb["ones_row"], WROW[0:1, :],
                    start=True, stop=True),
                       reads=[sb["ones_row"], WROW], writes=[PS[:, 5, :]])
                p.emit("scalar", lambda E: E.copy(WBE, PS[:, 5, :]),
                       reads=[PS[:, 5, :]], writes=[WBE])
                for mp in range(16):
                    gsl = GS[(2 * mp) % 4]
                    usl = GS[(2 * mp + 1) % 4]
                    p.dma("sync", gsl, d["gup_r"][e, mp], stream=f"gu{mp % 2}")
                    p.dma("sync", usl, d["gup_r"][e, 16 + mp],
                          stream=f"gv{mp % 2}")
                    bg = bank(mp % 2)
                    bu = bank(2 + mp % 2)
                    for wsl_, bk in ((gsl, bg), (usl, bu)):
                        fns = []
                        for c in range(NCH):
                            fns.append(lambda E, wsl_=wsl_, c=c, bk=bk:
                                       E.matmul(
                                           bk, wsl_[:, c * 128:(c + 1) * 128],
                                           XG[:, c * 512:(c + 1) * 512],
                                           start=(c == 0),
                                           stop=(c == NCH - 1)))
                        p.emit_group("tensor", fns, reads=[wsl_, XG],
                                     writes=[bk])
                    gcs = ACTB[:, mp * 512:(mp + 1) * 512]
                    p.emit("vector", lambda E, bg=bg, gcs=gcs, e=e, mp=mp:
                           E.tensor_scalar(
                               gcs, bg,
                               sb["gub1"][:, e * 32 + mp:e * 32 + mp + 1],
                               7.0, ALU.add, ALU.min),
                           reads=[bg, sb["gub1"]], writes=[gcs])
                    p.emit("scalar", lambda E, gcs=gcs: E.activation(
                        out=SIG, in_=gcs, func=AF.Sigmoid, scale=1.702),
                           reads=[gcs], writes=[SIG])
                    p.emit("vector", lambda E, gcs=gcs: E.tensor_tensor(
                        gcs, gcs, SIG, ALU.mult),
                           reads=[gcs, SIG], writes=[gcs])
                    p.emit("vector", lambda E, bu=bu, e=e, mp=mp:
                           E.tensor_scalar(
                               SIG, bu,
                               sb["gub1"][:, e * 32 + 16 + mp:e * 32 + 17 + mp],
                               8.0, ALU.add, ALU.min),
                           reads=[bu, sb["gub1"]], writes=[SIG])
                    p.emit("vector", lambda E: E.scalar_tensor_tensor(
                        SIG, SIG, -6.0, WBE, ALU.max, ALU.mult),
                           reads=[SIG, WBE], writes=[SIG])
                    p.emit("vector", lambda E, gcs=gcs: E.tensor_tensor(
                        gcs, gcs, SIG, ALU.mult),
                           reads=[gcs, SIG], writes=[gcs])
                if e == 0 and T == 0 and "act" in tap_t:
                    p.dma("gpsimd", tap_t["act"], ACTB, stream="tap")
                for dm in range(NCH):
                    dsl = DSL[dm % 2]
                    p.dma("sync", dsl, d["dwn_r"][e, dm], stream=f"dw{dm % 2}")
                    bk = bank(4 + dm % 2)
                    fns = []
                    for i in range(16):
                        fns.append(lambda E, i=i, bk=bk, dsl=dsl: E.matmul(
                            bk, dsl[:, i * 128:(i + 1) * 128],
                            ACTB[:, i * 512:(i + 1) * 512],
                            start=(i == 0), stop=(i == 15)))
                    p.emit_group("tensor", fns, reads=[dsl, ACTB], writes=[bk])
                    p.emit("scalar", lambda E, dm=dm, bk=bk: E.copy(
                        DTL[:, dm * 512:(dm + 1) * 512], bk),
                           reads=[bk],
                           writes=[DTL[:, dm * 512:(dm + 1) * 512]])
                dt3 = DTL.rearrange("p (c s) -> p c s", c=8)
                dtm = [bigc(8192, 1024), bigc(9216, 1024)]
                for s2 in range(4):
                    dstg = dtm[s2 % 2]
                    for cb in range(2):
                        bb = 6 + cb
                        fns = []
                        for cc in range(4):
                            dmc = cb * 4 + cc
                            fns.append(lambda E, dmc=dmc, cc=cc, s2=s2, bb=bb:
                                       E.transpose(
                                           bank(bb, cc * 128, (cc + 1) * 128),
                                           dt3[:, dmc,
                                              s2 * 128:(s2 + 1) * 128],
                                           sb["ident"]))
                        p.emit_group("tensor", fns, reads=[DTL, sb["ident"]],
                                     writes=[bank(bb)])
                        p.emit("scalar", lambda E, cb=cb, bb=bb, dstg=dstg:
                               E.copy(dstg[:, cb * 512:(cb + 1) * 512],
                                      bank(bb)),
                               reads=[bank(bb)],
                               writes=[dstg[:, cb * 512:(cb + 1) * 512]])
                    ix1 = ixb[:, T * 4 + s2:T * 4 + s2 + 1]
                    ytgt = d["ytmp"] if e % 2 == 0 else d["ytmpb"]
                    ynm = "ytmp" if e % 2 == 0 else "ytmpb"
                    p.idma("gpsimd", lambda E, dstg=dstg, ix1=ix1, ytgt=ytgt:
                           E.indirect_dma_start(
                               ytgt,
                               bass.IndirectOffsetOnAxis(ap=ix1, axis=0),
                               dstg, None, compute_op=ALU.add),
                           reads=[dstg, ix1, (ynm, 0, 1, 0, 1 << 40)],
                           writes=[(ynm, 0, 1, 0, 1 << 40)],
                           stream=f"sca{e % 2}")

        ACC = [BIGB[:, 0:1024], BIGB[:, 1024:2048]]
        BCC = [BIGB[:, 2048:3072], BIGB[:, 3072:4096]]
        for t in range(NT):
            acc = ACC[t % 2]
            bcc = BCC[t % 2]
            p.dma("sync", acc, d["ytmp"][t * 128:(t + 1) * 128, :],
                  stream=f"fin{t % 2}")
            p.dma("sync", bcc, d["ytmpb"][t * 128:(t + 1) * 128, :],
                  stream=f"fib{t % 2}")
            p.emit("vector", lambda E, acc=acc, bcc=bcc: E.tensor_tensor(
                acc, acc, bcc, ALU.add), reads=[acc, bcc], writes=[acc])
            p.dma("gpsimd", d["y"][t * 128:(t + 1) * 128, :], acc,
                  stream=f"fo{t % 2}")

    p.wait_all("sync")
    p.wait_all("gpsimd")
    p.finalize(stack)
    return nc, stack, p


def kernel(**inputs):
    tabs = host_tables()
    lay = host_layouts(inputs)
    x = np.ascontiguousarray(np.asarray(inputs["L0_attn_x"], dtype=np.float32))
    nc, stack, p = build_program()
    base = {}
    base.update(tabs)
    base.update(lay)
    in_maps = []
    for i in range(8):
        m = dict(base)
        m["x"] = np.ascontiguousarray(x[i])
        in_maps.append(m)
    res = run_bass_kernel_spmd(nc, in_maps, core_ids=list(range(8)))
    stack.close()
    out = np.stack([r["y"] for r in res.results], axis=0)
    return out


# version 20
# speedup vs baseline: 1.0277x; 1.0178x over previous
from contextlib import ExitStack
import numpy as np
import concourse.bass as bass
import concourse.mybir as mybir
from concourse.bass_utils import run_bass_kernel_spmd

F32 = mybir.dt.float32
I32 = mybir.dt.int32
AF = mybir.ActivationFunctionType
ALU = mybir.AluOpType
AX = mybir.AxisListType

ENGINES = ("tensor", "vector", "scalar", "gpsimd", "sync")



def ap_box(ap):
    t = ap.tensor
    esz = mybir.dt.size(ap.dtype)
    pairs = [list(pp) for pp in ap.ap]
    tname = type(t).__name__
    is_dram = "DRam" in tname
    off = int(ap.offset)
    if is_dram:
        plo, phi = 0, 1
        lo = hi = off * esz
        free = pairs
    else:
        pstep, pcount = pairs[0]
        base_part = getattr(t, "base_partition", 0) or 0
        pbase = off // pstep if pstep > 0 else 0
        foff = off - pbase * pstep if pstep > 0 else off
        plo = base_part + pbase
        phi = plo + pcount
        lo = hi = foff * esz
        free = pairs[1:]
    for step, count in free:
        dd = step * (count - 1) * esz
        if dd >= 0:
            hi += dd
        else:
            lo += dd
    hi += esz
    if "PSum" in tname:
        plo, phi = 0, 128
        lo = (lo // 2048) * 2048
        hi = ((hi + 2047) // 2048) * 2048
    return (t.name, plo, phi, lo, hi)


class _Item:
    __slots__ = ("fns", "waits", "inc", "sem")

    def __init__(self, fns, waits, inc, sem):
        self.fns = fns
        self.waits = waits
        self.inc = inc
        self.sem = sem


class _NameTrack:
    __slots__ = ("wring", "rring", "wfloor", "rfloor")
    RING = 192

    def __init__(self):
        self.wring = []
        self.rring = []
        self.wfloor = {}
        self.rfloor = {}

    def _push(self, ring, floor, entry):
        ring.append(entry)
        if len(ring) > self.RING:
            _, s, c = ring.pop(0)
            if c > floor.get(s, 0):
                floor[s] = c

    def add_w(self, box, sem, cnt):
        self._push(self.wring, self.wfloor, (box, sem, cnt))

    def add_r(self, box, sem, cnt):
        self._push(self.rring, self.rfloor, (box, sem, cnt))


def _olap(a, b):
    return (a[1] < b[2] and b[1] < a[2] and a[3] < b[4] and b[3] < a[4])


class Prog:
    def __init__(self, nc):
        self.nc = nc
        self.cnt = {e: 0 for e in ENGINES}
        self.is_dma_sem = {}
        self.guar = {e: {} for e in ENGINES}
        self.items = {e: [] for e in ENGINES}
        self.track = {}
        self.n_waits = 0
        self.n_overwaits = 0
        self.n_inst = 0

    def _tr(self, name):
        t = self.track.get(name)
        if t is None:
            t = self.track[name] = _NameTrack()
        return t

    def _deps(self, boxes_r, boxes_w):
        deps = {}

        def add(s, c):
            if c > deps.get(s, 0):
                deps[s] = c

        for box in boxes_r:
            t = self.track.get(box[0])
            if t is None:
                continue
            for s, c in t.wfloor.items():
                add(s, c)
            for wb, ws, wc in t.wring:
                if _olap(box, wb):
                    add(ws, wc)
        for box in boxes_w:
            t = self.track.get(box[0])
            if t is None:
                continue
            for s, c in t.wfloor.items():
                add(s, c)
            for s, c in t.rfloor.items():
                add(s, c)
            for wb, ws, wc in t.wring:
                if _olap(box, wb):
                    add(ws, wc)
            for rb, rs, rc in t.rring:
                if _olap(box, rb):
                    add(rs, rc)
        return deps

    def _plan(self, engine, fns, boxes_r, boxes_w, inc, sem):
        deps = self._deps(boxes_r, boxes_w)
        waits = []
        g = self.guar[engine]
        for s, c in sorted(deps.items()):
            if self.is_dma_sem.get(s) and c < self.cnt[s]:
                c = self.cnt[s]
                self.n_overwaits += 1
            if c > g.get(s, 0):
                waits.append((s, c))
                g[s] = c
        self.n_waits += len(waits)
        self.n_inst += len(fns)
        self.cnt[sem] += inc
        c = self.cnt[sem]
        self.items[engine].append(_Item(list(fns), waits, inc, sem))
        for b in boxes_w:
            self._tr(b[0]).add_w(b, sem, c)
        for b in boxes_r:
            self._tr(b[0]).add_r(b, sem, c)
        return c

    def emit(self, engine, fn, reads=(), writes=(), inc=1):
        return self.emit_group(engine, [fn], reads, writes, inc)

    def emit_group(self, engine, fns, reads=(), writes=(), inc=1):
        boxes_r = [b if isinstance(b, tuple) else ap_box(b) for b in reads]
        boxes_w = [b if isinstance(b, tuple) else ap_box(b) for b in writes]
        return self._plan(engine, fns, boxes_r, boxes_w, inc, engine)

    def dma(self, engine, out, in_, stream="d0", **kw):
        if stream not in self.cnt:
            self.cnt[stream] = 0
            self.is_dma_sem[stream] = True
        self.n_inst += 1
        return self._plan(engine,
                          [lambda E: E.dma_start(out=out, in_=in_, **kw)],
                          [ap_box(in_)], [ap_box(out)], 16, stream)

    def idma(self, engine, fn, reads=(), writes=(), stream="id0"):
        if stream not in self.cnt:
            self.cnt[stream] = 0
            self.is_dma_sem[stream] = True
        self.n_inst += 1
        boxes_r = [b if isinstance(b, tuple) else ap_box(b) for b in reads]
        boxes_w = [b if isinstance(b, tuple) else ap_box(b) for b in writes]
        return self._plan(engine, [fn], boxes_r, boxes_w, 16, stream)

    def wait_all(self, engine):
        waits = []
        g = self.guar[engine]
        for s, c in self.cnt.items():
            if c > g.get(s, 0):
                waits.append((s, c))
                g[s] = c
        self.items[engine].append(_Item([], waits, 0, engine))

    def finalize(self, stack):
        nc = self.nc
        self.sem = {s: stack.enter_context(nc.semaphore(f"S_{s}"))
                    for s in self.cnt}
        block = stack.enter_context(nc.Block())
        prog = self

        def section(eng_name):
            def body(E):
                sem = prog.sem
                for it in prog.items[eng_name]:
                    for (ds, dv) in it.waits:
                        E.wait_ge(sem[ds], dv)
                    ins = None
                    for fn in it.fns:
                        ins = fn(E)
                    if ins is not None and it.inc:
                        ins.then_inc(sem[it.sem], it.inc)
            return body

        for e in ENGINES:
            if self.items[e]:
                getattr(block, e)(section(e))
        return block



SEQ, DM, NH, DH = 2048, 1024, 16, 64
CTX, LAT = 1536, 512
NT = SEQ // 128
NCH = DM // 128
SCALE = 0.125


def _ref_trig():
    import jax
    import jax.numpy as jnp
    with jax.default_device(jax.devices("cpu")[0]):
        inv_freq = 1.0 / 10000.0 ** (jnp.arange(0, DH, 2,
                                                dtype=jnp.float32) / DH)
        freqs = jnp.arange(SEQ, dtype=jnp.float32)[:, None] * inv_freq[None, :]
        emb = jnp.concatenate([freqs, freqs], axis=-1)
        cos = np.asarray(jnp.cos(emb)).astype(np.float64)
        sin = np.asarray(jnp.sin(emb)).astype(np.float64)
    return cos, sin


def host_tables():
    cos64, sin64 = _ref_trig()
    sin64s = np.concatenate([-sin64[:, :32], sin64[:, 32:]], axis=1)
    tabs = {}
    j = np.arange(128) % 64
    tabs["cfm_q"] = np.ascontiguousarray(SCALE * cos64[:, j].T).astype(np.float32)
    tabs["sfm_q"] = np.ascontiguousarray(SCALE * sin64s[:, j].T).astype(np.float32)
    tabs["ctm"] = np.ascontiguousarray(
        cos64.reshape(NT, 128, DH).transpose(1, 0, 2)).astype(np.float32)
    tabs["stm"] = np.ascontiguousarray(
        sin64s.reshape(NT, 128, DH).transpose(1, 0, 2)).astype(np.float32)
    psw = np.zeros((128, 128), dtype=np.float32)
    m = np.arange(128)
    src = (m // 64) * 64 + ((m % 64) + 32) % 64
    psw[src, m] = 1.0
    tabs["psw"] = psw
    tabs["ident"] = np.eye(128, dtype=np.float32)
    pidx = np.arange(128)[:, None]
    kc = np.arange(512)[None, :]
    cm = np.zeros((128, 4, 512), dtype=np.float32)
    for qc in range(4):
        cm[:, qc, :] = np.where(kc <= 128 * qc + pidx, 0.0, -1e30)
    tabs["cmask"] = cm
    oh8 = np.zeros((8, 8, 128), dtype=np.float32)
    for e in range(8):
        oh8[e, e, :] = 1.0
    tabs["oh8"] = np.ascontiguousarray(oh8.transpose(1, 0, 2)).reshape(8, 1024)
    tabs["tokid"] = np.ascontiguousarray(
        (np.arange(16)[None, :] * 128 + np.arange(128)[:, None])
        .astype(np.int32))
    tk = (np.arange(16)[None, :] * 128 + np.arange(128)[:, None]).astype(np.float64)
    tabs["tokone"] = np.ascontiguousarray(
        np.stack([tk, np.ones_like(tk)], axis=2)).astype(np.float32)
    tabs["iota_r"] = np.ascontiguousarray(np.broadcast_to(
        np.arange(1024.0)[None, :], (128, 1024))).astype(np.float32)
    tabs["sent8"] = np.full((8, 1024), 2048, dtype=np.int32)
    tabs["eoff"] = np.ascontiguousarray(np.broadcast_to(
        (np.arange(8) * 1024.0)[None, :], (128, 8))).astype(np.float32)
    tabs["eclamp"] = np.ascontiguousarray(np.broadcast_to(
        (8192.0 + np.arange(8))[None, :], (128, 8))).astype(np.float32)
    tabs["eoff2"] = np.ascontiguousarray(np.broadcast_to(
        (np.arange(8) * 1025.0)[None, :], (128, 8))).astype(np.float32)
    return tabs


def host_layouts(ins):
    L = {}

    def wslab(W):
        return np.ascontiguousarray(
            np.asarray(W).reshape(8, 128, 8, 128).transpose(0, 3, 2, 1)
        ).reshape(8, 128, 1024)

    L["wq_r"] = wslab(ins["L0_attn_Wq"])
    L["wkv_r"] = wslab(ins["L0_attn_Wkv"])
    L["wo_r"] = wslab(ins["L0_attn_Wo"])
    gup = np.asarray(ins["L0_ffn_gate_up_proj"])
    L["gup_r"] = np.ascontiguousarray(
        gup.reshape(8, 8, 128, 32, 128).transpose(0, 3, 2, 1, 4)
    ).reshape(8, 32, 128, 1024)
    dwn = np.asarray(ins["L0_ffn_down_proj"])
    L["dwn_r"] = np.ascontiguousarray(
        dwn.reshape(8, 16, 128, 8, 128).transpose(0, 3, 2, 1, 4)
    ).reshape(8, 8, 128, 2048)
    rw = np.asarray(ins["L0_ffn_router_weight"])
    L["rt_r"] = np.ascontiguousarray(
        rw.T.reshape(8, 128, 8).transpose(1, 0, 2))
    L["rb_rep"] = np.ascontiguousarray(np.broadcast_to(
        np.asarray(ins["L0_ffn_router_bias"])[None, :], (128, 8)))
    L["bo_cm"] = np.ascontiguousarray(
        np.asarray(ins["L0_attn_bo"]).reshape(8, 128).T)
    L["gub_cm"] = np.ascontiguousarray(
        np.asarray(ins["L0_ffn_gate_up_bias"]).reshape(8, 32, 128)
        .transpose(2, 0, 1)).reshape(128, 256)
    L["dwb"] = np.ascontiguousarray(np.asarray(ins["L0_ffn_down_bias"]))
    L["g_rep"] = np.ascontiguousarray(np.broadcast_to(
        np.tile(np.asarray(ins["L0_attn_ln_g"]), NH)[None, :], (128, DM)))
    L["b_rep"] = np.ascontiguousarray(np.broadcast_to(
        np.tile(np.asarray(ins["L0_attn_ln_b"]), NH)[None, :], (128, DM)))
    return L



def build_program(taps=(), phases=99):
    nc = bass.Bass("TRN2", target_bir_lowering=False)
    stack = ExitStack()
    d = {}
    d["x"] = nc.dram_tensor("x", [SEQ, DM], F32, kind="ExternalInput").ap()
    for nm, shp in [("wq_r", [8, 128, 1024]), ("wkv_r", [8, 128, 1024]),
                    ("wo_r", [8, 128, 1024]), ("gup_r", [8, 32, 128, 1024]),
                    ("dwn_r", [8, 8, 128, 2048]), ("rt_r", [128, 8, 8]),
                    ("rb_rep", [128, 8]), ("bo_cm", [128, 8]),
                    ("gub_cm", [128, 256]), ("dwb", [8, 1024]),
                    ("g_rep", [128, DM]), ("b_rep", [128, DM]),
                    ("cfm_q", [128, SEQ]), ("sfm_q", [128, SEQ]),
                    ("ctm", [128, NT, DH]), ("stm", [128, NT, DH]),
                    ("psw", [128, 128]), ("ident", [128, 128]),
                    ("cmask", [128, 4, 512]), ("oh8", [8, 1024])]:
        d[nm] = nc.dram_tensor(nm, shp, F32, kind="ExternalInput").ap()
    d["y"] = nc.dram_tensor("y", [SEQ, DM], F32, kind="ExternalOutput").ap()
    d["tokid"] = nc.dram_tensor("tokid", [128, 16], I32,
                                kind="ExternalInput").ap()
    d["eoff"] = nc.dram_tensor("eoff", [128, 8], F32,
                               kind="ExternalInput").ap()
    d["eclamp"] = nc.dram_tensor("eclamp", [128, 8], F32,
                                 kind="ExternalInput").ap()
    d["eoff2"] = nc.dram_tensor("eoff2", [128, 8], F32,
                                kind="ExternalInput").ap()
    d["tokone"] = nc.dram_tensor("tokone", [128, 16, 2], F32,
                                 kind="ExternalInput").ap()
    d["iota_r"] = nc.dram_tensor("iota_r", [128, 1024], F32,
                                 kind="ExternalInput").ap()
    d["sent8"] = nc.dram_tensor("sent8", [8, 1024], I32,
                                kind="ExternalInput").ap()
    d["xfm_sp"] = nc.dram_tensor("xfm_sp", [NCH, 128, SEQ], F32).ap()
    d["x2tm_sp"] = nc.dram_tensor("x2tm_sp", [2049, 1032], F32).ap()
    d["idxtab"] = nc.dram_tensor("idxtab", [8200], I32).ap()
    d["ytmp"] = nc.dram_tensor("ytmp", [2049, 1024], F32).ap()
    d["ytmpb"] = nc.dram_tensor("ytmpb", [2049, 1024], F32).ap()
    d["qrot_sp"] = nc.dram_tensor("qrot_sp", [NCH, 128, SEQ], F32).ap()
    d["opair_sp"] = nc.dram_tensor("opair_sp", [NCH, 128, SEQ], F32).ap()
    tap_t = {}
    if "ddump" in taps:
        tap_t["ddump"] = nc.dram_tensor("tap_ddump", [8, 1025, 1024], F32,
                                        kind="ExternalOutput").ap()
    if "slotl" in taps:
        tap_t["slotl"] = nc.dram_tensor("tap_slotl", [128, 128], I32,
                                        kind="ExternalOutput").ap()
    for tname in taps:
        if tname in ("ddump", "slotl"):
            continue
        shp = {"xn": [128, NCH * SEQ], "lkv": [128, NT * DM],
               "qrot": [NCH, 128, SEQ], "opair": [NCH, 128, SEQ],
               "xfm": [NCH, 128, SEQ],
               "xn2": [128, 4096], "w8": [8, SEQ], "act": [128, 8192],
               "y1": [128, NCH * SEQ], "y2": [128, NCH * SEQ]}[tname]
        tap_t[tname] = nc.dram_tensor("tap_" + tname, shp, F32,
                                      kind="ExternalOutput").ap()

    sb = {}

    def S(name, shape):
        sb[name] = stack.enter_context(
            nc.sbuf_tensor("s_" + name, shape, F32)).ap()
        return sb[name]

    for nm, shp in [("rb_rep", [128, 8]), ("bo_cm", [128, 8]),
                    ("gub_cm", [128, 256]), ("dwb", [8, 1024]),
                    ("ident", [128, 128]), ("rt_sb", [128, 8, 8]),
                    ("eoff", [128, 8]), ("eclamp", [128, 8]),
                    ("eoff2", [128, 8]),
                    ("ones_col", [128, 1]),
                    ("ones_row", [1, 128]),
                    ("eps", [128, 1]), ("gub1", [128, 256]),
                    ("sm_a", [128, 16]), ("sm_b", [128, 16]),
                    ("sm_c", [128, 16]), ("sm_d", [128, 16]),
                    ("w8", [8, SEQ]), ("oh8", [8, 1024])]:
        S(nm, shp)
    sb["ixall"] = stack.enter_context(
        nc.sbuf_tensor("s_ixall", [128, 64], I32)).ap()
    BIGA = S("BIGA", [128, NCH * SEQ])
    BIGB = S("BIGB", [128, NT * DM])
    BIGC = S("BIGC", [128, 12288])
    PS = stack.enter_context(nc.psum_tensor("ps", [128, 8, 512], F32)).ap()

    p = Prog(nc)

    def biga(c, lo=0, hi=SEQ):
        return BIGA[:, c * SEQ + lo: c * SEQ + hi]

    def bigb_t(t, lo=0, hi=DM):
        return BIGB[:, t * DM + lo: t * DM + hi]

    def bank(b, lo=0, hi=512):
        return PS[:, b, lo:hi]

    def bigc(off, n):
        return BIGC[:, off:off + n]

    for nm in ["ident", "rb_rep", "bo_cm", "gub_cm", "dwb", "oh8", "eoff", "eclamp", "eoff2"]:
        p.dma("sync", sb[nm], d[nm], stream="tab")
    p.dma("sync", sb["rt_sb"], d["rt_r"], stream="tab")
    p.emit("vector", lambda E: E.memset(sb["ones_col"], 1.0),
           writes=[sb["ones_col"]])
    p.emit("vector", lambda E: E.memset(sb["ones_row"], 1.0),
           writes=[sb["ones_row"]])
    p.emit("vector", lambda E: E.memset(sb["eps"], 1e-5), writes=[sb["eps"]])
    p.emit("vector", lambda E: E.tensor_copy(
        sb["gub1"].rearrange("p (e m) -> p e m", e=8)[:, :, 0:16],
        sb["gub_cm"].rearrange("p (e m) -> p e m", e=8)[:, :, 0:16]),
           reads=[sb["gub_cm"]], writes=[sb["gub1"]])
    p.emit("vector", lambda E: E.tensor_scalar_add(
        sb["gub1"].rearrange("p (e m) -> p e m", e=8)[:, :, 16:32],
        sb["gub_cm"].rearrange("p (e m) -> p e m", e=8)[:, :, 16:32], 1.0),
           reads=[sb["gub_cm"]], writes=[sb["gub1"]])

    if phases >= 1:
        xt = [bigc(0, 1024), bigc(1024, 1024)]
        xn1 = bigc(2048, 1024)
        tfm = [bigc(3072, 1024), bigc(4096, 1024)]
        sqt = bigc(5120, 1024)
        for t in range(NT):
            xb = xt[t % 2]
            tf = tfm[t % 2]
            p.dma("sync", xb, d["x"][t * 128:(t + 1) * 128, :],
                  stream=f"ldx{t % 4}")
            s_ = sb["sm_a"][:, t:t + 1]
            ss_ = sb["sm_b"][:, t:t + 1]
            m_ = sb["sm_c"][:, t:t + 1]
            r_ = sb["sm_d"][:, t:t + 1]
            p.emit("vector", lambda E, xb=xb, s_=s_: E.reduce_sum(
                s_, xb, axis=AX.X), reads=[xb], writes=[s_])
            p.emit("scalar", lambda E, xb=xb: E.activation(
                out=sqt, in_=xb, func=AF.Square), reads=[xb], writes=[sqt])
            p.emit("vector", lambda E, ss_=ss_: E.reduce_sum(
                ss_, sqt, axis=AX.X), reads=[sqt], writes=[ss_])
            p.emit("vector", lambda E, s_=s_, m_=m_: E.tensor_scalar_mul(
                m_, s_, 1.0 / DM), reads=[s_], writes=[m_])
            p.emit("vector", lambda E, m_=m_, r_=r_: E.tensor_tensor(
                r_, m_, m_, ALU.mult), reads=[m_], writes=[r_])
            p.emit("vector", lambda E, ss_=ss_, r_=r_, s_=s_:
                   E.scalar_tensor_tensor(s_, ss_, 1.0 / DM, r_,
                                          ALU.mult, ALU.subtract),
                   reads=[ss_, r_], writes=[s_])
            p.emit("scalar", lambda E, s_=s_, r_=r_: E.activation(
                out=r_, in_=s_, func=AF.Sqrt, bias=sb["eps"], scale=1.0),
                   reads=[s_, sb["eps"]], writes=[r_])
            p.emit("vector", lambda E, r_=r_: E.reciprocal(r_, r_),
                   reads=[r_], writes=[r_])
            p.emit("vector", lambda E, xb=xb, m_=m_, r_=r_: E.tensor_scalar(
                xn1, xb, m_, r_, ALU.subtract, ALU.mult),
                   reads=[xb, m_, r_], writes=[xn1])
            for c in range(NCH):
                bb = c % 2
                p.emit("tensor", lambda E, xb=xb, c=c, bb=bb: E.transpose(
                    bank(bb, 0, 128), xb[:, c * 128:(c + 1) * 128],
                    sb["ident"]),
                       reads=[xb, sb["ident"]], writes=[bank(bb, 0, 128)])
                p.emit("scalar", lambda E, c=c, bb=bb, tf=tf: E.copy(
                    tf[:, c * 128:(c + 1) * 128], bank(bb, 0, 128)),
                       reads=[bank(bb, 0, 128)],
                       writes=[tf[:, c * 128:(c + 1) * 128]])
                p.emit("tensor", lambda E, c=c, bb=bb: E.transpose(
                    bank(bb, 128, 256), xn1[:, c * 128:(c + 1) * 128],
                    sb["ident"]),
                       reads=[xn1, sb["ident"]], writes=[bank(bb, 128, 256)])
                p.emit("scalar", lambda E, c=c, bb=bb, t=t: E.copy(
                    biga(c, t * 128, t * 128 + 128), bank(bb, 128, 256)),
                       reads=[bank(bb, 128, 256)],
                       writes=[biga(c, t * 128, t * 128 + 128)])
            xfm_dst = bass.AP(d["xfm_sp"].tensor, t * 128,
                              [[SEQ, 128], [128 * SEQ, NCH], [1, 128]])
            tfm_src = bass.AP(BIGC.tensor, int(tf.offset),
                              [[12288, 128], [128, NCH], [1, 128]])
            p.dma("gpsimd", xfm_dst, tfm_src, stream=f"xfo{t % 2}")
        if "xn" in tap_t:
            p.dma("gpsimd", tap_t["xn"], BIGA, stream="tap")
        if "xfm" in tap_t:
            for c in range(NCH):
                p.dma("gpsimd", tap_t["xfm"][c], d["xfm_sp"][c], stream="tap")

    if phases >= 2:
        wsl = [bigc(0, 1024), bigc(1024, 1024)]
        ktmp = bigc(2048, 512)
        t1 = bigc(2560, 1024)
        t2 = bigc(3584, 1024)
        ctm = bigc(4608, 1024)
        stm = bigc(5632, 1024)
        grep = bigc(6656, 1024)
        brep = bigc(7680, 1024)
        p.dma("sync", ctm, d["ctm"].rearrange("p t j -> p (t j)"), stream="tab2")
        p.dma("sync", stm, d["stm"].rearrange("p t j -> p (t j)"), stream="tab2")
        p.dma("sync", grep, d["g_rep"], stream="tab2")
        p.dma("sync", brep, d["b_rep"], stream="tab2")
        for mch in range(NCH):
            wb = wsl[mch % 2]
            p.dma("sync", wb, d["wkv_r"][mch], stream=f"wkv{mch % 2}")
            for g in range(4):
                fns = []
                rds = [wb]
                for c in range(NCH):
                    fns.append(lambda E, wb=wb, c=c, g=g: E.matmul(
                        bank(g % 2), wb[:, c * 128:(c + 1) * 128],
                        biga(c, g * 512, (g + 1) * 512),
                        start=(c == 0), stop=(c == NCH - 1)))
                    rds.append(biga(c, g * 512, (g + 1) * 512))
                p.emit_group("tensor", fns, reads=rds, writes=[bank(g % 2)])
                p.emit("scalar", lambda E, g=g: E.copy(ktmp, bank(g % 2)),
                       reads=[bank(g % 2)], writes=[ktmp])
                for qq in range(4):
                    t = g * 4 + qq
                    p.emit("tensor", lambda E, qq=qq: E.transpose(
                        bank(2 + (qq % 2), 0, 128),
                        ktmp[:, qq * 128:(qq + 1) * 128], sb["ident"]),
                           reads=[ktmp, sb["ident"]],
                           writes=[bank(2 + (qq % 2), 0, 128)])
                    p.emit("scalar", lambda E, t=t, mch=mch, qq=qq: E.copy(
                        bigb_t(t, mch * 128, (mch + 1) * 128),
                        bank(2 + (qq % 2), 0, 128)),
                           reads=[bank(2 + (qq % 2), 0, 128)],
                           writes=[bigb_t(t, mch * 128, (mch + 1) * 128)])
        for t in range(NT):
            kv = bigb_t(t)
            kv3 = kv.rearrange("p (h j) -> p h j", h=NH)
            ksw = bass.AP(BIGB.tensor, t * DM + 32,
                          [[NT * DM, 128], [DH, NH], [-32, 2], [1, 32]])
            ctm_t = ctm[:, t * DH:(t + 1) * DH]
            ctm_b = ctm_t.rearrange("p (o j) -> p o j", o=1)
            stm_t = stm[:, t * DH:(t + 1) * DH]
            stm_b = stm_t.rearrange("p (o s j) -> p o s j", o=1, s=2)
            t1v3 = t1.rearrange("p (h j) -> p h j", h=NH)
            t1v4 = t1.rearrange("p (h s j) -> p h s j", h=NH, s=2)
            t2v3 = t2.rearrange("p (h j) -> p h j", h=NH)
            p.emit("vector", lambda E, ksw=ksw, t1v4=t1v4, stm_b=stm_b:
                   E.tensor_tensor(t1v4, ksw,
                                   stm_b.to_broadcast((128, NH, 2, 32)),
                                   ALU.mult),
                   reads=[kv, stm], writes=[t1])
            p.emit("vector", lambda E, kv3=kv3, t2v3=t2v3, ctm_b=ctm_b:
                   E.tensor_tensor(t2v3, kv3,
                                   ctm_b.to_broadcast((128, NH, DH)), ALU.mult),
                   reads=[kv, ctm], writes=[t2])
            p.emit("vector", lambda E: E.tensor_tensor(t1, t1, t2, ALU.add),
                   reads=[t1, t2], writes=[t1])
            s16, ss16, m16, r16 = (sb["sm_a"], sb["sm_b"], sb["sm_c"],
                                   sb["sm_d"])
            p.emit("vector", lambda E, t1v3=t1v3: E.reduce_sum(
                s16, t1v3, axis=AX.X), reads=[t1], writes=[s16])
            p.emit("scalar", lambda E: E.activation(
                out=t2, in_=t1, func=AF.Square), reads=[t1], writes=[t2])
            p.emit("vector", lambda E, t2v3=t2v3: E.reduce_sum(
                ss16, t2v3, axis=AX.X), reads=[t2], writes=[ss16])
            p.emit("vector", lambda E: E.tensor_scalar_mul(m16, s16, 1.0 / DH),
                   reads=[s16], writes=[m16])
            p.emit("vector", lambda E: E.tensor_tensor(r16, m16, m16, ALU.mult),
                   reads=[m16], writes=[r16])
            p.emit("vector", lambda E: E.scalar_tensor_tensor(
                s16, ss16, 1.0 / DH, r16, ALU.mult, ALU.subtract),
                   reads=[ss16, r16], writes=[s16])
            p.emit("scalar", lambda E: E.activation(
                out=r16, in_=s16, func=AF.Sqrt, bias=sb["eps"], scale=1.0),
                   reads=[s16, sb["eps"]], writes=[r16])
            p.emit("vector", lambda E: E.reciprocal(r16, r16),
                   reads=[r16], writes=[r16])
            m16b = m16.rearrange("p (h j) -> p h j", j=1)
            r16b = r16.rearrange("p (h j) -> p h j", j=1)
            p.emit("vector", lambda E, t1v3=t1v3, m16b=m16b: E.tensor_tensor(
                t1v3, t1v3, m16b.to_broadcast((128, NH, DH)), ALU.subtract),
                   reads=[t1, m16], writes=[t1])
            p.emit("vector", lambda E, t1v3=t1v3, r16b=r16b: E.tensor_tensor(
                t1v3, t1v3, r16b.to_broadcast((128, NH, DH)), ALU.mult),
                   reads=[t1, r16], writes=[t1])
            p.emit("vector", lambda E: E.tensor_tensor(
                t2, t1, grep, ALU.mult), reads=[t1, grep], writes=[t2])
            p.emit("vector", lambda E, kv=kv: E.tensor_tensor(
                kv, t2, brep, ALU.add), reads=[t2, brep], writes=[kv])
        if "lkv" in tap_t:
            p.dma("gpsimd", tap_t["lkv"], BIGB, stream="tap")

    if phases >= 3:
        wsl = [bigc(0, 1024), bigc(1024, 1024)]
        qtmp = bigc(2048, 2048)
        qsw = bigc(4096, 512)
        qrot = bigc(4608, 2048)
        cfm = bigc(6656, 2048)
        sfm = bigc(8704, 2048)
        pswb = bigc(10752, 128)
        p.dma("sync", cfm, d["cfm_q"], stream="tab3")
        p.dma("sync", sfm, d["sfm_q"], stream="tab3")
        p.dma("sync", pswb, d["psw"], stream="tab3")
        for hp in range(NCH):
            wb = wsl[hp % 2]
            p.dma("sync", wb, d["wq_r"][hp], stream=f"wq{hp % 2}")
            for g in range(4):
                fns = []
                rds = [wb]
                for c in range(NCH):
                    fns.append(lambda E, wb=wb, c=c, g=g: E.matmul(
                        bank(g % 2), wb[:, c * 128:(c + 1) * 128],
                        biga(c, g * 512, (g + 1) * 512),
                        start=(c == 0), stop=(c == NCH - 1)))
                    rds.append(biga(c, g * 512, (g + 1) * 512))
                p.emit_group("tensor", fns, reads=rds, writes=[bank(g % 2)])
                p.emit("scalar", lambda E, g=g: E.copy(
                    qtmp[:, g * 512:(g + 1) * 512], bank(g % 2)),
                       reads=[bank(g % 2)],
                       writes=[qtmp[:, g * 512:(g + 1) * 512]])
            for g in range(4):
                p.emit("tensor", lambda E, g=g: E.matmul(
                    bank(2 + g % 2), pswb, qtmp[:, g * 512:(g + 1) * 512],
                    start=True, stop=True),
                       reads=[pswb, qtmp], writes=[bank(2 + g % 2)])
                p.emit("vector", lambda E, g=g: E.tensor_tensor(
                    qsw, bank(2 + g % 2), sfm[:, g * 512:(g + 1) * 512],
                    ALU.mult),
                       reads=[bank(2 + g % 2), sfm], writes=[qsw])
                p.emit("vector", lambda E, g=g: E.tensor_tensor(
                    qrot[:, g * 512:(g + 1) * 512],
                    qtmp[:, g * 512:(g + 1) * 512],
                    cfm[:, g * 512:(g + 1) * 512], ALU.mult),
                       reads=[qtmp, cfm],
                       writes=[qrot[:, g * 512:(g + 1) * 512]])
                p.emit("vector", lambda E, g=g: E.tensor_tensor(
                    qrot[:, g * 512:(g + 1) * 512],
                    qrot[:, g * 512:(g + 1) * 512], qsw, ALU.add),
                       reads=[qrot, qsw],
                       writes=[qrot[:, g * 512:(g + 1) * 512]])
            p.dma("gpsimd", d["qrot_sp"][hp], qrot, stream=f"qo{hp % 2}")
        if "qrot" in tap_t:
            for hp in range(NCH):
                p.dma("gpsimd", tap_t["qrot"][hp], d["qrot_sp"][hp],
                      stream="tap")

    if phases >= 4:
        QRX = [BIGA[:, 0:2048], BIGA[:, 14336:16384]]
        LT = BIGA[:, 2048:4096]
        PH = [BIGA[:, 4096:6144], BIGA[:, 6144:8192]]
        PT = BIGA[:, 8192:10240]
        OP = BIGA[:, 10240:12288]
        CMK = BIGA[:, 12288:14336]
        p.dma("sync", CMK, d["cmask"].rearrange("p a b -> p (a b)"),
              stream="tab4")
        sacc = sb["sm_a"]
        nmx = sb["sm_b"]
        ssum = sb["sm_c"]
        rin = sb["sm_d"]

        def softmax_h(h01, nbank, nkey, qc, masked):
            b0 = h01 * nbank
            nm = nmx[:, h01:h01 + 1]
            if h01 == 0:
                if masked:
                    for bm in (3, nbank + 3):
                        p.emit("vector", lambda E, bm=bm: E.tensor_tensor(
                            bank(bm), bank(bm),
                            CMK[:, qc * 512:(qc + 1) * 512], ALU.add),
                               reads=[bank(bm), CMK], writes=[bank(bm)])
                sview = PS[:, 0:2 * nbank, :].rearrange(
                    "p (h g) b -> p h (g b)", h=2)
                p.emit("vector", lambda E, sview=sview: E.reduce_max(
                    nmx[:, 0:2], sview, axis=AX.X, negate=True),
                       reads=[PS[:, 0:2 * nbank, :]], writes=[nmx[:, 0:2]])
            for kg in range(nbank):
                p.emit("scalar", lambda E, kg=kg, nm=nm, h01=h01, b0=b0:
                       E.activation(
                           out=PH[h01][:, kg * 512:(kg + 1) * 512],
                           in_=bank(b0 + kg), func=AF.Exp, bias=nm, scale=1.0,
                           accum_out=sacc[:, h01 * 8 + kg:h01 * 8 + kg + 1]),
                       reads=[bank(b0 + kg), nm],
                       writes=[PH[h01][:, kg * 512:(kg + 1) * 512],
                               sacc[:, h01 * 8 + kg:h01 * 8 + kg + 1]])
            ss = ssum[:, h01:h01 + 1]
            rv = rin[:, h01:h01 + 1]
            p.emit("vector", lambda E, ss=ss, h01=h01, nbank=nbank:
                   E.reduce_sum(ss, sacc[:, h01 * 8:h01 * 8 + nbank],
                                axis=AX.X),
                   reads=[sacc], writes=[ss])
            p.emit("vector", lambda E, ss=ss, rv=rv: E.reciprocal(rv, ss),
                   reads=[ss], writes=[rv])
            p.emit("vector", lambda E, h01=h01, nkey=nkey, rv=rv:
                   E.tensor_scalar_mul(PH[h01][:, 0:nkey],
                                       PH[h01][:, 0:nkey], rv),
                   reads=[PH[h01], rv], writes=[PH[h01][:, 0:nkey]])

        def pt_and_v(hp, h01, nkc, opcol):
            h = 2 * hp + h01
            for kb in range((nkc + 3) // 4):
                kk = list(range(4 * kb, min(4 * kb + 4, nkc)))
                fns = []
                for ii, kchunk in enumerate(kk):
                    fns.append(lambda E, kchunk=kchunk, ii=ii, kb=kb:
                               E.transpose(
                                   PS[:, 6 + kb % 2, ii * 128:(ii + 1) * 128],
                                   PH[h01][:, kchunk * 128:(kchunk + 1) * 128],
                                   sb["ident"]))
                p.emit_group("tensor", fns, reads=[PH[h01], sb["ident"]],
                             writes=[PS[:, 6 + kb % 2, 0:128 * len(kk)]])
                if kb % 2 == 0:
                    p.emit("scalar", lambda E, kb=kb, n=len(kk): E.copy(
                        PT[:, kb * 512:kb * 512 + n * 128],
                        PS[:, 6 + kb % 2, 0:n * 128]),
                           reads=[PS[:, 6 + kb % 2, 0:128 * len(kk)]],
                           writes=[PT[:, kb * 512:kb * 512 + 128 * len(kk)]])
                else:
                    p.emit("vector", lambda E, kb=kb, n=len(kk): E.tensor_copy(
                        PT[:, kb * 512:kb * 512 + n * 128],
                        PS[:, 6 + kb % 2, 0:n * 128]),
                           reads=[PS[:, 6 + kb % 2, 0:128 * len(kk)]],
                           writes=[PT[:, kb * 512:kb * 512 + 128 * len(kk)]])
            vb = 4 + h01
            vb_ap = PS[64 * h01:64 * h01 + 64, vb, 0:128]
            fns = []
            for kchunk in range(nkc):
                fns.append(lambda E, kchunk=kchunk: E.matmul(
                    vb_ap, bigb_t(kchunk, h * 64, h * 64 + 64),
                    PT[:, kchunk * 128:(kchunk + 1) * 128],
                    start=(kchunk == 0), stop=(kchunk == nkc - 1),
                    tile_position=(0, 64 * h01)))
            p.emit_group("tensor", fns, reads=[PT, BIGB], writes=[vb_ap])
            p.emit("scalar", lambda E: E.copy(
                OP[64 * h01:64 * h01 + 64, opcol:opcol + 128], vb_ap),
                   reads=[vb_ap],
                   writes=[OP[64 * h01:64 * h01 + 64, opcol:opcol + 128]])

        for hp in range(NCH):
            QR = QRX[hp % 2]
            p.dma("sync", QR, d["qrot_sp"][hp], stream=f"qi{hp % 3}")
            for t in range(NT):
                p.emit("tensor", lambda E, t=t, hp=hp: E.transpose(
                    bank(6 + t % 2, 0, 128),
                    bigb_t(t, hp * 128, (hp + 1) * 128), sb["ident"]),
                       reads=[BIGB, sb["ident"]],
                       writes=[bank(6 + t % 2, 0, 128)])
                eng = "scalar" if t % 2 == 0 else "vector"
                if t % 2 == 0:
                    p.emit("scalar", lambda E, t=t: E.copy(
                        LT[:, t * 128:(t + 1) * 128], bank(6 + t % 2, 0, 128)),
                           reads=[bank(6 + t % 2, 0, 128)],
                           writes=[LT[:, t * 128:(t + 1) * 128]])
                else:
                    p.emit("vector", lambda E, t=t: E.tensor_copy(
                        LT[:, t * 128:(t + 1) * 128], bank(6 + t % 2, 0, 128)),
                           reads=[bank(6 + t % 2, 0, 128)],
                           writes=[LT[:, t * 128:(t + 1) * 128]])
            for qc in range(4):
                qlo = 1536 + qc * 128
                for h01 in range(2):
                    for kg in range(4):
                        p.emit("tensor", lambda E, h01=h01, kg=kg, qlo=qlo,
                               QR=QR: E.matmul(bank(h01 * 4 + kg),
                                        QR[64 * h01:64 * h01 + 64,
                                           qlo:qlo + 128],
                                        LT[64 * h01:64 * h01 + 64,
                                           kg * 512:(kg + 1) * 512],
                                        start=True, stop=True,
                                        tile_position=(64 * h01, 0)),
                               reads=[QR, LT], writes=[bank(h01 * 4 + kg)])
                for h01 in range(2):
                    softmax_h(h01, 4, 2048, qc, True)
                for h01 in range(2):
                    pt_and_v(hp, h01, 16, qc * 128)
            for qc in range(12):
                qlo = qc * 128
                for h01 in range(2):
                    for kg in range(3):
                        p.emit("tensor", lambda E, h01=h01, kg=kg, qlo=qlo,
                               QR=QR: E.matmul(bank(h01 * 3 + kg),
                                        QR[64 * h01:64 * h01 + 64,
                                           qlo:qlo + 128],
                                        LT[64 * h01:64 * h01 + 64,
                                           kg * 512:(kg + 1) * 512],
                                        start=True, stop=True,
                                        tile_position=(64 * h01, 0)),
                               reads=[QR, LT], writes=[bank(h01 * 3 + kg)])
                for h01 in range(2):
                    softmax_h(h01, 3, 1536, qc, False)
                for h01 in range(2):
                    pt_and_v(hp, h01, 12, 512 + qc * 128)
            p.dma("gpsimd", d["opair_sp"][hp], OP, stream=f"oo{hp % 2}")
        if "opair" in tap_t:
            for hp in range(NCH):
                p.dma("gpsimd", tap_t["opair"][hp], d["opair_sp"][hp],
                      stream="tap")

    if phases >= 5:
        WOS = BIGB[:, 0:8192]
        XFI = BIGB[:, 8192:12288]
        OPI = BIGB[:, 12288:16384]
        for dm in range(NCH):
            p.dma("sync", WOS[:, dm * 1024:(dm + 1) * 1024], d["wo_r"][dm],
                  stream="wo")
        for g in range(4):
            for c in range(NCH):
                p.dma("sync", XFI[:, c * 512:(c + 1) * 512],
                      d["xfm_sp"][c][:, g * 512:(g + 1) * 512],
                      stream=f"xfi{g % 2}")
            for hp in range(NCH):
                p.dma("sync", OPI[:, hp * 512:(hp + 1) * 512],
                      d["opair_sp"][hp][:, g * 512:(g + 1) * 512],
                      stream=f"oi{g % 2}")
            for dm in range(NCH):
                fns = []
                for hp in range(NCH):
                    fns.append(lambda E, dm=dm, hp=hp: E.matmul(
                        bank(dm % 4),
                        WOS[:, dm * 1024 + hp * 128:dm * 1024 + hp * 128 + 128],
                        OPI[:, hp * 512:(hp + 1) * 512],
                        start=(hp == 0), stop=(hp == NCH - 1)))
                p.emit_group("tensor", fns, reads=[WOS, OPI],
                             writes=[bank(dm % 4)])
                p.emit("vector", lambda E, dm=dm, g=g: E.scalar_tensor_tensor(
                    biga(dm, g * 512, (g + 1) * 512), bank(dm % 4),
                    sb["bo_cm"][:, dm:dm + 1],
                    XFI[:, dm * 512:(dm + 1) * 512], ALU.add, ALU.add),
                       reads=[bank(dm % 4), sb["bo_cm"], XFI],
                       writes=[biga(dm, g * 512, (g + 1) * 512)])
        if "y1" in tap_t:
            p.dma("gpsimd", tap_t["y1"], BIGA, stream="tap")

    if phases >= 6:
        MB2 = BIGB[:, 0:2048]
        RB2 = BIGB[:, 2048:4096]
        SQT = BIGB[:, 4096:6144]
        st_s = bigc(0, 2048)[0:1, :]
        st_ss = bigc(2048, 2048)[0:1, :]
        st_m = bigc(4096, 2048)[0:1, :]
        st_r = bigc(6144, 2048)[0:1, :]
        for c in range(NCH):
            p.emit("scalar", lambda E, c=c: E.activation(
                out=SQT, in_=biga(c), func=AF.Square),
                   reads=[biga(c)], writes=[SQT])
            for g in range(4):
                p.emit("tensor", lambda E, c=c, g=g: E.matmul(
                    PS[0:1, g, :], sb["ones_col"],
                    biga(c, g * 512, (g + 1) * 512),
                    start=(c == 0), stop=(c == NCH - 1)),
                       reads=[sb["ones_col"], biga(c)],
                       writes=[PS[0:1, g, :]])
                p.emit("tensor", lambda E, c=c, g=g: E.matmul(
                    PS[0:1, 4 + g, :], sb["ones_col"],
                    SQT[:, g * 512:(g + 1) * 512],
                    start=(c == 0), stop=(c == NCH - 1)),
                       reads=[sb["ones_col"], SQT],
                       writes=[PS[0:1, 4 + g, :]])
        for g in range(4):
            p.emit("scalar", lambda E, g=g: E.copy(
                st_s[:, g * 512:(g + 1) * 512], PS[0:1, g, :]),
                   reads=[PS[0:1, g, :]],
                   writes=[st_s[:, g * 512:(g + 1) * 512]])
            p.emit("scalar", lambda E, g=g: E.copy(
                st_ss[:, g * 512:(g + 1) * 512], PS[0:1, 4 + g, :]),
                   reads=[PS[0:1, 4 + g, :]],
                   writes=[st_ss[:, g * 512:(g + 1) * 512]])
        p.emit("vector", lambda E: E.tensor_scalar_mul(st_m, st_s, 1.0 / DM),
               reads=[st_s], writes=[st_m])
        p.emit("vector", lambda E: E.tensor_tensor(st_r, st_m, st_m, ALU.mult),
               reads=[st_m], writes=[st_r])
        p.emit("vector", lambda E: E.scalar_tensor_tensor(
            st_s, st_ss, 1.0 / DM, st_r, ALU.mult, ALU.subtract),
               reads=[st_ss, st_r], writes=[st_s])
        p.emit("scalar", lambda E: E.activation(
            out=st_r, in_=st_s, func=AF.Sqrt, bias=sb["eps"][0:1, :],
            scale=1.0),
               reads=[st_s, sb["eps"]], writes=[st_r])
        p.emit("vector", lambda E: E.reciprocal(st_r, st_r),
               reads=[st_r], writes=[st_r])
        for g in range(4):
            p.emit("tensor", lambda E, g=g: E.matmul(
                PS[:, g % 2, :], sb["ones_row"],
                st_m[:, g * 512:(g + 1) * 512], start=True, stop=True),
                   reads=[sb["ones_row"], st_m], writes=[PS[:, g % 2, :]])
            p.emit("scalar", lambda E, g=g: E.copy(
                MB2[:, g * 512:(g + 1) * 512], PS[:, g % 2, :]),
                   reads=[PS[:, g % 2, :]],
                   writes=[MB2[:, g * 512:(g + 1) * 512]])
            p.emit("tensor", lambda E, g=g: E.matmul(
                PS[:, 2 + g % 2, :], sb["ones_row"],
                st_r[:, g * 512:(g + 1) * 512], start=True, stop=True),
                   reads=[sb["ones_row"], st_r], writes=[PS[:, 2 + g % 2, :]])
            p.emit("scalar", lambda E, g=g: E.copy(
                RB2[:, g * 512:(g + 1) * 512], PS[:, 2 + g % 2, :]),
                   reads=[PS[:, 2 + g % 2, :]],
                   writes=[RB2[:, g * 512:(g + 1) * 512]])

    if phases >= 7:
        MB2 = BIGB[:, 0:2048]
        RB2 = BIGB[:, 2048:4096]
        XN2 = BIGB[:, 4096:8192]
        ACTB = BIGB[:, 8192:16384]
        GS = [bigc(0, 1024), bigc(1024, 1024), bigc(2048, 1024),
              bigc(3072, 1024)]
        DSL = [bigc(4096, 2048), bigc(6144, 2048)]
        WBE = bigc(8192, 512)
        SIG = bigc(8704, 512)
        for g in range(4):
            glo = g * 512
            for c in range(NCH):
                p.emit("vector", lambda E, c=c, glo=glo: E.tensor_tensor(
                    XN2[:, c * 512:(c + 1) * 512], biga(c, glo, glo + 512),
                    MB2[:, glo:glo + 512], ALU.subtract),
                       reads=[biga(c, glo, glo + 512), MB2],
                       writes=[XN2[:, c * 512:(c + 1) * 512]])
                p.emit("vector", lambda E, c=c, glo=glo: E.tensor_tensor(
                    XN2[:, c * 512:(c + 1) * 512],
                    XN2[:, c * 512:(c + 1) * 512],
                    RB2[:, glo:glo + 512], ALU.mult),
                       reads=[XN2, RB2], writes=[XN2[:, c * 512:(c + 1) * 512]])
            if g == 0 and "xn2" in tap_t:
                p.dma("gpsimd", tap_t["xn2"], XN2, stream="tap")
            LGA = bigc(5376, 32)
            MXA = bigc(5408, 16)
            RVA = bigc(5424, 16)
            TMA = bigc(5440, 32)
            lg3 = LGA.rearrange("p (t e) -> p t e", e=8)
            tm3 = TMA.rearrange("p (t e) -> p t e", e=8)
            mx3 = MXA[:, 0:4].rearrange("p (t o) -> p t o", o=1)
            rv3 = RVA[:, 0:4].rearrange("p (t o) -> p t o", o=1)
            rb3 = sb["rb_rep"].rearrange("p (o e) -> p o e", o=1)
            for tch in range(4):
                fns = []
                for c in range(NCH):
                    fns.append(lambda E, c=c, tch=tch: E.matmul(
                        PS[:, 4 + tch, 0:8],
                        XN2[:, c * 512 + tch * 128:c * 512 + tch * 128 + 128],
                        sb["rt_sb"][:, c, :],
                        start=(c == 0), stop=(c == NCH - 1)))
                p.emit_group("tensor", fns, reads=[XN2, sb["rt_sb"]],
                             writes=[PS[:, 4 + tch, 0:8]])
            for tch in range(4):
                p.emit("scalar", lambda E, tch=tch: E.copy(
                    LGA[:, tch * 8:(tch + 1) * 8], PS[:, 4 + tch, 0:8]),
                       reads=[PS[:, 4 + tch, 0:8]],
                       writes=[LGA[:, tch * 8:(tch + 1) * 8]])
            p.emit("vector", lambda E: E.tensor_tensor(
                lg3, lg3, rb3.to_broadcast((128, 4, 8)), ALU.add),
                   reads=[LGA, sb["rb_rep"]], writes=[LGA])
            p.emit("vector", lambda E: E.reduce_max(
                MXA[:, 0:4], lg3, axis=AX.X, negate=True),
                   reads=[LGA], writes=[MXA])
            p.emit("vector", lambda E: E.tensor_tensor(
                lg3, lg3, mx3.to_broadcast((128, 4, 8)), ALU.add),
                   reads=[LGA, MXA], writes=[LGA])
            p.emit("scalar", lambda E: E.activation(
                out=LGA, in_=LGA, func=AF.Exp),
                   reads=[LGA], writes=[LGA])
            p.emit("vector", lambda E: E.reduce_sum(
                RVA[:, 0:4], lg3, axis=AX.X),
                   reads=[LGA], writes=[RVA])
            p.emit("vector", lambda E: E.reciprocal(
                RVA[:, 0:4], RVA[:, 0:4]), reads=[RVA], writes=[RVA])
            p.emit("vector", lambda E: E.tensor_tensor(
                lg3, lg3, rv3.to_broadcast((128, 4, 8)), ALU.mult),
                   reads=[LGA, RVA], writes=[LGA])
            p.emit("vector", lambda E: E.reduce_max(
                MXA[:, 0:4], lg3, axis=AX.X), reads=[LGA], writes=[MXA])
            p.emit("vector", lambda E: E.tensor_tensor(
                tm3, lg3, mx3.to_broadcast((128, 4, 8)), ALU.is_ge),
                   reads=[LGA, MXA], writes=[TMA])
            p.emit("vector", lambda E: E.scalar_tensor_tensor(
                TMA, TMA, -1e30, LGA, ALU.mult, ALU.add),
                   reads=[TMA, LGA], writes=[TMA])
            p.emit("vector", lambda E: E.reduce_max(
                MXA[:, 0:4], tm3, axis=AX.X), reads=[TMA], writes=[MXA])
            p.emit("vector", lambda E: E.tensor_tensor(
                tm3, lg3, mx3.to_broadcast((128, 4, 8)), ALU.is_ge),
                   reads=[LGA, MXA], writes=[TMA])
            p.emit("vector", lambda E: E.tensor_tensor(
                TMA, TMA, LGA, ALU.mult), reads=[TMA, LGA], writes=[TMA])
            for tch in range(4):
                bb = 2 + tch % 2
                p.emit("tensor", lambda E, tch=tch, bb=bb: E.transpose(
                    PS[0:8, bb, 0:128], TMA[:, tch * 8:(tch + 1) * 8],
                    sb["ident"]),
                       reads=[TMA, sb["ident"]], writes=[PS[0:8, bb, 0:128]])
                p.emit("scalar", lambda E, tch=tch, glo=glo, bb=bb: E.copy(
                    sb["w8"][:, glo + tch * 128:glo + tch * 128 + 128],
                    PS[0:8, bb, 0:128]),
                       reads=[PS[0:8, bb, 0:128]],
                       writes=[sb["w8"][:, glo + tch * 128:glo + tch * 128 + 128]])
            for dm in range(NCH):
                p.emit("tensor", lambda E, dm=dm, glo=glo: E.matmul(
                    PS[:, 4 + dm % 2, :], sb["dwb"][:, dm * 128:(dm + 1) * 128],
                    sb["w8"][:, glo:glo + 512], start=True, stop=True),
                       reads=[sb["dwb"], sb["w8"]],
                       writes=[PS[:, 4 + dm % 2, :]])
                p.emit("vector", lambda E, dm=dm, glo=glo: E.tensor_tensor(
                    biga(dm, glo, glo + 512), biga(dm, glo, glo + 512),
                    PS[:, 4 + dm % 2, :], ALU.add),
                       reads=[biga(dm, glo, glo + 512), PS[:, 4 + dm % 2, :]],
                       writes=[biga(dm, glo, glo + 512)])
            x2t = [bigc(0, 1024), bigc(1024, 1024)]
            for tch in range(4):
                xt2 = x2t[tch % 2]
                for c in range(NCH):
                    bb = c % 2
                    p.emit("tensor", lambda E, c=c, tch=tch, bb=bb: E.transpose(
                        bank(bb, 0, 128),
                        XN2[:, c * 512 + tch * 128:c * 512 + tch * 128 + 128],
                        sb["ident"]),
                           reads=[XN2, sb["ident"]], writes=[bank(bb, 0, 128)])
                    p.emit("scalar", lambda E, c=c, bb=bb, xt2=xt2: E.copy(
                        xt2[:, c * 128:(c + 1) * 128], bank(bb, 0, 128)),
                           reads=[bank(bb, 0, 128)],
                           writes=[xt2[:, c * 128:(c + 1) * 128]])
                row0 = glo + tch * 128
                p.dma("gpsimd", d["x2tm_sp"][row0:row0 + 128, 0:1024], xt2,
                      stream=f"x2o{tch % 2}")
            wstg = bigc(2048, 128)
            for toff in range(4):
                t = 4 * g + toff
                p.emit("tensor", lambda E, t=t, toff=toff: E.transpose(
                    bank(2 + toff % 2, 0, 8), sb["w8"][:, t * 128:(t + 1) * 128],
                    sb["ident"][0:8, 0:8]),
                       reads=[sb["w8"], sb["ident"]],
                       writes=[bank(2 + toff % 2, 0, 8)])
                p.emit("scalar", lambda E, t=t, toff=toff: E.copy(
                    wstg[:, toff * 8:toff * 8 + 8], bank(2 + toff % 2, 0, 8)),
                       reads=[bank(2 + toff % 2, 0, 8)],
                       writes=[wstg[:, toff * 8:toff * 8 + 8]])
                p.dma("gpsimd", d["x2tm_sp"][t * 128:(t + 1) * 128, 1024:1032],
                      wstg[:, toff * 8:toff * 8 + 8], stream=f"wto{t % 2}")
            yst = [bigc(3328, 1024), bigc(4352, 1024)]
            for toff in range(4):
                t = 4 * g + toff
                ys = yst[t % 2]
                for c in range(NCH):
                    bb = 4 + c % 2
                    p.emit("tensor", lambda E, t=t, c=c, bb=bb: E.transpose(
                        bank(bb, 0, 128), biga(c, t * 128, t * 128 + 128),
                        sb["ident"]),
                           reads=[biga(c, t * 128, t * 128 + 128),
                                  sb["ident"]],
                           writes=[bank(bb, 0, 128)])
                    p.emit("scalar", lambda E, c=c, bb=bb, ys=ys: E.copy(
                        ys[:, c * 128:(c + 1) * 128], bank(bb, 0, 128)),
                           reads=[bank(bb, 0, 128)],
                           writes=[ys[:, c * 128:(c + 1) * 128]])
                p.dma("gpsimd", d["ytmp"][t * 128:(t + 1) * 128, :], ys,
                      stream=f"ylo{t % 2}")
        if "w8" in tap_t:
            p.dma("gpsimd", tap_t["w8"], sb["w8"], stream="tap")
        zr = bigc(2176, 1032)
        p.emit("vector", lambda E: E.memset(zr[0:1, :], 0.0),
               writes=[zr[0:1, :]])
        p.dma("gpsimd", d["x2tm_sp"][2048:2049, :], zr[0:1, :], stream="zr")
        p.dma("gpsimd", d["ytmp"][2048:2049, :], zr[0:1, 0:1024], stream="zr")


        FL8 = bigc(0, 2048)[0:8, :]
        CU8 = bigc(2048, 2048)[0:8, :]
        ZER = bigc(4096, 2048)[0:8, :]
        CUT = bigc(6144, 128)
        FLT = bigc(6272, 128)
        SLT = bigc(6400, 128)
        PEN = bigc(6528, 128)
        IXF = bigc(10240, 16)
        IXG = bigc(10256, 8)
        SELB = bigc(6656, 1024)
        IOTA = bigc(7680, 1024)
        TKO = bigc(8704, 32)
        p.dma("sync", IOTA, d["iota_r"], stream="tki")
        p.dma("sync", TKO, d["tokone"].rearrange("p a b -> p (a b)"),
              stream="tki")
        p.emit("vector", lambda E: E.memset(ZER, 0.0), writes=[ZER])
        p.emit("vector", lambda E: E.tensor_single_scalar(
            FL8, sb["w8"], 0.0, ALU.is_gt),
               reads=[sb["w8"]], writes=[FL8])
        p.emit("vector", lambda E: E.tensor_tensor_scan(
            CU8, FL8, ZER, 0.0, ALU.add, ALU.bypass),
               reads=[FL8, ZER], writes=[CU8])
        for t in range(NT):
            bb = 6 + t % 2
            p.emit("tensor", lambda E, t=t, bb=bb: E.transpose(
                bank(bb, 0, 8), CU8[:, t * 128:(t + 1) * 128],
                sb["ident"][0:8, 0:8]),
                   reads=[CU8, sb["ident"]], writes=[bank(bb, 0, 8)])
            p.emit("scalar", lambda E, t=t, bb=bb: E.copy(
                CUT[:, t * 8:(t + 1) * 8], bank(bb, 0, 8)),
                   reads=[bank(bb, 0, 8)], writes=[CUT[:, t * 8:(t + 1) * 8]])
            p.emit("tensor", lambda E, t=t, bb=bb: E.transpose(
                bank(bb, 8, 16), FL8[:, t * 128:(t + 1) * 128],
                sb["ident"][0:8, 0:8]),
                   reads=[FL8, sb["ident"]], writes=[bank(bb, 8, 16)])
            p.emit("scalar", lambda E, t=t, bb=bb: E.copy(
                FLT[:, t * 8:(t + 1) * 8], bank(bb, 8, 16)),
                   reads=[bank(bb, 8, 16)], writes=[FLT[:, t * 8:(t + 1) * 8]])
        p.emit("vector", lambda E: E.tensor_scalar_add(SLT, CUT, -1.0),
               reads=[CUT], writes=[SLT])
        p.emit("vector", lambda E: E.tensor_tensor(SLT, SLT, FLT, ALU.mult),
               reads=[SLT, FLT], writes=[SLT])
        p.emit("vector", lambda E: E.tensor_scalar(
            PEN, FLT, -100000.0, 100000.0, ALU.mult, ALU.add),
               reads=[FLT], writes=[PEN])
        p.emit("vector", lambda E: E.tensor_tensor(SLT, SLT, PEN, ALU.add),
               reads=[SLT, PEN], writes=[SLT])
        for e in range(8):
            for t in range(NT):
                p.emit("vector", lambda E, e=e, t=t: E.tensor_scalar(
                    SELB, IOTA, SLT[:, t * 8 + e:t * 8 + e + 1], None,
                    ALU.is_equal),
                       reads=[IOTA, SLT], writes=[SELB])
                fns = []
                for sn in range(8):
                    fns.append(lambda E, sn=sn, t=t: E.matmul(
                        PS[:, sn, 0:2], SELB[:, sn * 128:(sn + 1) * 128],
                        TKO[:, t * 2:t * 2 + 2],
                        start=(t == 0), stop=(t == NT - 1)))
                p.emit_group("tensor", fns, reads=[SELB, TKO],
                             writes=[PS[:, sn_, 0:2] for sn_ in range(8)])
            for sn in range(8):
                p.emit("scalar", lambda E, sn=sn: E.copy(
                    IXF[:, sn * 2:sn * 2 + 2], PS[:, sn, 0:2]),
                       reads=[PS[:, sn, 0:2]],
                       writes=[IXF[:, sn * 2:sn * 2 + 2]])
            idxv = bass.AP(BIGC.tensor, int(IXF.offset), [[12288, 128], [2, 8]])
            occv = bass.AP(BIGC.tensor, int(IXF.offset) + 1,
                           [[12288, 128], [2, 8]])
            p.emit("vector", lambda E: E.scalar_tensor_tensor(
                IXG, occv, -2048.0, idxv, ALU.mult, ALU.add),
                   reads=[IXF], writes=[IXG])
            p.emit("vector", lambda E: E.tensor_scalar_add(IXG, IXG, 2048.0),
                   reads=[IXG], writes=[IXG])
            p.emit("vector", lambda E, e=e: E.tensor_copy(
                sb["ixall"][:, e * 8:(e + 1) * 8], IXG),
                   reads=[IXG], writes=[sb["ixall"]])
        p.dma("gpsimd", d["ytmpb"][2048:2049, :], zr[0:1, 0:1024], stream="zr")
        zt16 = bigc(8736, 1024)
        p.emit("vector", lambda E: E.memset(zt16, 0.0), writes=[zt16])
        for t in range(NT):
            p.dma("gpsimd", d["ytmpb"][t * 128:(t + 1) * 128, :], zt16,
                  stream=f"zb{t % 2}")

        XG = BIGB[:, 0:4096]
        ACTB = BIGB[:, 4096:12288]
        DTL = BIGB[:, 12288:16384]
        GS = [bigc(0, 1024), bigc(1024, 1024), bigc(2048, 1024),
              bigc(3072, 1024)]
        DSL = [bigc(4096, 2048), bigc(6144, 2048)]
        GTI = [bigc(8192, 1032), bigc(9224, 1032)]
        WROW = bigc(10256, 512)
        WBE = bigc(10768, 512)
        SIG = bigc(11280, 512)
        for e in range(8):
            ixb = sb["ixall"][:, e * 8:(e + 1) * 8]
            for T in range(2):
                for s2 in range(4):
                    si = T * 4 + s2
                    gt_ = GTI[s2 % 2]
                    ix1 = ixb[:, si:si + 1]
                    p.idma("gpsimd", lambda E, gt_=gt_, ix1=ix1:
                           E.indirect_dma_start(
                               gt_, None, d["x2tm_sp"],
                               bass.IndirectOffsetOnAxis(ap=ix1, axis=0)),
                           reads=[("x2tm_sp", 0, 1, 0, 1 << 40), ix1],
                           writes=[gt_], stream=f"gth{s2 % 2}")
                    for cb in range(2):
                        bb = 6 + cb
                        fns = []
                        for cc in range(4):
                            c = cb * 4 + cc
                            fns.append(lambda E, c=c, cc=cc, gt_=gt_, bb=bb:
                                       E.transpose(
                                           bank(bb, cc * 128, (cc + 1) * 128),
                                           gt_[:, c * 128:(c + 1) * 128],
                                           sb["ident"]))
                        p.emit_group("tensor", fns, reads=[gt_, sb["ident"]],
                                     writes=[bank(bb)])
                        xg3 = XG.rearrange("p (c s) -> p c s", c=8)
                        p.emit("scalar", lambda E, cb=cb, s2=s2, bb=bb, xg3=xg3:
                               E.copy(xg3[:, cb * 4:(cb + 1) * 4,
                                          s2 * 128:(s2 + 1) * 128],
                                      PS[:, bb, :].rearrange(
                                          "p (a b) -> p a b", a=4)),
                               reads=[bank(bb)], writes=[XG])
                    p.emit("tensor", lambda E, gt_=gt_, e=e: E.transpose(
                        PS[0:1, 5, 0:128], gt_[:, 1024 + e:1025 + e],
                        sb["ident"]),
                           reads=[gt_, sb["ident"]], writes=[PS[0:1, 5, 0:128]])
                    p.emit("scalar", lambda E, s2=s2: E.copy(
                        WROW[0:1, s2 * 128:(s2 + 1) * 128], PS[0:1, 5, 0:128]),
                           reads=[PS[0:1, 5, 0:128]],
                           writes=[WROW[0:1, s2 * 128:(s2 + 1) * 128]])
                p.emit("tensor", lambda E: E.matmul(
                    PS[:, 5, :], sb["ones_row"], WROW[0:1, :],
                    start=True, stop=True),
                       reads=[sb["ones_row"], WROW], writes=[PS[:, 5, :]])
                p.emit("scalar", lambda E: E.copy(WBE, PS[:, 5, :]),
                       reads=[PS[:, 5, :]], writes=[WBE])
                for mp in range(16):
                    gsl = GS[(2 * mp) % 4]
                    usl = GS[(2 * mp + 1) % 4]
                    p.dma("sync", gsl, d["gup_r"][e, mp], stream=f"gu{mp % 2}")
                    p.dma("sync", usl, d["gup_r"][e, 16 + mp],
                          stream=f"gv{mp % 2}")
                    bg = bank(mp % 2)
                    bu = bank(2 + mp % 2)
                    for wsl_, bk in ((gsl, bg), (usl, bu)):
                        fns = []
                        for c in range(NCH):
                            fns.append(lambda E, wsl_=wsl_, c=c, bk=bk:
                                       E.matmul(
                                           bk, wsl_[:, c * 128:(c + 1) * 128],
                                           XG[:, c * 512:(c + 1) * 512],
                                           start=(c == 0),
                                           stop=(c == NCH - 1)))
                        p.emit_group("tensor", fns, reads=[wsl_, XG],
                                     writes=[bk])
                    gcs = ACTB[:, mp * 512:(mp + 1) * 512]
                    p.emit("vector", lambda E, bg=bg, gcs=gcs, e=e, mp=mp:
                           E.tensor_scalar(
                               gcs, bg,
                               sb["gub1"][:, e * 32 + mp:e * 32 + mp + 1],
                               7.0, ALU.add, ALU.min),
                           reads=[bg, sb["gub1"]], writes=[gcs])
                    p.emit("scalar", lambda E, gcs=gcs: E.activation(
                        out=SIG, in_=gcs, func=AF.Sigmoid, scale=1.702),
                           reads=[gcs], writes=[SIG])
                    p.emit("vector", lambda E, gcs=gcs: E.tensor_tensor(
                        gcs, gcs, SIG, ALU.mult),
                           reads=[gcs, SIG], writes=[gcs])
                    p.emit("vector", lambda E, bu=bu, e=e, mp=mp:
                           E.tensor_scalar(
                               SIG, bu,
                               sb["gub1"][:, e * 32 + 16 + mp:e * 32 + 17 + mp],
                               8.0, ALU.add, ALU.min),
                           reads=[bu, sb["gub1"]], writes=[SIG])
                    p.emit("vector", lambda E: E.scalar_tensor_tensor(
                        SIG, SIG, -6.0, WBE, ALU.max, ALU.mult),
                           reads=[SIG, WBE], writes=[SIG])
                    p.emit("vector", lambda E, gcs=gcs: E.tensor_tensor(
                        gcs, gcs, SIG, ALU.mult),
                           reads=[gcs, SIG], writes=[gcs])
                if e == 0 and T == 0 and "act" in tap_t:
                    p.dma("gpsimd", tap_t["act"], ACTB, stream="tap")
                for dm in range(NCH):
                    dsl = DSL[dm % 2]
                    p.dma("sync", dsl, d["dwn_r"][e, dm], stream=f"dw{dm % 2}")
                    bk = bank(4 + dm % 2)
                    fns = []
                    for i in range(16):
                        fns.append(lambda E, i=i, bk=bk, dsl=dsl: E.matmul(
                            bk, dsl[:, i * 128:(i + 1) * 128],
                            ACTB[:, i * 512:(i + 1) * 512],
                            start=(i == 0), stop=(i == 15)))
                    p.emit_group("tensor", fns, reads=[dsl, ACTB], writes=[bk])
                    p.emit("scalar", lambda E, dm=dm, bk=bk: E.copy(
                        DTL[:, dm * 512:(dm + 1) * 512], bk),
                           reads=[bk],
                           writes=[DTL[:, dm * 512:(dm + 1) * 512]])
                dt3 = DTL.rearrange("p (c s) -> p c s", c=8)
                dtm = [bigc(8192, 1024), bigc(9216, 1024)]
                for s2 in range(4):
                    dstg = dtm[s2 % 2]
                    for cb in range(2):
                        bb = 6 + cb
                        fns = []
                        for cc in range(4):
                            dmc = cb * 4 + cc
                            fns.append(lambda E, dmc=dmc, cc=cc, s2=s2, bb=bb:
                                       E.transpose(
                                           bank(bb, cc * 128, (cc + 1) * 128),
                                           dt3[:, dmc,
                                              s2 * 128:(s2 + 1) * 128],
                                           sb["ident"]))
                        p.emit_group("tensor", fns, reads=[DTL, sb["ident"]],
                                     writes=[bank(bb)])
                        p.emit("scalar", lambda E, cb=cb, bb=bb, dstg=dstg:
                               E.copy(dstg[:, cb * 512:(cb + 1) * 512],
                                      bank(bb)),
                               reads=[bank(bb)],
                               writes=[dstg[:, cb * 512:(cb + 1) * 512]])
                    ix1 = ixb[:, T * 4 + s2:T * 4 + s2 + 1]
                    ytgt = d["ytmp"] if e % 2 == 0 else d["ytmpb"]
                    ynm = "ytmp" if e % 2 == 0 else "ytmpb"
                    p.idma("gpsimd", lambda E, dstg=dstg, ix1=ix1, ytgt=ytgt:
                           E.indirect_dma_start(
                               ytgt,
                               bass.IndirectOffsetOnAxis(ap=ix1, axis=0),
                               dstg, None, compute_op=ALU.add),
                           reads=[dstg, ix1, (ynm, 0, 1, 0, 1 << 40)],
                           writes=[(ynm, 0, 1, 0, 1 << 40)],
                           stream=f"sca{e % 2}")

        ACC = [BIGB[:, 0:1024], BIGB[:, 1024:2048]]
        BCC = [BIGB[:, 2048:3072], BIGB[:, 3072:4096]]
        for t in range(NT):
            acc = ACC[t % 2]
            bcc = BCC[t % 2]
            p.dma("sync", acc, d["ytmp"][t * 128:(t + 1) * 128, :],
                  stream=f"fin{t % 2}")
            p.dma("sync", bcc, d["ytmpb"][t * 128:(t + 1) * 128, :],
                  stream=f"fib{t % 2}")
            p.emit("vector", lambda E, acc=acc, bcc=bcc: E.tensor_tensor(
                acc, acc, bcc, ALU.add), reads=[acc, bcc], writes=[acc])
            p.dma("gpsimd", d["y"][t * 128:(t + 1) * 128, :], acc,
                  stream=f"fo{t % 2}")

    p.wait_all("sync")
    p.wait_all("gpsimd")
    p.finalize(stack)
    return nc, stack, p


def kernel(**inputs):
    tabs = host_tables()
    lay = host_layouts(inputs)
    x = np.ascontiguousarray(np.asarray(inputs["L0_attn_x"], dtype=np.float32))
    nc, stack, p = build_program()
    base = {}
    base.update(tabs)
    base.update(lay)
    in_maps = []
    for i in range(8):
        m = dict(base)
        m["x"] = np.ascontiguousarray(x[i])
        in_maps.append(m)
    res = run_bass_kernel_spmd(nc, in_maps, core_ids=list(range(8)))
    stack.close()
    out = np.stack([r["y"] for r in res.results], axis=0)
    return out
